# version 06
# baseline (speedup 1.0000x reference)
import contextlib
import numpy as np
import ml_dtypes
import concourse.bass as bass
import concourse.mybir as mybir
from concourse.bass import IndirectOffsetOnAxis
from concourse.bass_utils import run_bass_kernel_spmd

F32 = mybir.dt.float32
BF16 = mybir.dt.bfloat16
I32 = mybir.dt.int32
AF = mybir.ActivationFunctionType
ALU = mybir.AluOpType
AX = mybir.AxisListType

BF = ml_dtypes.bfloat16

DIM = 2048
HEADS = 16
HD = 128
RANK = 16
SCALING = 0.5
SCALE = HD ** -0.5
E = 8
H = 2816
HC = H // 128
DC = DIM // 128
T = 8192
TC = 1024
CAP = 1024
NCORE = 8
EPS = 1e-5

ENGINES = ("tensor", "scalar", "vector", "gpsimd", "sync")


class Prog:

    def __init__(self, nc, stack):
        self.nc = nc
        self.stack = stack
        self.ops = {e: [] for e in ENGINES}
        self.ctr = {}

    def sem(self, name):
        h = self.stack.enter_context(self.nc.semaphore(name))
        self.ctr[h] = 0
        return h

    def sb(self, name, shape, dtype):
        return self.stack.enter_context(self.nc.sbuf_tensor(name, shape, dtype))

    def ps(self, name, shape, dtype=F32):
        return self.stack.enter_context(self.nc.psum_tensor(name, shape, dtype))

    def sb_tmp(self, name, shape, dtype):
        cm = self.nc.sbuf_tensor(name, shape, dtype)
        t = cm.__enter__()
        if not hasattr(self, "live"):
            self.live = []
        self.live.append(cm)
        return t, cm

    def ps_tmp(self, name, shape, dtype=F32):
        cm = self.nc.psum_tensor(name, shape, dtype)
        t = cm.__enter__()
        if not hasattr(self, "live"):
            self.live = []
        self.live.append(cm)
        return t, cm

    def free(self, cm):
        cm.__exit__(None, None, None)
        if hasattr(self, "live") and cm in self.live:
            self.live.remove(cm)

    def free_all(self):
        if hasattr(self, "live"):
            for cm in reversed(self.live):
                cm.__exit__(None, None, None)
            self.live = []

    def on(self, eng, fn):
        self.ops[eng].append(fn)

    def inc(self, eng, sem, k=1):
        self.ctr[sem] += k
        prev = self.ops[eng].pop()

        def wrapped(e, prev=prev, sem=sem, k=k):
            return prev(e).then_inc(sem, k)

        self.ops[eng].append(wrapped)
        return self.ctr[sem]

    def wait(self, eng, sem, val=None):
        v = self.ctr[sem] if val is None else val
        if v > 0:
            self.on(eng, lambda e, sem=sem, v=v: e.wait_ge(sem, v))
        return v

    def drain(self):
        for sem, v in list(self.ctr.items()):
            if v > 0:
                self.wait("sync", sem, v)
                self.wait("gpsimd", sem, v)

    def emit(self, block):
        for name in ENGINES:
            ops = self.ops[name]
            if not ops:
                continue

            def body(e, ops=ops):
                for f in ops:
                    f(e)

            getattr(block, name)(body)



_L1_PROJS = ["q", "k", "v"]
_F1_NORMS = True
_F1_KEXT = False
_F1_STORES = True
_F1_PROJS = ["q", "k", "v"]


def build_l1():
    nc = bass.Bass("TRN2", target_bir_lowering=False)
    xT = nc.dram_tensor("xT", [DIM, TC], F32, kind="ExternalInput").ap()
    W = {}
    for pch in "qkv":
        W[pch] = {
            "WT": nc.dram_tensor(f"W{pch}T", [DIM, DIM], F32, kind="ExternalInput").ap(),
            "BT": nc.dram_tensor(f"B{pch}T", [DIM, RANK], F32, kind="ExternalInput").ap(),
            "AT": nc.dram_tensor(f"A{pch}T", [RANK, DIM], F32, kind="ExternalInput").ap(),
            "b": nc.dram_tensor(f"b{pch}", [128, DC], F32, kind="ExternalInput").ap(),
            "brow": nc.dram_tensor(f"b{pch}row", [1, DIM], F32, kind="ExternalInput").ap(),
        }
    QF = nc.dram_tensor("QF", [DIM, TC], BF16, kind="ExternalOutput").ap()
    KF = nc.dram_tensor("KF", [DIM, TC], BF16, kind="ExternalOutput").ap()
    V = nc.dram_tensor("V", [TC, DIM], BF16, kind="ExternalOutput").ap()
    QN = nc.dram_tensor("QN", [1, HEADS], F32, kind="ExternalOutput").ap()
    KN = nc.dram_tensor("KN", [1, HEADS], F32, kind="ExternalOutput").ap()

    NSTG = 4
    with contextlib.ExitStack() as stack:
        p = Prog(nc, stack)
        xb = p.sb("xb", [128, DC * TC], BF16)
        wt0 = p.sb("wt0", [128, DC * DIM], BF16)
        wt1 = p.sb("wt1", [128, DC * DIM], BF16)
        bt0 = p.sb("bt0", [128, DC * RANK], BF16)
        bt1 = p.sb("bt1", [128, DC * RANK], BF16)
        at0 = p.sb("at0", [RANK, DIM], BF16)
        at1 = p.sb("at1", [RANK, DIM], BF16)
        btile0 = p.sb("btile0", [128, DC], F32)
        btile1 = p.sb("btile1", [128, DC], F32)
        brow0 = p.sb("brow0", [1, DIM], BF16)
        brow1 = p.sb("brow1", [1, DIM], BF16)
        xlora = p.sb("xlora", [RANK, TC], BF16)
        stage_ring = [p.sb(f"stg{i}", [128, TC], BF16) for i in range(NSTG)]
        sq = p.sb("sq", [128, TC], BF16)
        ones_col = p.sb("ones_col", [128, 1], BF16)
        ones_row = p.sb("ones_row", [1, 128], BF16)
        qn_sb = p.sb("qn_sb", [1, HEADS], F32)
        kn_sb = p.sb("kn_sb", [1, HEADS], F32)
        psA = p.ps("psA", [128, 1024])
        psB = p.ps("psB", [128, 1024])
        psxb = p.ps("psxb", [RANK, TC])
        psn = p.ps("psn", [1, 1024])
        block = stack.enter_context(nc.Block())
        s_x = p.sem("s_x")
        s_w = [p.sem("s_w0"), p.sem("s_w1")]
        s_mm = p.sem("s_mm")
        s_ev = p.sem("s_ev")
        s_dv = p.sem("s_dv")
        s_stg = [p.sem(f"s_stg{i}") for i in range(NSTG)]
        s_on = p.sem("s_on")

        wts = [wt0, wt1]
        bts = [bt0, bt1]
        ats = [at0, at1]
        btiles = [btile0, btile1]
        brows = [brow0, brow1]
        pss = [psA, psB]
        projs = list(_L1_PROJS)

        for dc in range(DC):
            p.on("gpsimd", lambda g, dc=dc: g.dma_start(
                out=xb[:, dc * TC:(dc + 1) * TC], in_=xT[dc * 128:(dc + 1) * 128, :]))
            p.inc("gpsimd", s_x, 16)
        p.on("vector", lambda v: v.memset(ones_col[:], 1.0))
        p.inc("vector", s_dv)
        p.on("vector", lambda v: v.memset(ones_row[:], 1.0))
        v_const = p.inc("vector", s_dv)

        def load_w(pi):
            name = projs[pi]
            k = pi % 2
            if pi >= 2:
                p.wait("gpsimd", s_mm)
            for dc in range(DC):
                p.on("gpsimd", lambda g, dc=dc, k=k, name=name: g.dma_start(
                    out=wts[k][:, dc * DIM:(dc + 1) * DIM],
                    in_=W[name]["WT"][dc * 128:(dc + 1) * 128, :]))
                p.inc("gpsimd", s_w[k], 16)
            p.on("gpsimd", lambda g, k=k, name=name: g.dma_start(
                out=bts[k][:].rearrange("p (dc r) -> p dc r", r=RANK),
                in_=W[name]["BT"].rearrange("(dc p) r -> p dc r", p=128)))
            p.inc("gpsimd", s_w[k], 16)
            p.on("gpsimd", lambda g, k=k, name=name: g.dma_start(out=ats[k][:], in_=W[name]["AT"][:, :]))
            p.inc("gpsimd", s_w[k], 16)
            p.on("gpsimd", lambda g, k=k, name=name: g.dma_start(out=btiles[k][:], in_=W[name]["b"][:, :]))
            p.inc("gpsimd", s_w[k], 16)
            p.on("gpsimd", lambda g, k=k, name=name: g.dma_start(out=brows[k][:], in_=W[name]["brow"][:, :]))
            return p.inc("gpsimd", s_w[k], 16)

        ld_done = [load_w(0)]
        outdst = {"q": QF, "k": KF}
        ndst = {"q": qn_sb, "k": kn_sb}
        ev_ms = []
        stg_uses = [0] * NSTG
        gflat = [0]

        def group(emit_mms, emit_store, do_norm, name, oc):
            gi = gflat[0]
            gflat[0] += 1
            ps = pss[gi % 2]
            slot = gi % NSTG
            stg = stage_ring[slot]
            if gi >= 2:
                p.wait("tensor", s_ev, ev_ms[gi - 2])
            emit_mms(ps)
            mm_grp = p.inc("tensor", s_mm)
            p.wait("scalar", s_mm, mm_grp)
            if stg_uses[slot] > 0:
                p.wait("scalar", s_stg[slot], 16 * stg_uses[slot])
            emit_evac = emit_store[0]
            emit_evac(ps, stg)
            ev = p.inc("scalar", s_ev)
            ev_ms.append(ev)
            p.wait("sync", s_ev, ev)
            emit_store[1](stg)
            stg_uses[slot] += 1
            p.inc("sync", s_stg[slot], 16)
            if do_norm:
                p.wait("scalar", s_ev, ev)
                p.on("scalar", lambda s, stg=stg: s.activation(sq[:], stg[:], AF.Square))
                ev_sq = p.inc("scalar", s_ev)
                p.wait("tensor", s_ev, ev_sq)
                p.wait("tensor", s_dv)
                for half in range(2):
                    p.on("tensor", lambda t, half=half: t.matmul(
                        psn[:, half * 512:(half + 1) * 512], ones_col[:],
                        sq[:, half * 512:(half + 1) * 512],
                        start=True, stop=True))
                mm_n = p.inc("tensor", s_mm)
                p.wait("vector", s_mm, mm_n)
                p.on("vector", lambda v, oc=oc, name=name: v.tensor_reduce(
                    ndst[name][0:1, oc:oc + 1], psn[0:1, :], axis=AX.X, op=ALU.max))
                p.inc("vector", s_dv)

        for pi, name in enumerate(projs):
            k = pi % 2
            wbuf, btb, atb, btlb, brb = wts[k], bts[k], ats[k], btiles[k], brows[k]
            p.wait("tensor", s_w[k], ld_done[pi])
            if pi == 0:
                p.wait("tensor", s_x)
            if pi + 1 < len(projs):
                ld_done.append(load_w(pi + 1))

            for half in range(2):
                for dc in range(DC):
                    p.on("tensor", lambda t, dc=dc, half=half, btb=btb: t.matmul(
                        psxb[:, half * 512:(half + 1) * 512],
                        btb[:, dc * RANK:(dc + 1) * RANK],
                        xb[:, dc * TC + half * 512: dc * TC + (half + 1) * 512],
                        start=(dc == 0), stop=(dc == DC - 1)))
            p.inc("tensor", s_mm)
            p.wait("scalar", s_mm)
            p.on("scalar", lambda s: s.activation(xlora[:], psxb[:], AF.Identity, scale=SCALING))
            ev_lora = p.inc("scalar", s_ev)

            if name in ("q", "k"):
                for oc in range(DC):
                    def mms(ps, oc=oc, wbuf=wbuf, atb=atb):
                        if oc == 0:
                            p.wait("tensor", s_ev, ev_lora)
                        for half in range(2):
                            for dc in range(DC):
                                p.on("tensor", lambda t, oc=oc, dc=dc, half=half, ps=ps, wbuf=wbuf: t.matmul(
                                    ps[:, half * 512:(half + 1) * 512],
                                    wbuf[:, dc * DIM + oc * 128: dc * DIM + (oc + 1) * 128],
                                    xb[:, dc * TC + half * 512: dc * TC + (half + 1) * 512],
                                    start=(dc == 0), stop=False))
                            p.on("tensor", lambda t, oc=oc, half=half, ps=ps, atb=atb: t.matmul(
                                ps[:, half * 512:(half + 1) * 512],
                                atb[:, oc * 128:(oc + 1) * 128],
                                xlora[:, half * 512:(half + 1) * 512],
                                start=False, stop=True))

                    def evac(ps, stg, oc=oc, btlb=btlb):
                        p.on("scalar", lambda s, ps=ps, stg=stg, oc=oc, btlb=btlb: s.activation(
                            stg[:], ps[:], AF.Identity, bias=btlb[:, oc:oc + 1]))

                    def store(stg, oc=oc, name=name):
                        p.on("sync", lambda sy, oc=oc, stg=stg, name=name: sy.dma_start(
                            out=outdst[name][oc * 128:(oc + 1) * 128, :], in_=stg[:]))

                    group(mms, (evac, store), True, name, oc)
            else:
                for tc8 in range(8):
                    for on4 in range(4):
                        def mms(ps, tc8=tc8, on4=on4, wbuf=wbuf, atb=atb, brb=brb):
                            if tc8 == 0 and on4 == 0:
                                p.wait("tensor", s_ev, ev_lora)
                                p.wait("tensor", s_dv, v_const)
                            for dc in range(DC):
                                p.on("tensor", lambda t, tc8=tc8, on4=on4, dc=dc, ps=ps, wbuf=wbuf: t.matmul(
                                    ps[:, 0:512],
                                    xb[:, dc * TC + tc8 * 128: dc * TC + (tc8 + 1) * 128],
                                    wbuf[:, dc * DIM + on4 * 512: dc * DIM + (on4 + 1) * 512],
                                    start=(dc == 0), stop=False))
                            p.on("tensor", lambda t, tc8=tc8, on4=on4, ps=ps, atb=atb: t.matmul(
                                ps[:, 0:512],
                                xlora[:, tc8 * 128:(tc8 + 1) * 128],
                                atb[:, on4 * 512:(on4 + 1) * 512],
                                start=False, stop=False))
                            p.on("tensor", lambda t, on4=on4, ps=ps, brb=brb: t.matmul(
                                ps[:, 0:512], ones_row[:],
                                brb[0:1, on4 * 512:(on4 + 1) * 512],
                                start=False, stop=True))

                        def evac(ps, stg):
                            p.on("scalar", lambda s, ps=ps, stg=stg: s.activation(
                                stg[:, 0:512], ps[:, 0:512], AF.Identity))

                        def store(stg, tc8=tc8, on4=on4):
                            p.on("sync", lambda sy, tc8=tc8, on4=on4, stg=stg: sy.dma_start(
                                out=V[tc8 * 128:(tc8 + 1) * 128, on4 * 512:(on4 + 1) * 512],
                                in_=stg[:, 0:512]))

                        group(mms, (evac, store), False, name, 0)

        p.wait("sync", s_dv)
        p.on("sync", lambda sy: sy.dma_start(out=QN[:, :], in_=qn_sb[:]))
        p.inc("sync", s_on, 16)
        p.on("sync", lambda sy: sy.dma_start(out=KN[:, :], in_=kn_sb[:]))
        p.inc("sync", s_on, 16)
        for i in range(NSTG):
            p.wait("sync", s_stg[i], 16 * stg_uses[i])
        p.wait("sync", s_on)
        p.emit(block)
    return nc


def _host_l1_inputs(inputs, core):
    x = np.asarray(inputs["L0_attn_x"], np.float32).reshape(T, DIM)
    xT = np.ascontiguousarray(x[core * TC:(core + 1) * TC, :].T)
    d = {"xT": xT}
    for pch in "qkv":
        Wm = np.asarray(inputs[f"L0_attn_W{pch}"], np.float32)
        Bm = np.asarray(inputs[f"L0_attn_B{pch}"], np.float32)
        Am = np.asarray(inputs[f"L0_attn_A{pch}"], np.float32)
        bm = np.asarray(inputs[f"L0_attn_b{pch}"], np.float32)
        d[f"W{pch}T"] = np.ascontiguousarray(Wm.T)
        d[f"B{pch}T"] = np.ascontiguousarray(Bm.T)
        d[f"A{pch}T"] = np.ascontiguousarray(Am.T)
        d[f"b{pch}"] = np.ascontiguousarray(bm.reshape(DC, 128).T)
        d[f"b{pch}row"] = bm.reshape(1, DIM)
    return d






def build_l2():
    nc = bass.Bass("TRN2", target_bir_lowering=False)
    QFi = nc.dram_tensor("QFi", [DIM, TC], BF16, kind="ExternalInput").ap()
    KFall = nc.dram_tensor("KFall", [DIM, 4 * TC], BF16, kind="ExternalInput").ap()
    Vall = nc.dram_tensor("Vall", [4 * TC, DIM], BF16, kind="ExternalInput").ap()
    QNi = nc.dram_tensor("QNi", [1, HEADS], F32, kind="ExternalInput").ap()
    KN4 = nc.dram_tensor("KN4", [1, 4 * HEADS], F32, kind="ExternalInput").ap()
    xTi = nc.dram_tensor("xTi", [DIM, TC], F32, kind="ExternalInput").ap()
    WoT = nc.dram_tensor("WoT", [DIM, DIM], F32, kind="ExternalInput").ap()
    BoT = nc.dram_tensor("BoT", [DIM, RANK], F32, kind="ExternalInput").ap()
    AoT = nc.dram_tensor("AoT", [RANK, DIM], F32, kind="ExternalInput").ap()
    bo = nc.dram_tensor("bo", [128, DC], F32, kind="ExternalInput").ap()
    Wg = nc.dram_tensor("Wg", [DIM, E], F32, kind="ExternalInput").ap()
    IDENT = nc.dram_tensor("IDENT", [128, 128], F32, kind="ExternalInput").ap()
    X2T = nc.dram_tensor("X2T", [DIM, TC], F32, kind="ExternalOutput").ap()
    X2TOK = nc.dram_tensor("X2TOK", [TC, DIM], BF16, kind="ExternalOutput").ap()
    LOGITS = nc.dram_tensor("LOGITS", [TC, E], F32, kind="ExternalOutput").ap()

    NKC = 4 * TC // 128

    with contextlib.ExitStack() as stack:
        p = Prog(nc, stack)
        cat = p.sb("cat", [128, HEADS * TC], BF16)
        ones_col = p.sb("ones_col", [128, 1], BF16)
        ones_rf = p.sb("ones_rf", [1, 128], F32)
        negb = p.sb("negb", [128, HEADS], F32)
        nrow = p.sb("nrow", [1, HEADS], F32)
        krow = p.sb("krow", [1, 4 * HEADS], F32)
        kmax = p.sb("kmax", [1, HEADS], F32)
        r_row = p.sb("r_row", [1, 512], F32)
        rt = p.sb("rt", [128, 512], F32)
        wg_sb = p.sb("wg_sb", [128, DC * E], BF16)
        ident = p.sb("ident", [128, 128], BF16)
        botile = p.sb("botile", [128, DC], F32)
        btlora = p.sb("btlora", [128, DC * RANK], BF16)
        atlora = p.sb("atlora", [RANK, DIM], BF16)
        qf, cm_qf = p.sb_tmp("qf", [128, HEADS * TC], BF16)
        kh0, cm_kh0 = p.sb_tmp("kh0", [128, 4 * TC], BF16)
        kh1, cm_kh1 = p.sb_tmp("kh1", [128, 4 * TC], BF16)
        vh0, cm_vh0 = p.sb_tmp("vh0", [128, 4 * TC], BF16)
        vh1, cm_vh1 = p.sb_tmp("vh1", [128, 4 * TC], BF16)
        e0, cm_e0 = p.sb_tmp("e0", [128, 512], BF16)
        e1, cm_e1 = p.sb_tmp("e1", [128, 512], BF16)
        ps_s0, cm_ps_s0 = p.ps_tmp("ps_s0", [128, 512])
        ps_s1, cm_ps_s1 = p.ps_tmp("ps_s1", [128, 512])
        ps_o, cm_ps_o = p.ps_tmp("ps_o", [128, 512])
        ps_g, cm_ps_g = p.ps_tmp("ps_g", [1, 512])
        ps_r, cm_ps_r = p.ps_tmp("ps_r", [128, 512])

        block = stack.enter_context(nc.Block())
        s_qf = p.sem("s_qf")
        s_kh = [p.sem("s_kh0"), p.sem("s_kh1")]
        s_vh = [p.sem("s_vh0"), p.sem("s_vh1")]
        s_const = p.sem("s_const")
        s_s = p.sem("s_s")
        s_e = p.sem("s_e")
        s_av = p.sem("s_av")
        s_mm = p.sem("s_mm")
        s_ev = p.sem("s_ev")
        s_dv = p.sem("s_dv")
        s_xt = [p.sem("s_xt0"), p.sem("s_xt1")]
        s_wo = [p.sem("s_wo0"), p.sem("s_wo1")]
        s_sto = p.sem("s_sto")
        s_xts = [p.sem("s_xts0"), p.sem("s_xts1")]

        khs, vhs, es = [kh0, kh1], [vh0, vh1], [e0, e1]

        for dc in range(DC):
            p.on("gpsimd", lambda g, dc=dc: g.dma_start(
                out=qf[:, dc * TC:(dc + 1) * TC], in_=QFi[dc * 128:(dc + 1) * 128, :]))
            p.inc("gpsimd", s_qf, 16)
        for t, src in ((nrow, QNi), (krow, KN4)):
            p.on("gpsimd", lambda g, t=t, src=src: g.dma_start(out=t[:], in_=src[:, :]))
            p.inc("gpsimd", s_const, 16)
        p.on("gpsimd", lambda g: g.dma_start(
            out=wg_sb[:].rearrange("p (dc e) -> p dc e", e=E),
            in_=Wg.rearrange("(dc p) e -> p dc e", p=128)))
        p.inc("gpsimd", s_const, 16)
        p.on("gpsimd", lambda g: g.dma_start(out=ident[:], in_=IDENT[:, :]))
        p.inc("gpsimd", s_const, 16)
        p.on("gpsimd", lambda g: g.dma_start(out=botile[:], in_=bo[:, :]))
        p.inc("gpsimd", s_const, 16)
        p.on("gpsimd", lambda g: g.dma_start(
            out=btlora[:].rearrange("p (dc r) -> p dc r", r=RANK),
            in_=BoT.rearrange("(dc p) r -> p dc r", p=128)))
        p.inc("gpsimd", s_const, 16)
        p.on("gpsimd", lambda g: g.dma_start(out=atlora[:], in_=AoT[:, :]))
        p.inc("gpsimd", s_const, 16)
        p.on("vector", lambda v: v.memset(ones_col[:], 1.0))
        p.inc("vector", s_dv)
        p.on("vector", lambda v: v.memset(ones_rf[:], 1.0))
        p.inc("vector", s_dv)

        p.wait("vector", s_const)
        p.on("vector", lambda v: v.tensor_reduce(
            kmax[0:1, :],
            krow[:].rearrange("o (r h) -> o h r", h=HEADS),
            axis=AX.X, op=ALU.max))
        kn_red = p.inc("vector", s_dv)
        p.wait("vector", s_dv, kn_red)
        p.on("vector", lambda v: v.tensor_mul(nrow[0:1, :], nrow[0:1, :], kmax[0:1, :]))
        nmul = p.inc("vector", s_dv)
        p.wait("scalar", s_dv, nmul)
        p.on("scalar", lambda s: s.activation(nrow[0:1, :], nrow[0:1, :], AF.Ln))
        ev1 = p.inc("scalar", s_ev)
        p.wait("scalar", s_ev, ev1)
        p.on("scalar", lambda s: s.activation(nrow[0:1, :], nrow[0:1, :], AF.Exp, scale=0.5))
        ev2 = p.inc("scalar", s_ev)
        p.wait("vector", s_ev, ev2)
        p.on("vector", lambda v: v.tensor_scalar(
            nrow[0:1, :], nrow[0:1, :], -SCALE, None, ALU.mult))
        nneg = p.inc("vector", s_dv)
        p.wait("tensor", s_dv, nneg)
        p.on("tensor", lambda t: t.matmul(ps_r[:, 0:HEADS], ones_rf[:], nrow[0:1, :],
                                          start=True, stop=True))
        bmm = p.inc("tensor", s_mm)
        p.wait("vector", s_mm, bmm)
        p.on("vector", lambda v: v.tensor_copy(negb[:], ps_r[:, 0:HEADS]))
        negb_done = p.inc("vector", s_dv)

        kh_uses = [0, 0]
        vh_uses = [0, 0]
        s_cnt = [0]
        av_cnt = [0]
        kh_ms = {}
        vh_ms = {}

        def load_head(h):
            sl = h % 2
            if h >= 2:
                p.wait("gpsimd", s_s, 64 * (h - 1))
                p.wait("gpsimd", s_av, 64 * (h - 1))
            p.on("gpsimd", lambda g, h=h, sl=sl: g.dma_start(
                out=khs[sl][:], in_=KFall[h * 128:(h + 1) * 128, :]))
            kh_uses[sl] += 1
            kh_ms[h] = 16 * kh_uses[sl]
            p.inc("gpsimd", s_kh[sl], 16)
            for qtr in range(4):
                p.on("gpsimd", lambda g, h=h, sl=sl, qtr=qtr: g.dma_start(
                    out=vhs[sl][:].rearrange("p (c j) -> p c j", j=128)[:, qtr * 8:(qtr + 1) * 8, :],
                    in_=Vall[:, h * 128:(h + 1) * 128].rearrange(
                        "(c p) j -> p c j", p=128)[:, qtr * 8:(qtr + 1) * 8, :]))
                p.inc("gpsimd", s_vh[sl], 16)
            vh_uses[sl] += 1
            vh_ms[h] = 64 * vh_uses[sl]

        load_head(0)
        load_head(1)

        for h in range(HEADS):
            sl = h % 2
            if h + 2 < HEADS:
                load_head(h + 2)
            p.wait("tensor", s_kh[sl], kh_ms[h])
            p.wait("tensor", s_vh[sl], vh_ms[h])
            if h == 0:
                p.wait("tensor", s_qf)
                p.wait("tensor", s_dv, negb_done)
            for qh in range(2):
                s_base = s_cnt[0]
                e_base = av_cnt[0]
                e_sem_base = s_cnt[0]
                prev_exps = s_cnt[0]
                p.wait("tensor", s_e, prev_exps)
                p.wait("tensor", s_dv)
                qsl = slice(h * TC + qh * 512, h * TC + (qh + 1) * 512)
                for c in range(2):
                    p.on("tensor", lambda t, c=c, sl=sl, qsl=qsl: t.matmul(
                        (ps_s0 if c % 2 == 0 else ps_s1)[:, :],
                        khs[sl][:, c * 128:(c + 1) * 128],
                        qf[:, qsl], start=True, stop=True))
                    s_cnt[0] += 1
                    p.inc("tensor", s_s)
                for c in range(NKC):
                    p.wait("scalar", s_s, s_base + c + 1)
                    if c >= 2:
                        p.wait("scalar", s_av, av_cnt[0] - 1)
                    p.on("scalar", lambda s, c=c, h=h: s.activation(
                        es[c % 2][:], (ps_s0 if c % 2 == 0 else ps_s1)[:],
                        AF.Exp, bias=negb[:, h:h + 1], scale=SCALE))
                    p.inc("scalar", s_e)
                    p.wait("tensor", s_e, s_base + c + 1)
                    p.on("tensor", lambda t, c=c, sl=sl: t.matmul(
                        ps_o[:, :], vhs[sl][:, c * 128:(c + 1) * 128], es[c % 2][:],
                        start=(c == 0), stop=(c == NKC - 1)))
                    p.on("tensor", lambda t, c=c: t.matmul(
                        ps_g[:, :], ones_col[:], es[c % 2][:],
                        start=(c == 0), stop=(c == NKC - 1)))
                    av_cnt[0] += 1
                    p.inc("tensor", s_av)
                    if c + 2 < NKC:
                        p.on("tensor", lambda t, c=c, sl=sl, qsl=qsl: t.matmul(
                            (ps_s0 if c % 2 == 0 else ps_s1)[:, :],
                            khs[sl][:, (c + 2) * 128:(c + 3) * 128],
                            qf[:, qsl], start=True, stop=True))
                        s_cnt[0] += 1
                        p.inc("tensor", s_s)
                p.wait("vector", s_av, av_cnt[0])
                p.on("vector", lambda v: v.reciprocal(r_row[:], ps_g[0:1, :]))
                rc = p.inc("vector", s_dv)
                p.wait("tensor", s_dv, rc)
                p.on("tensor", lambda t: t.matmul(ps_r[:, :], ones_rf[:], r_row[:],
                                                  start=True, stop=True))
                rmm = p.inc("tensor", s_mm)
                p.wait("vector", s_mm, rmm)
                p.on("vector", lambda v: v.tensor_copy(rt[:], ps_r[:, :]))
                rcp = p.inc("vector", s_dv)
                p.wait("vector", s_dv, rcp)
                p.on("vector", lambda v, qsl=qsl: v.tensor_mul(cat[:, qsl], ps_o[:], rt[:]))
                p.inc("vector", s_dv)

        att_s_total = s_cnt[0]
        att_av_total = av_cnt[0]
        att_dv_after = p.ctr[s_dv]

        for cm in (cm_ps_r, cm_ps_g, cm_ps_o, cm_ps_s1, cm_ps_s0):
            p.free(cm)
        for cm in (cm_e1, cm_e0, cm_vh1, cm_vh0, cm_kh1, cm_kh0, cm_qf):
            p.free(cm)
        pre1 = p.sb("pre1", [128, DC * TC], F32)
        x2b = None
        tmp32 = [p.sb("tmp32_0", [128, TC], F32), p.sb("tmp32_1", [128, TC], F32)]
        prebf = [p.sb("prebf0", [128, TC], BF16), p.sb("prebf1", [128, TC], BF16)]
        sqc = [p.sb("sqc0", [128, TC], BF16), p.sb("sqc1", [128, TC], BF16)]
        xts = [p.sb("xts0", [128, TC], F32), p.sb("xts1", [128, TC], F32)]
        wocs = [p.sb("woc0", [128, DC * 128], BF16), p.sb("woc1", [128, DC * 128], BF16)]
        xlo = p.sb("xlo", [RANK, TC], BF16)
        mean_row = p.sb("mean_row", [1, TC], F32)
        q_row = p.sb("q_row", [1, TC], F32)
        t_row = p.sb("t_row", [1, TC], F32)
        rstd_row = p.sb("rstd_row", [1, TC], F32)
        psA, cm_psA = p.ps_tmp("psA", [128, 1024])
        psB, cm_psB = p.ps_tmp("psB", [128, 1024])
        psS, cm_psS = p.ps_tmp("psS", [1, 1024])
        psQ, cm_psQ = p.ps_tmp("psQ", [1, 1024])
        pss = [psA, psB]

        p.wait("gpsimd", s_s, att_s_total)
        p.wait("gpsimd", s_av, att_av_total)
        p.wait("gpsimd", s_e, att_s_total)
        p.wait("gpsimd", s_dv, att_dv_after)

        p.wait("tensor", s_dv, att_dv_after)
        p.wait("tensor", s_const)
        for half in range(2):
            for dc in range(DC):
                p.on("tensor", lambda t, dc=dc, half=half: t.matmul(
                    psA[0:RANK, half * 512:(half + 1) * 512],
                    btlora[:, dc * RANK:(dc + 1) * RANK],
                    cat[:, dc * TC + half * 512: dc * TC + (half + 1) * 512],
                    start=(dc == 0), stop=(dc == DC - 1)))
        lomm = p.inc("tensor", s_mm)
        p.wait("scalar", s_mm, lomm)
        p.on("scalar", lambda s: s.activation(xlo[:], psA[0:RANK, :], AF.Identity, scale=SCALING))
        ev_lo = p.inc("scalar", s_ev)

        xt_ms = {}
        wo_ms = {}
        xt_uses = [0, 0]
        wo_uses = [0, 0]
        dv_tt_ms = {}
        ov_ms = {}

        def load_oc(oc):
            sl = oc % 2
            if oc >= 2:
                p.wait("gpsimd", s_dv, dv_tt_ms[oc - 2])
                p.wait("gpsimd", s_mm, ov_ms[oc - 2])
            p.on("gpsimd", lambda g, oc=oc, sl=sl: g.dma_start(
                out=xts[sl][:], in_=xTi[oc * 128:(oc + 1) * 128, :]))
            xt_uses[sl] += 1
            xt_ms[oc] = 16 * xt_uses[sl]
            p.inc("gpsimd", s_xt[sl], 16)
            p.on("gpsimd", lambda g, oc=oc, sl=sl: g.dma_start(
                out=wocs[sl][:].rearrange("p (dc j) -> p dc j", j=128),
                in_=WoT[:, oc * 128:(oc + 1) * 128].rearrange("(dc p) j -> p dc j", p=128)))
            wo_uses[sl] += 1
            wo_ms[oc] = 16 * wo_uses[sl]
            p.inc("gpsimd", s_wo[sl], 16)

        load_oc(0)
        load_oc(1)
        ev_ms2 = []
        for oc in range(DC):
            sl = oc % 2
            ps = pss[oc % 2]
            p.wait("tensor", s_wo[sl], wo_ms[oc])
            if oc >= 2:
                p.wait("tensor", s_ev, ev_ms2[oc - 2])
            if oc == 0:
                p.wait("tensor", s_ev, ev_lo)
            for half in range(2):
                for dc in range(DC):
                    p.on("tensor", lambda t, oc=oc, dc=dc, half=half, ps=ps, sl=sl: t.matmul(
                        ps[:, half * 512:(half + 1) * 512],
                        wocs[sl][:, dc * 128:(dc + 1) * 128],
                        cat[:, dc * TC + half * 512: dc * TC + (half + 1) * 512],
                        start=(dc == 0), stop=False))
                p.on("tensor", lambda t, oc=oc, half=half, ps=ps: t.matmul(
                    ps[:, half * 512:(half + 1) * 512],
                    atlora[:, oc * 128:(oc + 1) * 128],
                    xlo[:, half * 512:(half + 1) * 512],
                    start=False, stop=True))
            ov_ms[oc] = p.inc("tensor", s_mm)
            p.wait("scalar", s_mm, ov_ms[oc])
            if oc >= 2:
                p.wait("scalar", s_dv, dv_tt_ms[oc - 2])
            p.on("scalar", lambda s, oc=oc, ps=ps, sl=sl: s.activation(
                tmp32[sl][:], ps[:], AF.Identity, bias=botile[:, oc:oc + 1]))
            ev = p.inc("scalar", s_ev)
            ev_ms2.append(ev)
            p.wait("vector", s_ev, ev)
            p.wait("vector", s_xt[sl], xt_ms[oc])
            p.on("vector", lambda v, oc=oc, sl=sl: v.tensor_add(
                pre1[:, oc * TC:(oc + 1) * TC], tmp32[sl][:], xts[sl][:]))
            dv_tt_ms[oc] = p.inc("vector", s_dv)
            p.wait("vector", s_dv, dv_tt_ms[oc])
            p.on("vector", lambda v, oc=oc, sl=sl: v.tensor_copy(
                prebf[sl][:], pre1[:, oc * TC:(oc + 1) * TC]))
            cpms = p.inc("vector", s_dv)
            p.wait("scalar", s_dv, cpms)
            p.on("scalar", lambda s, sl=sl: s.activation(sqc[sl][:], prebf[sl][:], AF.Square))
            sqev = p.inc("scalar", s_ev)
            p.wait("tensor", s_ev, sqev)
            for half in range(2):
                p.on("tensor", lambda t, oc=oc, half=half, sl=sl: t.matmul(
                    psS[0:1, half * 512:(half + 1) * 512], ones_col[:],
                    prebf[sl][:, half * 512:(half + 1) * 512],
                    start=(oc == 0), stop=(oc == DC - 1)))
                p.on("tensor", lambda t, oc=oc, half=half, sl=sl: t.matmul(
                    psQ[0:1, half * 512:(half + 1) * 512], ones_col[:],
                    sqc[sl][:, half * 512:(half + 1) * 512],
                    start=(oc == 0), stop=(oc == DC - 1)))
            stat_ms = p.inc("tensor", s_mm)
            if oc + 2 < DC:
                load_oc(oc + 2)
            dv_tt_ms[oc] = dv_tt_ms[oc]
            if oc >= 2:
                pass
            wo_ms[("stat", oc)] = stat_ms


        last_stat = wo_ms[("stat", DC - 1)]
        p.wait("vector", s_mm, last_stat)
        p.on("vector", lambda v: v.tensor_scalar(mean_row[:], psS[0:1, :], 1.0 / DIM, None, ALU.mult))
        m1 = p.inc("vector", s_dv)
        p.on("vector", lambda v: v.tensor_scalar(q_row[:], psQ[0:1, :], 1.0 / DIM, None, ALU.mult))
        m2 = p.inc("vector", s_dv)
        p.wait("vector", s_dv, m2)
        p.on("vector", lambda v: v.tensor_mul(t_row[:], mean_row[:], mean_row[:]))
        m3 = p.inc("vector", s_dv)
        p.wait("vector", s_dv, m3)
        p.on("vector", lambda v: v.tensor_sub(q_row[:], q_row[:], t_row[:]))
        m4 = p.inc("vector", s_dv)
        p.wait("vector", s_dv, m4)
        p.on("vector", lambda v: v.tensor_scalar(q_row[:], q_row[:], EPS, None, ALU.add))
        m5 = p.inc("vector", s_dv)
        p.wait("scalar", s_dv, m5)
        p.on("scalar", lambda s: s.activation(t_row[:], q_row[:], AF.Ln))
        e1_ = p.inc("scalar", s_ev)
        p.wait("scalar", s_ev, e1_)
        p.on("scalar", lambda s: s.activation(rstd_row[:], t_row[:], AF.Exp, scale=-0.5))
        e2_ = p.inc("scalar", s_ev)
        p.wait("tensor", s_ev, e2_)
        p.wait("tensor", s_dv, m1)
        for half in range(2):
            p.on("tensor", lambda t, half=half: t.matmul(
                psA[:, half * 512:(half + 1) * 512], ones_rf[:],
                mean_row[0:1, half * 512:(half + 1) * 512], start=True, stop=True))
            p.on("tensor", lambda t, half=half: t.matmul(
                psB[:, half * 512:(half + 1) * 512], ones_rf[:],
                rstd_row[0:1, half * 512:(half + 1) * 512], start=True, stop=True))
        mr_ms = p.inc("tensor", s_mm)

        p.free(cm_psQ)
        p.free(cm_psS)
        x2b = p.sb("x2b", [128, DC * TC], BF16)
        nrm_ms = {}
        for dc in range(DC):
            sl = dc % 2
            p.wait("vector", s_mm, mr_ms)
            p.on("vector", lambda v, dc=dc, sl=sl: v.tensor_sub(
                tmp32[sl][:], pre1[:, dc * TC:(dc + 1) * TC], psA[:]))
            n1 = p.inc("vector", s_dv)
            p.wait("vector", s_dv, n1)
            p.on("vector", lambda v, dc=dc, sl=sl: v.tensor_mul(
                pre1[:, dc * TC:(dc + 1) * TC], tmp32[sl][:], psB[:]))
            n2 = p.inc("vector", s_dv)
            p.wait("vector", s_dv, n2)
            p.on("vector", lambda v, dc=dc: v.tensor_copy(
                x2b[:, dc * TC:(dc + 1) * TC], pre1[:, dc * TC:(dc + 1) * TC]))
            n3 = p.inc("vector", s_dv)
            nrm_ms[dc] = n3
            p.wait("sync", s_dv, n3)
            p.on("sync", lambda sy, dc=dc: sy.dma_start(
                out=X2T[dc * 128:(dc + 1) * 128, :], in_=pre1[:, dc * TC:(dc + 1) * TC]))
            p.inc("sync", s_sto, 16)

        lstage = p.sb("lstage", [128, 8 * E], F32)
        psL, cm_psL = p.ps_tmp("psL", [128, E])
        p.wait("tensor", s_dv, nrm_ms[DC - 1])
        for tc8 in range(8):
            for dc in range(DC):
                p.on("tensor", lambda t, tc8=tc8, dc=dc: t.matmul(
                    psL[:, :],
                    x2b[:, dc * TC + tc8 * 128: dc * TC + (tc8 + 1) * 128],
                    wg_sb[:, dc * E:(dc + 1) * E],
                    start=(dc == 0), stop=(dc == DC - 1)))
            lg = p.inc("tensor", s_mm)
            p.wait("scalar", s_mm, lg)
            p.on("scalar", lambda s, tc8=tc8: s.activation(
                lstage[:, tc8 * E:(tc8 + 1) * E], psL[:], AF.Identity))
            lev = p.inc("scalar", s_ev)
            p.wait("tensor", s_ev, lev)
        p.wait("sync", s_ev)
        p.on("sync", lambda sy: sy.dma_start(
            out=LOGITS.rearrange("(t8 p) e -> p t8 e", p=128),
            in_=lstage[:].rearrange("p (t8 e) -> p t8 e", e=E)))
        p.inc("sync", s_sto, 16)

        xtok = [p.sb("xtok0", [128, DIM], BF16), p.sb("xtok1", [128, DIM], BF16)]
        ps_t0, cm_pst0 = p.ps_tmp("ps_t0", [128, 128], BF16)
        ps_t1, cm_pst1 = p.ps_tmp("ps_t1", [128, 128], BF16)
        pst = [ps_t0, ps_t1]
        xts_uses = [0, 0]
        for tc8 in range(8):
            sl = tc8 % 2
            if xts_uses[sl] > 0:
                p.wait("vector", s_xts[sl], 16 * xts_uses[sl])
            for dc in range(DC):
                psl2 = pst[dc % 2]
                p.on("tensor", lambda t, tc8=tc8, dc=dc, psl2=psl2: t.transpose(
                    psl2[:], x2b[:, dc * TC + tc8 * 128: dc * TC + (tc8 + 1) * 128],
                    ident[:]))
                trm = p.inc("tensor", s_mm)
                p.wait("vector", s_mm, trm)
                p.on("vector", lambda v, dc=dc, sl=sl, psl2=psl2: v.tensor_copy(
                    xtok[sl][:, dc * 128:(dc + 1) * 128], psl2[:]))
                cpm = p.inc("vector", s_dv)
                p.wait("tensor", s_dv, cpm)
            p.wait("sync", s_dv)
            p.on("sync", lambda sy, tc8=tc8, sl=sl: sy.dma_start(
                out=X2TOK[tc8 * 128:(tc8 + 1) * 128, :], in_=xtok[sl][:]))
            xts_uses[sl] += 1
            p.inc("sync", s_xts[sl], 16)

        p.wait("sync", s_sto)
        for i in range(2):
            p.wait("sync", s_xts[i], 16 * xts_uses[i])
        p.emit(block)
    return nc


def _host_l2_inputs(inputs, core, l1res):
    x = np.asarray(inputs["L0_attn_x"], np.float32).reshape(T, DIM)
    seq = core // 4
    d = {
        "QFi": np.asarray(l1res[core]["QF"]),
        "KFall": np.concatenate([np.asarray(l1res[4 * seq + i]["KF"]) for i in range(4)], axis=1),
        "Vall": np.concatenate([np.asarray(l1res[4 * seq + i]["V"]) for i in range(4)], axis=0),
        "QNi": np.asarray(l1res[core]["QN"]),
        "KN4": np.concatenate([np.asarray(l1res[4 * seq + i]["KN"]) for i in range(4)], axis=1),
        "xTi": np.ascontiguousarray(x[core * TC:(core + 1) * TC, :].T),
        "WoT": np.ascontiguousarray(np.asarray(inputs["L0_attn_Wo"], np.float32).T),
        "BoT": np.ascontiguousarray(np.asarray(inputs["L0_attn_Bo"], np.float32).T),
        "AoT": np.ascontiguousarray(np.asarray(inputs["L0_attn_Ao"], np.float32).T),
        "bo": np.ascontiguousarray(np.asarray(inputs["L0_attn_bo"], np.float32).reshape(DC, 128).T),
        "Wg": np.asarray(inputs["L0_ffn_Wg"], np.float32),
        "IDENT": np.eye(128, dtype=np.float32),
    }
    return d



def build_l3():
    nc = bass.Bass("TRN2", target_bir_lowering=False)
    X2Ti = nc.dram_tensor("X2Ti", [DIM, TC], F32, kind="ExternalInput").ap()
    X2ALL = nc.dram_tensor("X2ALL", [T, DIM], BF16, kind="ExternalInput").ap()
    LOGALL = nc.dram_tensor("LOGALL", [T, E], F32, kind="ExternalInput").ap()
    W1c = nc.dram_tensor("W1c", [DIM, H], F32, kind="ExternalInput").ap()
    W3c = nc.dram_tensor("W3c", [DIM, H], F32, kind="ExternalInput").ap()
    W2c = nc.dram_tensor("W2c", [H, DIM], F32, kind="ExternalInput").ap()
    SW1 = nc.dram_tensor("SW1", [DIM, H], F32, kind="ExternalInput").ap()
    SW3 = nc.dram_tensor("SW3", [DIM, H], F32, kind="ExternalInput").ap()
    SW2 = nc.dram_tensor("SW2", [H, DIM], F32, kind="ExternalInput").ap()
    MYEXP = nc.dram_tensor("MYEXP", [1, 1], F32, kind="ExternalInput").ap()
    LT = nc.dram_tensor("LT", [128, 128], F32, kind="ExternalInput").ap()
    IDENT = nc.dram_tensor("IDENT", [128, 128], F32, kind="ExternalInput").ap()
    OE = nc.dram_tensor("OE", [CAP, DIM], BF16, kind="ExternalOutput").ap()
    PRE2AT = nc.dram_tensor("PRE2AT", [DIM, TC], F32, kind="ExternalOutput").ap()
    FLAT = nc.dram_tensor("FLAT", [T, 1], I32, kind="ExternalOutput").ap()
    GK = nc.dram_tensor("GK", [T, 1], F32, kind="ExternalOutput").ap()
    st8 = nc.dram_tensor("st8", [1152, E], I32)

    with contextlib.ExitStack() as stack:
        p = Prog(nc, stack)
        lg = p.sb("lg", [128, 512], F32)
        mx = p.sb("mx", [128, 64], F32)
        exf = p.sb("exf", [128, 512], F32)
        esum = p.sb("esum", [128, 64], F32)
        gate = p.sb("gate", [128, 64], F32)
        ohge = p.sb("ohge", [128, 512], F32)
        iotaE = p.sb("iotaE", [128, 512], F32)
        sel = p.sb("sel", [128, 512], F32)
        eidx = p.sb("eidx", [128, 64], F32)
        oh = p.sb("oh", [128, 512], F32)
        cnt = p.sb("cnt", [128, 8], F32)
        cntb = p.sb("cntb", [128, 8], BF16)
        offs = p.sb("offs", [128, 8], F32)
        run = p.sb("run", [128, 8], F32)
        tmp8 = p.sb("tmp8", [128, 8], F32)
        posv = p.sb("posv", [128, 64], F32)
        keep = p.sb("keep", [128, 64], F32)
        gk = p.sb("gk", [128, 64], F32)
        f1 = p.sb("f1", [128, 64], F32)
        flat_i = p.sb("flat_i", [128, 64], I32)
        mycol = p.sb("mycol", [128, 1], F32)
        myexp_sb = p.sb("myexp_sb", [1, 1], F32)
        mk = p.sb("mk", [128, 64], F32)
        t2 = p.sb("t2", [128, 64], F32)
        flloc_i = p.sb("flloc_i", [128, 64], I32)
        lt_sb = p.sb("lt_sb", [128, 128], BF16)
        ident = p.sb("ident", [128, 128], BF16)
        ones_rf = p.sb("ones_rf", [1, 128], F32)
        zero72 = p.sb("zero72", [128, 72], I32)
        tok8 = [p.sb("tok80", [128, 8], I32), p.sb("tok81", [128, 8], I32)]
        g_sb, cm_g_sb = p.sb_tmp("g_sb", [128, HC * TC], BF16)
        x2tb = p.sb("x2tb", [128, DC * TC], BF16)

        block = stack.enter_context(nc.Block())
        s_c = p.sem("s_c")
        s_mm = p.sem("s_mm")
        s_ev = p.sem("s_ev")
        s_dv = p.sem("s_dv")
        s_gp = p.sem("s_gp")
        s_z = p.sem("s_z")
        s_sc = [p.sem("s_sc0"), p.sem("s_sc1")]
        s_ix = [p.sem("s_ix0"), p.sem("s_ix1")]
        s_gx = [p.sem("s_gx0"), p.sem("s_gx1")]
        s_w1 = [p.sem("s_w10"), p.sem("s_w11")]
        s_w2 = [p.sem("s_w20"), p.sem("s_w21")]
        s_x2 = p.sem("s_x2")
        s_xf = [p.sem("s_xf0"), p.sem("s_xf1")]
        s_so = [p.sem("s_so0"), p.sem("s_so1")]
        s_sp = [p.sem("s_sp0"), p.sem("s_sp1")]
        s_sfg = p.sem("s_sfg")

        p.on("gpsimd", lambda g: g.dma_start(
            out=lg[:].rearrange("p (a e) -> p a e", e=E),
            in_=LOGALL.rearrange("(p a) e -> p a e", p=128)))
        p.inc("gpsimd", s_c, 16)
        p.on("gpsimd", lambda g: g.dma_start(out=lt_sb[:], in_=LT[:, :]))
        p.inc("gpsimd", s_c, 16)
        p.on("gpsimd", lambda g: g.dma_start(out=ident[:], in_=IDENT[:, :]))
        p.inc("gpsimd", s_c, 16)
        p.on("gpsimd", lambda g: g.dma_start(out=myexp_sb[:], in_=MYEXP[:, :]))
        p.inc("gpsimd", s_c, 16)
        p.on("gpsimd", lambda g: g.iota(iotaE[:].rearrange("p (a e) -> p a e", e=E),
                                        pattern=[[0, 64], [1, 8]], base=0,
                                        channel_multiplier=0,
                                        allow_small_or_imprecise_dtypes=True))
        p.inc("gpsimd", s_gp)
        p.on("gpsimd", lambda g: g.memset(zero72[:], 0))
        p.inc("gpsimd", s_gp)
        p.on("vector", lambda v: v.memset(ones_rf[:], 1.0))
        v0 = p.inc("vector", s_dv)

        def vstep(fn):
            p.on("vector", fn)
            ms = p.inc("vector", s_dv)
            p.wait("vector", s_dv, ms)
            return ms

        p.wait("vector", s_c)
        p.wait("vector", s_gp, 1)
        vstep(lambda v: v.tensor_reduce(
            mx[:], lg[:].rearrange("p (a e) -> p a e", e=E), axis=AX.X, op=ALU.max))
        for e_ in range(E):
            p.on("vector", lambda v, e_=e_: v.tensor_sub(
                exf[:].rearrange("p (a e) -> p a e", e=E)[:, :, e_],
                lg[:].rearrange("p (a e) -> p a e", e=E)[:, :, e_], mx[:]))
            p.inc("vector", s_dv)
        dsub = p.wait("vector", s_dv)
        p.wait("scalar", s_dv, dsub)
        p.on("scalar", lambda s: s.activation(exf[:], exf[:], AF.Exp))
        eexp = p.inc("scalar", s_ev)
        p.wait("vector", s_ev, eexp)
        vstep(lambda v: v.tensor_reduce(
            esum[:], exf[:].rearrange("p (a e) -> p a e", e=E), axis=AX.X, op=ALU.add))
        vstep(lambda v: v.reciprocal(gate[:], esum[:]))
        for e_ in range(E):
            p.on("vector", lambda v, e_=e_: v.tensor_tensor(
                ohge[:].rearrange("p (a e) -> p a e", e=E)[:, :, e_],
                lg[:].rearrange("p (a e) -> p a e", e=E)[:, :, e_], mx[:], ALU.is_ge))
            p.inc("vector", s_dv)
        p.wait("vector", s_dv)
        vstep(lambda v: v.tensor_scalar(sel[:], iotaE[:], -8.0, None, ALU.add))
        vstep(lambda v: v.tensor_mul(sel[:], sel[:], ohge[:]))
        vstep(lambda v: v.tensor_scalar(sel[:], sel[:], 8.0, None, ALU.add))
        vstep(lambda v: v.tensor_reduce(
            eidx[:], sel[:].rearrange("p (a e) -> p a e", e=E), axis=AX.X, op=ALU.min))
        for e_ in range(E):
            p.on("vector", lambda v, e_=e_: v.tensor_scalar(
                oh[:].rearrange("p (a e) -> p a e", e=E)[:, :, e_],
                eidx[:], float(e_), None, ALU.is_equal))
            p.inc("vector", s_dv)
        p.wait("vector", s_dv)
        vstep(lambda v: v.tensor_reduce(
            cnt[:], oh[:].rearrange("p (a e) -> p e a", e=E), axis=AX.X, op=ALU.add))
        vstep(lambda v: v.tensor_copy(cntb[:], cnt[:]))
        cnt_ms = p.ctr[s_dv]
        p.wait("tensor", s_dv, cnt_ms)
        p.wait("tensor", s_c)
        ps8, cm_ps8 = p.ps_tmp("ps8", [128, 8])
        psc, cm_psc = p.ps_tmp("psc", [128, 1])
        p.on("tensor", lambda t: t.matmul(ps8[:], lt_sb[:], cntb[:], start=True, stop=True))
        p.inc("tensor", s_mm)
        p.on("tensor", lambda t: t.matmul(psc[:], ones_rf[:], myexp_sb[:], start=True, stop=True))
        offmm = p.inc("tensor", s_mm)
        p.wait("vector", s_mm, offmm)
        vstep(lambda v: v.tensor_copy(offs[:], ps8[:]))
        vstep(lambda v: v.tensor_copy(mycol[:], psc[:]))
        vstep(lambda v: v.tensor_copy(run[:], offs[:]))
        ohv = oh[:].rearrange("p (a e) -> p a e", e=E)
        for a in range(64):
            vstep(lambda v, a=a: v.tensor_mul(tmp8[:], run[:], ohv[:, a, :]))
            vstep(lambda v, a=a: v.tensor_reduce(
                posv[:, a:a + 1], tmp8[:], axis=AX.X, op=ALU.add))
            vstep(lambda v, a=a: v.tensor_add(run[:], run[:], ohv[:, a, :]))
        vstep(lambda v: v.tensor_scalar(keep[:], posv[:], float(CAP), None, ALU.is_lt))
        vstep(lambda v: v.tensor_mul(gk[:], gate[:], keep[:]))
        vstep(lambda v: v.tensor_scalar(f1[:], eidx[:], float(CAP), None, ALU.mult))
        vstep(lambda v: v.tensor_add(f1[:], f1[:], posv[:]))
        vstep(lambda v: v.tensor_scalar(f1[:], f1[:], float(T - 1), None, ALU.min))
        vstep(lambda v: v.tensor_copy(flat_i[:], f1[:]))
        vstep(lambda v: v.tensor_scalar(mk[:], eidx[:], mycol[:, 0:1], None, ALU.is_equal))
        vstep(lambda v: v.tensor_mul(mk[:], mk[:], keep[:]))
        vstep(lambda v: v.tensor_scalar(t2[:], posv[:], -float(CAP), None, ALU.add))
        vstep(lambda v: v.tensor_mul(t2[:], t2[:], mk[:]))
        vstep(lambda v: v.tensor_scalar(t2[:], t2[:], float(CAP), None, ALU.add))
        route_dv = vstep(lambda v: v.tensor_copy(flloc_i[:], t2[:]))
        p.wait("sync", s_dv, route_dv)
        p.on("sync", lambda sy: sy.dma_start(
            out=FLAT.rearrange("(p a) o -> p (a o)", p=128), in_=flat_i[:]))
        p.inc("sync", s_sfg, 16)
        p.on("sync", lambda sy: sy.dma_start(
            out=GK.rearrange("(p a) o -> p (a o)", p=128), in_=gk[:]))
        p.inc("sync", s_sfg, 16)

        p.wait("gpsimd", s_gp, 2)
        p.on("gpsimd", lambda g: g.dma_start(
            out=st8.ap().rearrange("(p a) e -> p a e", p=128),
            in_=zero72[:].rearrange("p (a e) -> p a e", e=E)))
        p.inc("gpsimd", s_z, 16)
        p.wait("gpsimd", s_z, 16)
        p.wait("gpsimd", s_dv, route_dv)
        tok_uses = [0, 0]
        for a in range(64):
            sl = a % 2
            if tok_uses[sl] > 0:
                p.wait("gpsimd", s_sc[sl], 16 * tok_uses[sl])
            p.on("gpsimd", lambda g, a=a, sl=sl: g.iota(
                tok8[sl][:], pattern=[[0, 8]], base=a, channel_multiplier=64))
            p.inc("gpsimd", s_gp)
            p.on("gpsimd", lambda g, a=a, sl=sl: g.indirect_dma_start(
                out=st8.ap(), out_offset=IndirectOffsetOnAxis(flloc_i[:, a:a + 1], 0),
                in_=tok8[sl][:], in_offset=None))
            tok_uses[sl] += 1
            p.inc("gpsimd", s_sc[sl], 16)
        for sl in range(2):
            p.wait("gpsimd", s_sc[sl], 16 * tok_uses[sl])

        p.free(cm_psc)
        p.free(cm_ps8)
        idxt0, cm_idxt0 = p.sb_tmp("idxt0", [128, E], I32)
        idxt1, cm_idxt1 = p.sb_tmp("idxt1", [128, E], I32)
        idxt = [idxt0, idxt1]
        xdr0, cm_xdr0 = p.sb_tmp("xdrow0", [128, DIM], BF16)
        xdr1, cm_xdr1 = p.sb_tmp("xdrow1", [128, DIM], BF16)
        xdrow = [xdr0, xdr1]
        w1s0_, cm_w1s0 = p.sb_tmp("w1s0", [128, DC * 128], BF16)
        w1s1_, cm_w1s1 = p.sb_tmp("w1s1", [128, DC * 128], BF16)
        w1s = [w1s0_, w1s1_]
        w3s0_, cm_w3s0 = p.sb_tmp("w3s0", [128, DC * 128], BF16)
        w3s1_, cm_w3s1 = p.sb_tmp("w3s1", [128, DC * 128], BF16)
        w3s = [w3s0_, w3s1_]
        slbf0_, cm_slbf0 = p.sb_tmp("slbf0", [128, TC], BF16)
        slbf1_, cm_slbf1 = p.sb_tmp("slbf1", [128, TC], BF16)
        slbf = [slbf0_, slbf1_]
        xdT, cm_xdT = p.sb_tmp("xdT", [128, DC * TC], BF16)
        ps_t0, cm_pst0 = p.ps_tmp("ps_t0", [128, 128], BF16)
        ps_t1, cm_pst1 = p.ps_tmp("ps_t1", [128, 128], BF16)
        pst = [ps_t0, ps_t1]
        ix_uses = [0, 0]
        gx_uses = [0, 0]
        for i in range(8):
            sl = i % 2
            if ix_uses[sl] > 0:
                p.wait("gpsimd", s_gx[sl], 16 * gx_uses[sl])
            p.on("gpsimd", lambda g, i=i, sl=sl: g.dma_start(
                out=idxt[sl][:], in_=st8[i * 128:(i + 1) * 128, :]))
            ix_uses[sl] += 1
            p.inc("gpsimd", s_ix[sl], 16)
            p.wait("gpsimd", s_ix[sl], 16 * ix_uses[sl])
            if i >= 2:
                p.wait("gpsimd", s_mm)
            p.on("gpsimd", lambda g, sl=sl: g.indirect_dma_start(
                out=xdrow[sl][:], out_offset=None,
                in_=X2ALL, in_offset=IndirectOffsetOnAxis(idxt[sl][:, 0:1], 0)))
            gx_uses[sl] += 1
            p.inc("gpsimd", s_gx[sl], 16)
            p.wait("tensor", s_gx[sl], 16 * gx_uses[sl])
            for dc in range(DC):
                psl2 = pst[dc % 2]
                p.on("tensor", lambda t, i=i, dc=dc, sl=sl, psl2=psl2: t.transpose(
                    psl2[:], xdrow[sl][:, dc * 128:(dc + 1) * 128], ident[:]))
                trm = p.inc("tensor", s_mm)
                p.wait("vector", s_mm, trm)
                p.on("vector", lambda v, i=i, dc=dc, psl2=psl2: v.tensor_copy(
                    xdT[:, dc * TC + i * 128: dc * TC + (i + 1) * 128], psl2[:]))
                cpm = p.inc("vector", s_dv)
                p.wait("tensor", s_dv, cpm)

        w1_uses = [0, 0]

        def h_phase(W1src, W3src, xsrc, xready_sem, xready_val, ph1s, ph3s,
                    ev_gate=None):
            gdv = {}
            for hc in range(HC):
                sl = hc % 2
                if w1_uses[sl] > 0:
                    p.wait("gpsimd", s_mm)
                p.on("gpsimd", lambda g, hc=hc, sl=sl, W1src=W1src: g.dma_start(
                    out=w1s[sl][:].rearrange("p (dc j) -> p dc j", j=128),
                    in_=W1src[:, hc * 128:(hc + 1) * 128].rearrange("(dc p) j -> p dc j", p=128)))
                p.inc("gpsimd", s_w1[sl], 16)
                p.on("gpsimd", lambda g, hc=hc, sl=sl, W3src=W3src: g.dma_start(
                    out=w3s[sl][:].rearrange("p (dc j) -> p dc j", j=128),
                    in_=W3src[:, hc * 128:(hc + 1) * 128].rearrange("(dc p) j -> p dc j", p=128)))
                p.inc("gpsimd", s_w1[sl], 16)
                w1_uses[sl] += 1
                p.wait("tensor", s_w1[sl], 32 * w1_uses[sl])
                if hc == 0:
                    p.wait("tensor", xready_sem, xready_val)
                ph1, ph3 = ph1s[sl], ph3s[sl]
                if hc == 0 and ev_gate is not None:
                    p.wait("tensor", s_ev, ev_gate)
                if hc >= 2:
                    p.wait("tensor", s_ev)
                    p.wait("tensor", s_dv)
                for half in range(2):
                    for dc in range(DC):
                        p.on("tensor", lambda t, hc=hc, dc=dc, half=half, sl=sl, ph1=ph1, xsrc=xsrc: t.matmul(
                            ph1[:, half * 512:(half + 1) * 512],
                            w1s[sl][:, dc * 128:(dc + 1) * 128],
                            xsrc[:, dc * TC + half * 512: dc * TC + (half + 1) * 512],
                            start=(dc == 0), stop=(dc == DC - 1)))
                    for dc in range(DC):
                        p.on("tensor", lambda t, hc=hc, dc=dc, half=half, sl=sl, ph3=ph3, xsrc=xsrc: t.matmul(
                            ph3[:, half * 512:(half + 1) * 512],
                            w3s[sl][:, dc * 128:(dc + 1) * 128],
                            xsrc[:, dc * TC + half * 512: dc * TC + (half + 1) * 512],
                            start=(dc == 0), stop=(dc == DC - 1)))
                hm = p.inc("tensor", s_mm)
                p.wait("scalar", s_mm, hm)
                if hc >= 2:
                    p.wait("scalar", s_dv)
                p.on("scalar", lambda s, sl=sl, ph1=ph1: s.activation(
                    slbf[sl][:], ph1[:], AF.Silu))
                sev = p.inc("scalar", s_ev)
                p.wait("vector", s_ev, sev)
                p.on("vector", lambda v, hc=hc, sl=sl, ph3=ph3: v.tensor_mul(
                    g_sb[:, hc * TC:(hc + 1) * TC], slbf[sl][:], ph3[:]))
                gdv[hc] = p.inc("vector", s_dv)
            return gdv

        p.free(cm_pst1)
        p.free(cm_pst0)
        ph1a, cm_ph1a = p.ps_tmp("ph1a", [128, 1024])
        ph1b, cm_ph1b = p.ps_tmp("ph1b", [128, 1024])
        ph3a, cm_ph3a = p.ps_tmp("ph3a", [128, 1024])
        ph3b, cm_ph3b = p.ps_tmp("ph3b", [128, 1024])
        xd_ready = p.ctr[s_dv]
        gdvE = h_phase(W1c, W3c, xdT, s_dv, xd_ready, [ph1a, ph1b], [ph3a, ph3b])
        glastE = gdvE[HC - 1]

        p.free(cm_ph3b)
        p.free(cm_ph3a)
        p.free(cm_ph1b)
        p.free(cm_ph1a)
        p.free(cm_xdT)
        w2c0_, cm_w2c0 = p.sb_tmp("w2col0", [128, HC * 512], BF16)
        w2c1_, cm_w2c1 = p.sb_tmp("w2col1", [128, HC * 512], BF16)
        w2col = [w2c0_, w2c1_]
        oes0_, cm_oes0 = p.sb_tmp("oest0", [128, 512], BF16)
        oes1_, cm_oes1 = p.sb_tmp("oest1", [128, 512], BF16)
        oest = [oes0_, oes1_]
        psO0, cm_psO0 = p.ps_tmp("psO0", [128, 512])
        psO1, cm_psO1 = p.ps_tmp("psO1", [128, 512])
        psOs = [psO0, psO1]
        w2_uses = [0, 0]
        so_uses = [0, 0]
        ev3 = []

        def w2_pass(W2src, outdst, gready, tag):
            gidx = [0]
            for on4 in range(4):
                sl = on4 % 2
                if w2_uses[sl] > 0 or on4 == 0:
                    p.wait("gpsimd", s_mm)
                p.on("gpsimd", lambda g, on4=on4, sl=sl, W2src=W2src: g.dma_start(
                    out=w2col[sl][:].rearrange("p (hc j) -> p hc j", j=512),
                    in_=W2src[:, on4 * 512:(on4 + 1) * 512].rearrange("(hc p) j -> p hc j", p=128)))
                w2_uses[sl] += 1
                p.inc("gpsimd", s_w2[sl], 16)
                p.wait("tensor", s_w2[sl], 16 * w2_uses[sl])
                if on4 == 0:
                    p.wait("tensor", s_dv, gready)
                for tc8 in range(8):
                    gi = gidx[0]
                    gidx[0] += 1
                    ps = psOs[gi % 2]
                    stsl = gi % 2
                    if gi >= 2:
                        p.wait("tensor", s_ev, ev3[-2])
                    for hc in range(HC):
                        p.on("tensor", lambda t, hc=hc, tc8=tc8, on4=on4, sl=sl, ps=ps: t.matmul(
                            ps[:],
                            g_sb[:, hc * TC + tc8 * 128: hc * TC + (tc8 + 1) * 128],
                            w2col[sl][:, hc * 512:(hc + 1) * 512],
                            start=(hc == 0), stop=(hc == HC - 1)))
                    om = p.inc("tensor", s_mm)
                    p.wait("scalar", s_mm, om)
                    if so_uses[stsl] > 0:
                        p.wait("scalar", s_so[stsl], 16 * so_uses[stsl])
                    p.on("scalar", lambda s, ps=ps, stsl=stsl: s.activation(
                        oest[stsl][:], ps[:], AF.Identity))
                    ev = p.inc("scalar", s_ev)
                    ev3.append(ev)
                    p.wait("sync", s_ev, ev)
                    p.on("sync", lambda sy, tc8=tc8, on4=on4, stsl=stsl, outdst=outdst: sy.dma_start(
                        out=outdst[tc8 * 128:(tc8 + 1) * 128, on4 * 512:(on4 + 1) * 512],
                        in_=oest[stsl][:]))
                    so_uses[stsl] += 1
                    p.inc("sync", s_so[stsl], 16)

        w2_pass(W2c, OE, glastE, "E")

        p.free(cm_psO1)
        p.free(cm_psO0)
        ph1a2, cm_ph1a2 = p.ps_tmp("ph1a2", [128, 1024])
        ph1b2, cm_ph1b2 = p.ps_tmp("ph1b2", [128, 1024])
        ph3a2, cm_ph3a2 = p.ps_tmp("ph3a2", [128, 1024])
        ph3b2, cm_ph3b2 = p.ps_tmp("ph3b2", [128, 1024])
        for dc in range(DC):
            p.on("gpsimd", lambda g, dc=dc: g.dma_start(
                out=x2tb[:, dc * TC:(dc + 1) * TC], in_=X2Ti[dc * 128:(dc + 1) * 128, :]))
            p.inc("gpsimd", s_x2, 16)
        x2_ready_val = p.ctr[s_x2]
        gdvS = h_phase(SW1, SW3, x2tb, s_x2, x2_ready_val,
                       [ph1a2, ph1b2], [ph3a2, ph3b2], ev_gate=ev3[-1])
        glastS = gdvS[HC - 1]

        sw0_, cm_sw0 = p.sb_tmp("sw2oc0", [128, HC * 128], BF16)
        sw1_, cm_sw1t = p.sb_tmp("sw2oc1", [128, HC * 128], BF16)
        sw2oc = [sw0_, sw1_]
        x2f0_, cm_x2f0 = p.sb_tmp("x2f0", [128, TC], F32)
        x2f1_, cm_x2f1 = p.sb_tmp("x2f1", [128, TC], F32)
        x2f = [x2f0_, x2f1_]
        p2s0_, cm_p2s0 = p.sb_tmp("p2st0", [128, TC], F32)
        p2s1_, cm_p2s1 = p.sb_tmp("p2st1", [128, TC], F32)
        p2st = [p2s0_, p2s1_]
        p.free(cm_ph3b2)
        p.free(cm_ph3a2)
        p.free(cm_ph1b2)
        p.free(cm_ph1a2)
        psP0, cm_psP0 = p.ps_tmp("psP0", [128, 1024])
        psP1, cm_psP1 = p.ps_tmp("psP1", [128, 1024])
        psPs = [psP0, psP1]
        sw2_uses = [0, 0]
        xf_uses = [0, 0]
        sp_uses = [0, 0]
        dvs_ms = []
        for oc in range(DC):
            sl = oc % 2
            if sw2_uses[sl] > 0:
                p.wait("gpsimd", s_mm)
            p.on("gpsimd", lambda g, oc=oc, sl=sl: g.dma_start(
                out=sw2oc[sl][:].rearrange("p (hc j) -> p hc j", j=128),
                in_=SW2[:, oc * 128:(oc + 1) * 128].rearrange("(hc p) j -> p hc j", p=128)))
            sw2_uses[sl] += 1
            p.inc("gpsimd", s_w2[sl], 16)
            if xf_uses[sl] > 0:
                p.wait("gpsimd", s_dv, dvs_ms[oc - 2])
            p.on("gpsimd", lambda g, oc=oc, sl=sl: g.dma_start(
                out=x2f[sl][:], in_=X2Ti[oc * 128:(oc + 1) * 128, :]))
            xf_uses[sl] += 1
            p.inc("gpsimd", s_xf[sl], 16)
            ps = psPs[oc % 2]
            p.wait("tensor", s_w2[sl], 16 * (w2_uses[sl] + sw2_uses[sl]))
            if oc == 0:
                p.wait("tensor", s_dv, glastS)
            if oc >= 2:
                p.wait("tensor", s_dv, dvs_ms[oc - 2])
            for half in range(2):
                for hc in range(HC):
                    p.on("tensor", lambda t, oc=oc, hc=hc, half=half, sl=sl, ps=ps: t.matmul(
                        ps[:, half * 512:(half + 1) * 512],
                        sw2oc[sl][:, hc * 128:(hc + 1) * 128],
                        g_sb[:, hc * TC + half * 512: hc * TC + (half + 1) * 512],
                        start=(hc == 0), stop=(hc == HC - 1)))
            pm = p.inc("tensor", s_mm)
            p.wait("vector", s_mm, pm)
            p.wait("vector", s_xf[sl], 16 * xf_uses[sl])
            if sp_uses[sl] > 0:
                p.wait("vector", s_sp[sl], 16 * sp_uses[sl])
            p.on("vector", lambda v, sl=sl, ps=ps: v.tensor_add(p2st[sl][:], ps[:], x2f[sl][:]))
            dv_ = p.inc("vector", s_dv)
            dvs_ms.append(dv_)
            p.wait("sync", s_dv, dv_)
            p.on("sync", lambda sy, oc=oc, sl=sl: sy.dma_start(
                out=PRE2AT[oc * 128:(oc + 1) * 128, :], in_=p2st[sl][:]))
            sp_uses[sl] += 1
            p.inc("sync", s_sp[sl], 16)

        for sl in range(2):
            p.wait("sync", s_so[sl], 16 * so_uses[sl])
            p.wait("sync", s_sp[sl], 16 * sp_uses[sl])
        p.wait("sync", s_sfg)
        p.emit(block)
    return nc


def _host_l3_inputs(inputs, core, l2res):
    x2all = np.concatenate([np.asarray(l2res[c]["X2TOK"]) for c in range(NCORE)], axis=0)
    logall = np.concatenate([np.asarray(l2res[c]["LOGITS"]) for c in range(NCORE)], axis=0)
    d = {
        "X2Ti": np.asarray(l2res[core]["X2T"]),
        "X2ALL": x2all,
        "LOGALL": logall,
        "W1c": np.ascontiguousarray(np.asarray(inputs["L0_ffn_W1"], np.float32)[core]),
        "W3c": np.ascontiguousarray(np.asarray(inputs["L0_ffn_W3"], np.float32)[core]),
        "W2c": np.ascontiguousarray(np.asarray(inputs["L0_ffn_W2"], np.float32)[core]),
        "SW1": np.asarray(inputs["L0_ffn_sw1"], np.float32),
        "SW3": np.asarray(inputs["L0_ffn_sw3"], np.float32),
        "SW2": np.asarray(inputs["L0_ffn_sw2"], np.float32),
        "MYEXP": np.full((1, 1), float(core), np.float32),
        "LT": np.triu(np.ones((128, 128), np.float32), 1),
        "IDENT": np.eye(128, dtype=np.float32),
    }
    return d



def build_l4():
    nc = bass.Bass("TRN2", target_bir_lowering=False)
    OEALL = nc.dram_tensor("OEALL", [T, DIM], BF16, kind="ExternalInput").ap()
    PRE2A = nc.dram_tensor("PRE2A", [TC, DIM], F32, kind="ExternalInput").ap()
    FLATO = nc.dram_tensor("FLATO", [128, 8], I32, kind="ExternalInput").ap()
    GKO = nc.dram_tensor("GKO", [128, 8], F32, kind="ExternalInput").ap()
    OUT = nc.dram_tensor("OUT", [TC, DIM], F32, kind="ExternalOutput").ap()

    with contextlib.ExitStack() as stack:
        p = Prog(nc, stack)
        flato = p.sb("flato", [128, 8], I32)
        gko = p.sb("gko", [128, 8], F32)
        oeg = [p.sb("oeg0", [128, DIM], BF16), p.sb("oeg1", [128, DIM], BF16)]
        pre2 = [p.sb("pre20", [128, DIM], F32), p.sb("pre21", [128, DIM], F32)]
        pfin = [p.sb("pfin0", [128, DIM], F32), p.sb("pfin1", [128, DIM], F32)]
        sqf = p.sb("sqf", [128, DIM], F32)
        srow = p.sb("srow", [128, 1], F32)
        qrow = p.sb("qrow", [128, 1], F32)
        mrow = p.sb("mrow", [128, 1], F32)
        trow = p.sb("trow", [128, 1], F32)
        rrow = p.sb("rrow", [128, 1], F32)
        stg = [p.sb("stg0", [128, DIM], F32), p.sb("stg1", [128, DIM], F32)]
        block = stack.enter_context(nc.Block())
        s_c = p.sem("s_c")
        s_g = [p.sem("s_g0"), p.sem("s_g1")]
        s_p = [p.sem("s_p0"), p.sem("s_p1")]
        s_dv = p.sem("s_dv")
        s_ev = p.sem("s_ev")
        s_st = [p.sem("s_st0"), p.sem("s_st1")]

        p.on("gpsimd", lambda g: g.dma_start(out=flato[:], in_=FLATO[:, :]))
        p.inc("gpsimd", s_c, 16)
        p.on("gpsimd", lambda g: g.dma_start(out=gko[:], in_=GKO[:, :]))
        p.inc("gpsimd", s_c, 16)
        p.wait("gpsimd", s_c)
        g_uses = [0, 0]
        p_uses = [0, 0]
        st_uses = [0, 0]
        cons_ms = {}

        def vstep(fn):
            p.on("vector", fn)
            ms = p.inc("vector", s_dv)
            p.wait("vector", s_dv, ms)
            return ms

        for tc in range(8):
            sl = tc % 2
            if tc >= 2:
                p.wait("gpsimd", s_dv, cons_ms[tc - 2])
            p.on("gpsimd", lambda g, tc=tc, sl=sl: g.indirect_dma_start(
                out=oeg[sl][:], out_offset=None,
                in_=OEALL, in_offset=IndirectOffsetOnAxis(flato[:, tc:tc + 1], 0)))
            g_uses[sl] += 1
            p.inc("gpsimd", s_g[sl], 16)
            p.on("gpsimd", lambda g, tc=tc, sl=sl: g.dma_start(
                out=pre2[sl][:], in_=PRE2A[tc * 128:(tc + 1) * 128, :]))
            p_uses[sl] += 1
            p.inc("gpsimd", s_p[sl], 16)
            p.wait("vector", s_g[sl], 16 * g_uses[sl])
            p.wait("vector", s_p[sl], 16 * p_uses[sl])
            p.on("vector", lambda v, tc=tc, sl=sl: v.scalar_tensor_tensor(
                pfin[sl][:], oeg[sl][:], gko[:, tc:tc + 1], pre2[sl][:],
                ALU.mult, ALU.add))
            cm = p.inc("vector", s_dv)
            cons_ms[tc] = cm
            p.wait("vector", s_dv, cm)
            vstep(lambda v, sl=sl: v.reduce_sum(srow[:], pfin[sl][:], axis=AX.X))
            p.wait("scalar", s_dv)
            p.on("scalar", lambda s, sl=sl: s.activation(sqf[:], pfin[sl][:], AF.Square))
            sq_ev = p.inc("scalar", s_ev)
            p.wait("vector", s_ev, sq_ev)
            vstep(lambda v: v.reduce_sum(qrow[:], sqf[:], axis=AX.X))
            vstep(lambda v: v.tensor_scalar(mrow[:], srow[:], 1.0 / DIM, None, ALU.mult))
            vstep(lambda v: v.tensor_mul(trow[:], mrow[:], mrow[:]))
            vstep(lambda v: v.tensor_scalar(qrow[:], qrow[:], 1.0 / DIM, None, ALU.mult))
            vstep(lambda v: v.tensor_sub(qrow[:], qrow[:], trow[:]))
            veps = vstep(lambda v: v.tensor_scalar(qrow[:], qrow[:], EPS, None, ALU.add))
            p.wait("scalar", s_dv, veps)
            p.on("scalar", lambda s: s.activation(trow[:], qrow[:], AF.Ln))
            ln_ev = p.inc("scalar", s_ev)
            p.wait("scalar", s_ev, ln_ev)
            p.on("scalar", lambda s: s.activation(rrow[:], trow[:], AF.Exp, scale=-0.5))
            ex_ev = p.inc("scalar", s_ev)
            p.wait("vector", s_ev, ex_ev)
            if st_uses[sl] > 0:
                p.wait("vector", s_st[sl], 16 * st_uses[sl])
            fin = vstep(lambda v, sl=sl: v.tensor_scalar(
                stg[sl][:], pfin[sl][:], mrow[:], rrow[:], ALU.subtract, ALU.mult))
            p.wait("sync", s_dv, fin)
            p.on("sync", lambda sy, tc=tc, sl=sl: sy.dma_start(
                out=OUT[tc * 128:(tc + 1) * 128, :], in_=stg[sl][:]))
            st_uses[sl] += 1
            p.inc("sync", s_st[sl], 16)

        for sl in range(2):
            p.wait("sync", s_st[sl], 16 * st_uses[sl])
        p.emit(block)
    return nc


def _host_l4_inputs(core, l3res):
    oeall = np.concatenate([np.asarray(l3res[c]["OE"]) for c in range(NCORE)], axis=0)
    flat = np.asarray(l3res[0]["FLAT"]).astype(np.int32)[:, 0]
    gk = np.asarray(l3res[0]["GK"]).astype(np.float32)[:, 0]
    fo = flat[core * TC:(core + 1) * TC].reshape(8, 128).T.copy()
    go = gk[core * TC:(core + 1) * TC].reshape(8, 128).T.copy()
    pre2a = np.ascontiguousarray(np.asarray(l3res[core]["PRE2AT"]).T)
    return {"OEALL": oeall, "PRE2A": pre2a, "FLATO": fo, "GKO": go}


def kernel(**inputs):
    return kernel_fused(**inputs)


def kernel_multilaunch(**inputs):
    nc1 = build_l1()
    res1 = run_bass_kernel_spmd(nc1, [_host_l1_inputs(inputs, c) for c in range(NCORE)],
                                core_ids=list(range(NCORE)))
    nc2 = build_l2()
    res2 = run_bass_kernel_spmd(nc2, [_host_l2_inputs(inputs, c, res1.results) for c in range(NCORE)],
                                core_ids=list(range(NCORE)))
    nc3 = build_l3()
    res3 = run_bass_kernel_spmd(nc3, [_host_l3_inputs(inputs, c, res2.results) for c in range(NCORE)],
                                core_ids=list(range(NCORE)))
    nc4 = build_l4()
    res4 = run_bass_kernel_spmd(nc4, [_host_l4_inputs(c, res3.results) for c in range(NCORE)],
                                core_ids=list(range(NCORE)))
    out = np.concatenate([np.asarray(res4.results[c]["OUT"]) for c in range(NCORE)], axis=0)
    return out.reshape(2, 4096, DIM)



def build_fused(stop_phase=99):
    nc = bass.Bass("TRN2", target_bir_lowering=False)
    xT = nc.dram_tensor("xT", [DIM, TC], F32, kind="ExternalInput").ap()
    W = {}
    for pch in "qkvo":
        W[pch] = {
            "WT": nc.dram_tensor(f"W{pch}T", [DIM, DIM], F32, kind="ExternalInput").ap(),
            "BT": nc.dram_tensor(f"B{pch}T", [DIM, RANK], F32, kind="ExternalInput").ap(),
            "AT": nc.dram_tensor(f"A{pch}T", [RANK, DIM], F32, kind="ExternalInput").ap(),
            "b": nc.dram_tensor(f"b{pch}", [128, DC], F32, kind="ExternalInput").ap(),
            "brow": nc.dram_tensor(f"b{pch}row", [1, DIM], F32, kind="ExternalInput").ap(),
        }
    Wg = nc.dram_tensor("Wg", [DIM, E], F32, kind="ExternalInput").ap()
    W1c = nc.dram_tensor("W1c", [DIM, H], F32, kind="ExternalInput").ap()
    W3c = nc.dram_tensor("W3c", [DIM, H], F32, kind="ExternalInput").ap()
    W2c = nc.dram_tensor("W2c", [H, DIM], F32, kind="ExternalInput").ap()
    SW1 = nc.dram_tensor("SW1", [DIM, H], F32, kind="ExternalInput").ap()
    SW3 = nc.dram_tensor("SW3", [DIM, H], F32, kind="ExternalInput").ap()
    SW2 = nc.dram_tensor("SW2", [H, DIM], F32, kind="ExternalInput").ap()
    MYEXP = nc.dram_tensor("MYEXP", [1, 1], F32, kind="ExternalInput").ap()
    LT = nc.dram_tensor("LT", [128, 128], F32, kind="ExternalInput").ap()
    IDENT = nc.dram_tensor("IDENT", [128, 128], F32, kind="ExternalInput").ap()
    OUT_T = nc.dram_tensor("OUT_T", [DIM, TC], F32, kind="ExternalOutput").ap()
    KDBG = nc.dram_tensor("KDBG", [DIM, TC], BF16, kind="ExternalOutput").ap()
    agk_i = (nc.dram_tensor("agk_i", [DIM, TC], BF16) if not _F1_KEXT else None)
    if _F1_KEXT:
        agk_i = KDBG
    agk_o = nc.dram_tensor("agk_o", [4 * DIM, TC], BF16)
    agv_i = nc.dram_tensor("agv_i", [TC, DIM], BF16)
    agv_o = nc.dram_tensor("agv_o", [4 * TC, DIM], BF16)
    agn_i = nc.dram_tensor("agn_i", [1, HEADS], F32)
    agn_o = nc.dram_tensor("agn_o", [4, HEADS], F32)
    agx_i = nc.dram_tensor("agx_i", [TC, DIM], BF16)
    agx_o = nc.dram_tensor("agx_o", [T, DIM], BF16)
    agl_i = nc.dram_tensor("agl_i", [TC, E], F32)
    agl_o = nc.dram_tensor("agl_o", [T, E], F32)
    ago_i = nc.dram_tensor("ago_i", [CAP, DIM], BF16)
    ago_o = nc.dram_tensor("ago_o", [T, DIM], BF16)
    x2Td = nc.dram_tensor("x2Td", [DIM, TC], F32)
    p2Td = nc.dram_tensor("p2Td", [DIM, TC], F32)
    st8 = nc.dram_tensor("st8", [1152, E], I32)
    myf8 = nc.dram_tensor("myf8", [1152, E], I32)
    myg8 = nc.dram_tensor("myg8", [1152, E], F32)
    G14 = [[0, 1, 2, 3], [4, 5, 6, 7]]
    G18 = [list(range(8))]

    with contextlib.ExitStack() as stack:
        p = Prog(nc, stack)
        ones_col = p.sb("ones_col", [128, 1], BF16)
        ones_row = p.sb("ones_row", [1, 128], BF16)
        ones_rf = p.sb("ones_rf", [1, 128], F32)
        ident = p.sb("ident", [128, 128], BF16)
        qn_sb = p.sb("qn_sb", [1, HEADS], F32)
        negb = p.sb("negb", [128, HEADS], F32)
        krow = p.sb("krow", [1, 4 * HEADS], F32)
        kmax = p.sb("kmax", [1, HEADS], F32)
        wg_sb = p.sb("wg_sb", [128, DC * E], BF16)
        lt_sb = p.sb("lt_sb", [128, 128], BF16)
        myexp_sb = p.sb("myexp_sb", [1, 1], F32)
        mycol = p.sb("mycol", [128, 1], F32)
        basecol = p.sb("basecol", [128, 1], F32)
        ones8 = p.sb("ones8", [128, E], F32)

        block = stack.enter_context(nc.Block())
        s_st1 = [p.sem(f"s_st1_{i}") for i in range(4)]
        s_x = p.sem("s_x")
        s_w = [p.sem("s_w0"), p.sem("s_w1")]
        s_lc = [p.sem("s_lc0"), p.sem("s_lc1")]
        s_mm = p.sem("s_mm")
        s_ev = p.sem("s_ev")
        s_dv = p.sem("s_dv")
        s_gp = p.sem("s_gp")
        s_kst = p.sem("s_kst")
        s_cc = p.sem("s_cc")
        s_kh = [p.sem("s_kh0"), p.sem("s_kh1")]
        s_vh = [p.sem("s_vh0"), p.sem("s_vh1")]
        s_c = p.sem("s_c")
        s_xt = [p.sem("s_xt0"), p.sem("s_xt1")]
        s_x2st = p.sem("s_x2st")
        s_sc = [p.sem("s_sc0"), p.sem("s_sc1")]
        s_ix = [p.sem("s_ix0"), p.sem("s_ix1")]
        s_gx = [p.sem("s_gx0"), p.sem("s_gx1")]
        s_w1 = [p.sem("s_w10"), p.sem("s_w11")]
        s_w2 = [p.sem("s_w20"), p.sem("s_w21")]
        s_xf = [p.sem("s_xf0"), p.sem("s_xf1")]
        s_so = [p.sem("s_so0"), p.sem("s_so1")]
        s_sp = [p.sem("s_sp0"), p.sem("s_sp1")]
        s_z = p.sem("s_z")
        s_fo = p.sem("s_fo")
        s_s = p.sem("s_s")
        s_e = p.sem("s_e")
        s_av = p.sem("s_av")
        s_xts2 = [p.sem("s_xts20"), p.sem("s_xts21")]
        s_p2l = p.sem("s_p2l")
        s_if = [p.sem("s_if0"), p.sem("s_if1")]
        s_og = [p.sem("s_og0"), p.sem("s_og1")]
        s_fst = [p.sem("s_fst0"), p.sem("s_fst1")]
        s_sc2 = [p.sem("s_sc2_0"), p.sem("s_sc2_1")]

        NSTG = 4
        cat, cm_cat = p.sb_tmp("cat", [128, HEADS * TC], BF16)
        x2b, cm_x2b = p.sb_tmp("x2b", [128, DC * TC], BF16)
        qf, cm_qf = p.sb_tmp("qf", [128, HEADS * TC], BF16)
        xb, cm_xb = p.sb_tmp("xb", [128, DC * TC], BF16)
        wt0, cm_wt0 = p.sb_tmp("wt0", [128, DC * 1024], BF16)
        bts = [None, None]
        bts[0], cm_bt0 = p.sb_tmp("bt0", [128, DC * RANK], BF16)
        bts[1], cm_bt1 = p.sb_tmp("bt1", [128, DC * RANK], BF16)
        ats = [None, None]
        ats[0], cm_at0 = p.sb_tmp("at0", [RANK, DIM], BF16)
        ats[1], cm_at1 = p.sb_tmp("at1", [RANK, DIM], BF16)
        btils = [None, None]
        btils[0], cm_bl0 = p.sb_tmp("btile0", [128, DC], F32)
        btils[1], cm_bl1 = p.sb_tmp("btile1", [128, DC], F32)
        brws = [None, None]
        brws[0], cm_br0 = p.sb_tmp("brow0", [1, DIM], BF16)
        brws[1], cm_br1 = p.sb_tmp("brow1", [1, DIM], BF16)
        xlora, cm_xlora = p.sb_tmp("xlora", [RANK, TC], BF16)
        stg1 = [None] * NSTG
        cm_stg1 = [None] * NSTG
        for i in range(NSTG):
            stg1[i], cm_stg1[i] = p.sb_tmp(f"stg1_{i}", [128, TC], BF16)
        sq1, cm_sq1 = p.sb_tmp("sq1", [128, TC], BF16)
        kn_own, cm_kn_own = p.sb_tmp("kn_own", [1, HEADS], F32)
        psA, cm_psA = p.ps_tmp("psA", [128, 1024])
        psB, cm_psB = p.ps_tmp("psB", [128, 1024])
        psxb, cm_psxb = p.ps_tmp("psxb", [RANK, TC])
        psn, cm_psn = p.ps_tmp("psn", [1, 1024])
        pss1 = [psA, psB]

        for dc in range(DC):
            p.on("gpsimd", lambda g, dc=dc: g.dma_start(
                out=xb[:, dc * TC:(dc + 1) * TC], in_=xT[dc * 128:(dc + 1) * 128, :]))
            p.inc("gpsimd", s_x, 16)
        p.on("vector", lambda v: v.memset(ones_col[:], 1.0))
        p.inc("vector", s_dv)
        p.on("vector", lambda v: v.memset(ones_row[:], 1.0))
        p.inc("vector", s_dv)
        p.on("vector", lambda v: v.memset(ones_rf[:], 1.0))
        p.inc("vector", s_dv)
        p.on("vector", lambda v: v.memset(ones8[:], 1.0))
        v_const = p.inc("vector", s_dv)
        p.on("gpsimd", lambda g: g.dma_start(out=ident[:], in_=IDENT[:, :]))
        p.inc("gpsimd", s_c, 16)
        p.on("gpsimd", lambda g: g.dma_start(out=lt_sb[:], in_=LT[:, :]))
        p.inc("gpsimd", s_c, 16)
        p.on("gpsimd", lambda g: g.dma_start(out=myexp_sb[:], in_=MYEXP[:, :]))
        p.inc("gpsimd", s_c, 16)
        p.on("gpsimd", lambda g: g.dma_start(
            out=wg_sb[:].rearrange("p (dc e) -> p dc e", e=E),
            in_=Wg.rearrange("(dc p) e -> p dc e", p=128)))
        p.inc("gpsimd", s_c, 16)

        projs1 = list(_F1_PROJS)

        wbufs = [wt0, x2b]
        wload_n = [0]

        def load_w1(pi, half):
            name = projs1[pi]
            k = pi % 2
            buf = (2 * pi + half) % 2
            if wload_n[0] >= 2:
                p.wait("gpsimd", s_mm)
            wload_n[0] += 1
            for dc in range(DC):
                p.on("gpsimd", lambda g, dc=dc, name=name, half=half, buf=buf: g.dma_start(
                    out=wbufs[buf][:, dc * 1024:(dc + 1) * 1024],
                    in_=W[name]["WT"][dc * 128:(dc + 1) * 128,
                                      half * 1024:(half + 1) * 1024]))
                p.inc("gpsimd", s_w[buf], 16)
            if half == 0:
                p.on("gpsimd", lambda g, k=k, name=name, buf=buf: g.dma_start(
                    out=bts[k][:].rearrange("p (dc r) -> p dc r", r=RANK),
                    in_=W[name]["BT"].rearrange("(dc p) r -> p dc r", p=128)))
                p.inc("gpsimd", s_w[buf], 16)
                p.on("gpsimd", lambda g, k=k, name=name, buf=buf: g.dma_start(out=ats[k][:], in_=W[name]["AT"][:, :]))
                p.inc("gpsimd", s_w[buf], 16)
                p.on("gpsimd", lambda g, k=k, name=name, buf=buf: g.dma_start(out=btils[k][:], in_=W[name]["b"][:, :]))
                p.inc("gpsimd", s_w[buf], 16)
                p.on("gpsimd", lambda g, k=k, name=name, buf=buf: g.dma_start(out=brws[k][:], in_=W[name]["brow"][:, :]))
                p.inc("gpsimd", s_w[buf], 16)
            return buf, p.ctr[s_w[buf]]

        ev1_ms = []
        stg1_uses = [0] * NSTG
        gi1 = [0]

        def proj_group(emit_mms, evac_fn, store_fn, norm_oc=None):
            gi = gi1[0]
            gi1[0] += 1
            ps = pss1[gi % 2]
            slot = gi % NSTG
            stg = stg1[slot]
            if gi >= 2:
                p.wait("tensor", s_ev, ev1_ms[gi - 2])
            emit_mms(ps)
            mm_grp = p.inc("tensor", s_mm)
            p.wait("scalar", s_mm, mm_grp)
            if store_fn is not None and stg1_uses[slot] > 0:
                p.wait("scalar", s_st1[slot], 16 * stg1_uses[slot])
            evac_fn(ps, stg)
            ev = p.inc("scalar", s_ev)
            ev1_ms.append(ev)
            if store_fn is not None and _F1_STORES:
                p.wait("sync", s_ev, ev)
                store_fn(stg)
                stg1_uses[slot] += 1
                p.inc("sync", s_st1[slot], 16)
            if norm_oc is not None:
                oc, ndst = norm_oc
                p.wait("scalar", s_ev, ev)
                p.on("scalar", lambda s, stg=stg: s.activation(sq1[:], stg[:], AF.Square))
                ev_sq = p.inc("scalar", s_ev)
                ev1_ms[-1] = ev_sq
                p.wait("tensor", s_ev, ev_sq)
                p.wait("tensor", s_dv)
                for half in range(2):
                    p.on("tensor", lambda t, half=half: t.matmul(
                        psn[:, half * 512:(half + 1) * 512], ones_col[:],
                        sq1[:, half * 512:(half + 1) * 512], start=True, stop=True))
                mm_n = p.inc("tensor", s_mm)
                p.wait("vector", s_mm, mm_n)
                p.on("vector", lambda v, oc=oc, ndst=ndst: v.tensor_reduce(
                    ndst[0:1, oc:oc + 1], psn[0:1, :], axis=AX.X, op=ALU.max))
                p.inc("vector", s_dv)
            return ev

        wq = {}
        wq[(0, 0)] = load_w1(0, 0)
        wq[(0, 1)] = load_w1(0, 1)
        for pi, name in enumerate(projs1):
            k = pi % 2
            buf, wv = wq[(pi, 0)]
            p.wait("tensor", s_w[buf], wv)
            wcur = [wbufs[buf]]
            if (pi, 1) not in wq:
                wq[(pi, 1)] = load_w1(pi, 1)
            if pi == 0:
                p.wait("tensor", s_x)

            for half in range(2):
                for dc in range(DC):
                    p.on("tensor", lambda t, dc=dc, half=half, k=k: t.matmul(
                        psxb[:, half * 512:(half + 1) * 512],
                        bts[k][:, dc * RANK:(dc + 1) * RANK],
                        xb[:, dc * TC + half * 512: dc * TC + (half + 1) * 512],
                        start=(dc == 0), stop=(dc == DC - 1)))
            lomm = p.inc("tensor", s_mm)
            p.wait("scalar", s_mm, lomm)
            p.on("scalar", lambda s: s.activation(xlora[:], psxb[:], AF.Identity, scale=SCALING))
            ev_lora = p.inc("scalar", s_ev)

            if name in ("q", "k"):
                for oc in range(DC):
                    if oc == 8:
                        buf, wv = wq[(pi, 1)]
                        p.wait("tensor", s_w[buf], wv)
                        wcur[0] = wbufs[buf]
                        nxt = (pi + 1, 0)
                        if nxt[0] < len(projs1) and nxt not in wq:
                            wq[nxt] = load_w1(*nxt)

                    def mms(ps, oc=oc, k=k, ev_lora=ev_lora):
                        if oc == 0:
                            p.wait("tensor", s_ev, ev_lora)
                        for half in range(2):
                            for dc in range(DC):
                                p.on("tensor", lambda t, oc=oc, dc=dc, half=half, ps=ps, wb=wcur[0]: t.matmul(
                                    ps[:, half * 512:(half + 1) * 512],
                                    wb[:, dc * 1024 + (oc % 8) * 128: dc * 1024 + ((oc % 8) + 1) * 128],
                                    xb[:, dc * TC + half * 512: dc * TC + (half + 1) * 512],
                                    start=(dc == 0), stop=False))
                            p.on("tensor", lambda t, oc=oc, half=half, ps=ps, k=k: t.matmul(
                                ps[:, half * 512:(half + 1) * 512],
                                ats[k][:, oc * 128:(oc + 1) * 128],
                                xlora[:, half * 512:(half + 1) * 512],
                                start=False, stop=True))

                    if name == "q":
                        def evac(ps, stg, oc=oc, k=k):
                            p.on("scalar", lambda s, ps=ps, oc=oc, k=k: s.activation(
                                qf[:, oc * TC:(oc + 1) * TC], ps[:], AF.Identity,
                                bias=btils[k][:, oc:oc + 1]))
                        proj_group(mms, evac, None,
                                   norm_oc=None)
                        if _F1_NORMS:
                            p.wait("scalar", s_ev)
                            p.on("scalar", lambda s, oc=oc: s.activation(
                                sq1[:], qf[:, oc * TC:(oc + 1) * TC], AF.Square))
                            ev_sq = p.inc("scalar", s_ev)
                            ev1_ms[-1] = ev_sq
                            p.wait("tensor", s_ev, ev_sq)
                            p.wait("tensor", s_dv)
                            for half in range(2):
                                p.on("tensor", lambda t, half=half: t.matmul(
                                    psn[:, half * 512:(half + 1) * 512], ones_col[:],
                                    sq1[:, half * 512:(half + 1) * 512], start=True, stop=True))
                            mm_n = p.inc("tensor", s_mm)
                            p.wait("vector", s_mm, mm_n)
                            p.on("vector", lambda v, oc=oc: v.tensor_reduce(
                                qn_sb[0:1, oc:oc + 1], psn[0:1, :], axis=AX.X, op=ALU.max))
                            p.inc("vector", s_dv)
                    else:
                        def evac(ps, stg, oc=oc, k=k):
                            p.on("scalar", lambda s, ps=ps, stg=stg, oc=oc, k=k: s.activation(
                                stg[:], ps[:], AF.Identity, bias=btils[k][:, oc:oc + 1]))

                        def store(stg, oc=oc):
                            p.on("sync", lambda sy, oc=oc, stg=stg: sy.dma_start(
                                out=agk_i[oc * 128:(oc + 1) * 128, :], in_=stg[:]))
                        proj_group(mms, evac, store,
                                   norm_oc=(oc, kn_own) if _F1_NORMS else None)
            else:
                for on4 in range(4):
                    if on4 == 2:
                        buf, wv = wq[(pi, 1)]
                        p.wait("tensor", s_w[buf], wv)
                        wcur[0] = wbufs[buf]
                    for tc8 in range(8):
                        def mms(ps, tc8=tc8, on4=on4, k=k, ev_lora=ev_lora):
                            if tc8 == 0 and on4 == 0:
                                p.wait("tensor", s_ev, ev_lora)
                                p.wait("tensor", s_dv, v_const)
                            for dc in range(DC):
                                p.on("tensor", lambda t, tc8=tc8, on4=on4, dc=dc, ps=ps, wb=wcur[0]: t.matmul(
                                    ps[:, 0:512],
                                    xb[:, dc * TC + tc8 * 128: dc * TC + (tc8 + 1) * 128],
                                    wb[:, dc * 1024 + (on4 % 2) * 512: dc * 1024 + ((on4 % 2) + 1) * 512],
                                    start=(dc == 0), stop=False))
                            p.on("tensor", lambda t, tc8=tc8, ps=ps, k=k: t.matmul(
                                ps[:, 0:512],
                                xlora[:, tc8 * 128:(tc8 + 1) * 128],
                                ats[k][:, on4 * 512:(on4 + 1) * 512],
                                start=False, stop=False))
                            p.on("tensor", lambda t, on4=on4, ps=ps, k=k: t.matmul(
                                ps[:, 0:512], ones_row[:],
                                brws[k][0:1, on4 * 512:(on4 + 1) * 512],
                                start=False, stop=True))

                        def evac(ps, stg):
                            p.on("scalar", lambda s, ps=ps, stg=stg: s.activation(
                                stg[:, 0:512], ps[:, 0:512], AF.Identity))

                        def store(stg, tc8=tc8, on4=on4):
                            p.on("sync", lambda sy, tc8=tc8, on4=on4, stg=stg: sy.dma_start(
                                out=agv_i[tc8 * 128:(tc8 + 1) * 128, on4 * 512:(on4 + 1) * 512],
                                in_=stg[:, 0:512]))
                        proj_group(mms, evac, store, norm_oc=None)

        p.wait("sync", s_dv)
        p.on("sync", lambda sy: sy.dma_start(out=agn_i[:, :], in_=kn_own[:]))
        p.inc("sync", s_kst, 16)

        if stop_phase <= 0:
            p.drain()
            p.free_all()
            p.emit(block)
            return nc
        p.wait("gpsimd", s_kst)
        p.on("gpsimd", lambda g: g.collective_compute(
            "AllGather", ALU.bypass, replica_groups=G14,
            ins=[agk_i.ap().opt()], outs=[agk_o.ap().opt()]))
        p.inc("gpsimd", s_cc)
        p.on("gpsimd", lambda g: g.collective_compute(
            "AllGather", ALU.bypass, replica_groups=G14,
            ins=[agv_i.ap().opt()], outs=[agv_o.ap().opt()]))
        p.inc("gpsimd", s_cc)
        p.on("gpsimd", lambda g: g.collective_compute(
            "AllGather", ALU.bypass, replica_groups=G14,
            ins=[agn_i.ap().opt()], outs=[agn_o.ap().opt()]))
        cc_kv = p.inc("gpsimd", s_cc)

        p.free(cm_psn); p.free(cm_psxb); p.free(cm_psB); p.free(cm_psA)
        p.free(cm_kn_own)
        p.free(cm_sq1)
        for i in (3, 2, 1, 0):
            p.free(cm_stg1[i])
        p.free(cm_xlora)
        p.free(cm_br1); p.free(cm_br0)
        p.free(cm_bl1); p.free(cm_bl0)
        p.free(cm_at1); p.free(cm_at0)
        p.free(cm_bt1); p.free(cm_bt0)
        p.free(cm_wt0)
        p.free(cm_xb)

        if stop_phase <= 1:
            p.drain()
            p.free_all()
            p.emit(block)
            return nc
        nrow, cm_nrow = p.sb_tmp("nrow", [1, HEADS], F32)
        r_row, cm_rrow = p.sb_tmp("r_row", [1, 512], F32)
        rt, cm_rt = p.sb_tmp("rt", [128, 512], F32)
        kh0, cm_kh0 = p.sb_tmp("kh0", [128, 4 * TC], BF16)
        kh1, cm_kh1 = p.sb_tmp("kh1", [128, 4 * TC], BF16)
        vh0, cm_vh0 = p.sb_tmp("vh0", [128, 4 * TC], BF16)
        vh1, cm_vh1 = p.sb_tmp("vh1", [128, 4 * TC], BF16)
        e0, cm_e0 = p.sb_tmp("e0", [128, 512], BF16)
        e1, cm_e1 = p.sb_tmp("e1", [128, 512], BF16)
        ps_s0, cm_ps_s0 = p.ps_tmp("ps_s0", [128, 512])
        ps_s1, cm_ps_s1 = p.ps_tmp("ps_s1", [128, 512])
        ps_o, cm_ps_o = p.ps_tmp("ps_o", [128, 512])
        ps_g, cm_ps_g = p.ps_tmp("ps_g", [1, 512])
        ps_r, cm_ps_r = p.ps_tmp("ps_r", [128, 512])
        khs, vhs, es = [kh0, kh1], [vh0, vh1], [e0, e1]

        ph1_mm = p.ctr[s_mm]
        ph1_ev = p.ctr[s_ev]
        ph1_dv = p.ctr[s_dv]
        for eng in ("gpsimd",):
            p.wait(eng, s_mm, ph1_mm)
            p.wait(eng, s_ev, ph1_ev)
            p.wait(eng, s_kst)

        p.wait("gpsimd", s_cc, cc_kv)
        p.on("gpsimd", lambda g: g.dma_start(
            out=krow[:].rearrange("o (r h) -> o r h", h=HEADS), in_=agn_o[:, :]))
        p.inc("gpsimd", s_c, 16)
        p.wait("vector", s_c)
        p.on("vector", lambda v: v.tensor_reduce(
            kmax[0:1, :], krow[:].rearrange("o (r h) -> o h r", h=HEADS),
            axis=AX.X, op=ALU.max))
        knred = p.inc("vector", s_dv)
        p.wait("vector", s_dv, knred)
        p.on("vector", lambda v: v.tensor_mul(nrow[0:1, :], qn_sb[0:1, :], kmax[0:1, :]))
        nmul = p.inc("vector", s_dv)
        p.wait("scalar", s_dv, nmul)
        p.on("scalar", lambda s: s.activation(nrow[0:1, :], nrow[0:1, :], AF.Ln))
        eln = p.inc("scalar", s_ev)
        p.wait("scalar", s_ev, eln)
        p.on("scalar", lambda s: s.activation(nrow[0:1, :], nrow[0:1, :], AF.Exp, scale=0.5))
        eex = p.inc("scalar", s_ev)
        p.wait("vector", s_ev, eex)
        p.on("vector", lambda v: v.tensor_scalar(nrow[0:1, :], nrow[0:1, :], -SCALE, None, ALU.mult))
        nneg = p.inc("vector", s_dv)
        p.wait("tensor", s_dv, nneg)
        p.on("tensor", lambda t: t.matmul(ps_r[:, 0:HEADS], ones_rf[:], nrow[0:1, :],
                                          start=True, stop=True))
        bmm = p.inc("tensor", s_mm)
        p.wait("vector", s_mm, bmm)
        p.on("vector", lambda v: v.tensor_copy(negb[:], ps_r[:, 0:HEADS]))
        negb_done = p.inc("vector", s_dv)

        kh_uses = [0, 0]
        vh_uses = [0, 0]
        s_cnt = [0]
        av_cnt = [0]
        kh_ms = {}
        vh_ms = {}
        NKC = 4 * TC // 128

        def load_head(h):
            sl = h % 2
            if h >= 2:
                p.wait("gpsimd", s_s, 64 * (h - 1))
                p.wait("gpsimd", s_av, 64 * (h - 1))
            for r in range(4):
                p.on("gpsimd", lambda g, h=h, sl=sl, r=r: g.dma_start(
                    out=khs[sl][:, r * TC:(r + 1) * TC],
                    in_=agk_o[r * DIM + h * 128: r * DIM + (h + 1) * 128, :]))
                p.inc("gpsimd", s_kh[sl], 16)
            kh_uses[sl] += 1
            kh_ms[h] = 64 * kh_uses[sl]
            for qtr in range(4):
                p.on("gpsimd", lambda g, h=h, sl=sl, qtr=qtr: g.dma_start(
                    out=vhs[sl][:].rearrange("p (c j) -> p c j", j=128)[:, qtr * 8:(qtr + 1) * 8, :],
                    in_=agv_o.ap()[:, h * 128:(h + 1) * 128].rearrange(
                        "(c p) j -> p c j", p=128)[:, qtr * 8:(qtr + 1) * 8, :]))
                p.inc("gpsimd", s_vh[sl], 16)
            vh_uses[sl] += 1
            vh_ms[h] = 64 * vh_uses[sl]

        p.wait("gpsimd", s_cc, cc_kv)
        load_head(0)
        load_head(1)
        for h in range(HEADS):
            sl = h % 2
            if h + 2 < HEADS:
                load_head(h + 2)
            p.wait("tensor", s_kh[sl], kh_ms[h])
            p.wait("tensor", s_vh[sl], vh_ms[h])
            if h == 0:
                p.wait("tensor", s_dv, negb_done)
            for qh in range(2):
                s_base = s_cnt[0]
                prev_exps = s_cnt[0]
                p.wait("tensor", s_e, prev_exps)
                p.wait("tensor", s_dv)
                qsl = slice(h * TC + qh * 512, h * TC + (qh + 1) * 512)
                for c in range(2):
                    p.on("tensor", lambda t, c=c, sl=sl, qsl=qsl: t.matmul(
                        (ps_s0 if c % 2 == 0 else ps_s1)[:, :],
                        khs[sl][:, c * 128:(c + 1) * 128],
                        qf[:, qsl], start=True, stop=True))
                    s_cnt[0] += 1
                    p.inc("tensor", s_s)
                for c in range(NKC):
                    p.wait("scalar", s_s, s_base + c + 1)
                    if c >= 2:
                        p.wait("scalar", s_av, av_cnt[0] - 1)
                    p.on("scalar", lambda s, c=c, h=h: s.activation(
                        es[c % 2][:], (ps_s0 if c % 2 == 0 else ps_s1)[:],
                        AF.Exp, bias=negb[:, h:h + 1], scale=SCALE))
                    p.inc("scalar", s_e)
                    p.wait("tensor", s_e, s_base + c + 1)
                    p.on("tensor", lambda t, c=c, sl=sl: t.matmul(
                        ps_o[:, :], vhs[sl][:, c * 128:(c + 1) * 128], es[c % 2][:],
                        start=(c == 0), stop=(c == NKC - 1)))
                    p.on("tensor", lambda t, c=c: t.matmul(
                        ps_g[:, :], ones_col[:], es[c % 2][:],
                        start=(c == 0), stop=(c == NKC - 1)))
                    av_cnt[0] += 1
                    p.inc("tensor", s_av)
                    if c + 2 < NKC:
                        p.on("tensor", lambda t, c=c, sl=sl, qsl=qsl: t.matmul(
                            (ps_s0 if c % 2 == 0 else ps_s1)[:, :],
                            khs[sl][:, (c + 2) * 128:(c + 3) * 128],
                            qf[:, qsl], start=True, stop=True))
                        s_cnt[0] += 1
                        p.inc("tensor", s_s)
                p.wait("vector", s_av, av_cnt[0])
                p.on("vector", lambda v: v.reciprocal(r_row[:], ps_g[0:1, :]))
                rc_ = p.inc("vector", s_dv)
                p.wait("tensor", s_dv, rc_)
                p.on("tensor", lambda t: t.matmul(ps_r[:, :], ones_rf[:], r_row[:],
                                                  start=True, stop=True))
                rmm = p.inc("tensor", s_mm)
                p.wait("vector", s_mm, rmm)
                p.on("vector", lambda v: v.tensor_copy(rt[:], ps_r[:, :]))
                rcp = p.inc("vector", s_dv)
                p.wait("vector", s_dv, rcp)
                p.on("vector", lambda v, qsl=qsl: v.tensor_mul(cat[:, qsl], ps_o[:], rt[:]))
                p.inc("vector", s_dv)

        att_s_total = s_cnt[0]
        att_av_total = av_cnt[0]
        att_dv_after = p.ctr[s_dv]

        p.free(cm_ps_r); p.free(cm_ps_g); p.free(cm_ps_o)
        p.free(cm_ps_s1); p.free(cm_ps_s0)
        p.free(cm_e1); p.free(cm_e0)
        p.free(cm_vh1); p.free(cm_vh0)
        p.free(cm_kh1); p.free(cm_kh0)
        p.free(cm_rt); p.free(cm_rrow); p.free(cm_nrow)
        p.free(cm_qf)
        if stop_phase <= 2:
            p.drain()
            p.free_all()
            p.emit(block)
            return nc
        pre1, cm_pre1 = p.sb_tmp("pre1", [128, DC * TC], F32)
        tmp32 = [None, None]
        tmp32[0], cm_t320 = p.sb_tmp("tmp32_0", [128, TC], F32)
        tmp32[1], cm_t321 = p.sb_tmp("tmp32_1", [128, TC], F32)
        prebf = [None, None]
        prebf[0], cm_pb0 = p.sb_tmp("prebf0", [128, TC], BF16)
        prebf[1], cm_pb1 = p.sb_tmp("prebf1", [128, TC], BF16)
        sqc = [None, None]
        sqc[0], cm_sc0 = p.sb_tmp("sqc0", [128, TC], BF16)
        sqc[1], cm_sc1 = p.sb_tmp("sqc1", [128, TC], BF16)
        xts = [None, None]
        xts[0], cm_xts0 = p.sb_tmp("xts0", [128, TC], F32)
        xts[1], cm_xts1 = p.sb_tmp("xts1", [128, TC], F32)
        wocs = [None, None]
        wocs[0], cm_woc0 = p.sb_tmp("woc0", [128, DC * 128], BF16)
        wocs[1], cm_woc1 = p.sb_tmp("woc1", [128, DC * 128], BF16)
        btlo, cm_btlo = p.sb_tmp("btlo", [128, DC * RANK], BF16)
        atlo, cm_atlo = p.sb_tmp("atlo", [RANK, DIM], BF16)
        botile, cm_boti = p.sb_tmp("botile", [128, DC], F32)
        xlo, cm_xlo = p.sb_tmp("xlo", [RANK, TC], BF16)
        mean_row, cm_mr = p.sb_tmp("mean_row", [1, TC], F32)
        q_row, cm_qr = p.sb_tmp("q_row", [1, TC], F32)
        t_row, cm_tr = p.sb_tmp("t_row", [1, TC], F32)
        rstd_row, cm_rs = p.sb_tmp("rstd_row", [1, TC], F32)
        lstage, cm_lst = p.sb_tmp("lstage", [128, 8 * E], F32)
        xtok = [None, None]
        xtok[0], cm_xk0 = p.sb_tmp("xtok0", [128, DIM], BF16)
        xtok[1], cm_xk1 = p.sb_tmp("xtok1", [128, DIM], BF16)
        psA2, cm_psA2 = p.ps_tmp("psA2", [128, 1024])
        psB2, cm_psB2 = p.ps_tmp("psB2", [128, 1024])
        psS, cm_psS = p.ps_tmp("psS", [1, 1024])
        psQ, cm_psQ = p.ps_tmp("psQ", [1, 1024])
        pss2 = [psA2, psB2]

        p.wait("gpsimd", s_s, att_s_total)
        p.wait("gpsimd", s_av, att_av_total)
        p.wait("gpsimd", s_e, att_s_total)
        p.wait("gpsimd", s_dv, att_dv_after)
        p.on("gpsimd", lambda g: g.dma_start(
            out=btlo[:].rearrange("p (dc r) -> p dc r", r=RANK),
            in_=W["o"]["BT"].rearrange("(dc p) r -> p dc r", p=128)))
        p.inc("gpsimd", s_c, 16)
        p.on("gpsimd", lambda g: g.dma_start(out=atlo[:], in_=W["o"]["AT"][:, :]))
        p.inc("gpsimd", s_c, 16)
        p.on("gpsimd", lambda g: g.dma_start(out=botile[:], in_=W["o"]["b"][:, :]))
        o_c = p.inc("gpsimd", s_c, 16)

        p.wait("tensor", s_dv, att_dv_after)
        p.wait("tensor", s_c, o_c)
        for half in range(2):
            for dc in range(DC):
                p.on("tensor", lambda t, dc=dc, half=half: t.matmul(
                    psA2[0:RANK, half * 512:(half + 1) * 512],
                    btlo[:, dc * RANK:(dc + 1) * RANK],
                    cat[:, dc * TC + half * 512: dc * TC + (half + 1) * 512],
                    start=(dc == 0), stop=(dc == DC - 1)))
        lomm = p.inc("tensor", s_mm)
        p.wait("scalar", s_mm, lomm)
        p.on("scalar", lambda s: s.activation(xlo[:], psA2[0:RANK, :], AF.Identity, scale=SCALING))
        ev_lo = p.inc("scalar", s_ev)

        xt_ms = {}
        wo_ms = {}
        xt_uses2 = [0, 0]
        wo_uses2 = [0, 0]
        dv_tt_ms = {}
        ov_ms = {}

        def load_oc2(oc):
            sl = oc % 2
            if oc >= 2:
                p.wait("gpsimd", s_dv, dv_tt_ms[oc - 2])
                p.wait("gpsimd", s_mm, ov_ms[oc - 2])
            p.on("gpsimd", lambda g, oc=oc, sl=sl: g.dma_start(
                out=xts[sl][:], in_=xT[oc * 128:(oc + 1) * 128, :]))
            xt_uses2[sl] += 1
            xt_ms[oc] = 16 * xt_uses2[sl]
            p.inc("gpsimd", s_xt[sl], 16)
            p.on("gpsimd", lambda g, oc=oc, sl=sl: g.dma_start(
                out=wocs[sl][:].rearrange("p (dc j) -> p dc j", j=128),
                in_=W["o"]["WT"][:, oc * 128:(oc + 1) * 128].rearrange("(dc p) j -> p dc j", p=128)))
            wo_uses2[sl] += 1
            wo_ms[oc] = 16 * wo_uses2[sl]
            p.inc("gpsimd", s_w[sl], 16)

        wbase = [p.ctr[s_w[0]], p.ctr[s_w[1]]]
        load_oc2(0)
        load_oc2(1)
        ev_ms2 = []
        for oc in range(DC):
            sl = oc % 2
            ps = pss2[oc % 2]
            p.wait("tensor", s_w[sl], wbase[sl] + wo_ms[oc])
            if oc >= 2:
                p.wait("tensor", s_ev, ev_ms2[oc - 2])
            if oc == 0:
                p.wait("tensor", s_ev, ev_lo)
            for half in range(2):
                for dc in range(DC):
                    p.on("tensor", lambda t, oc=oc, dc=dc, half=half, ps=ps, sl=sl: t.matmul(
                        ps[:, half * 512:(half + 1) * 512],
                        wocs[sl][:, dc * 128:(dc + 1) * 128],
                        cat[:, dc * TC + half * 512: dc * TC + (half + 1) * 512],
                        start=(dc == 0), stop=False))
                p.on("tensor", lambda t, oc=oc, half=half, ps=ps: t.matmul(
                    ps[:, half * 512:(half + 1) * 512],
                    atlo[:, oc * 128:(oc + 1) * 128],
                    xlo[:, half * 512:(half + 1) * 512],
                    start=False, stop=True))
            ov_ms[oc] = p.inc("tensor", s_mm)
            p.wait("scalar", s_mm, ov_ms[oc])
            if oc >= 2:
                p.wait("scalar", s_dv, dv_tt_ms[oc - 2])
            p.on("scalar", lambda s, oc=oc, ps=ps, sl=sl: s.activation(
                tmp32[sl][:], ps[:], AF.Identity, bias=botile[:, oc:oc + 1]))
            ev = p.inc("scalar", s_ev)
            ev_ms2.append(ev)
            p.wait("vector", s_ev, ev)
            p.wait("vector", s_xt[sl], wbase[0] * 0 + xt_ms[oc])
            p.on("vector", lambda v, oc=oc, sl=sl: v.tensor_add(
                pre1[:, oc * TC:(oc + 1) * TC], tmp32[sl][:], xts[sl][:]))
            dv_tt_ms[oc] = p.inc("vector", s_dv)
            p.wait("vector", s_dv, dv_tt_ms[oc])
            p.on("vector", lambda v, oc=oc, sl=sl: v.tensor_copy(
                prebf[sl][:], pre1[:, oc * TC:(oc + 1) * TC]))
            cpms = p.inc("vector", s_dv)
            p.wait("scalar", s_dv, cpms)
            p.on("scalar", lambda s, sl=sl: s.activation(sqc[sl][:], prebf[sl][:], AF.Square))
            sqev = p.inc("scalar", s_ev)
            p.wait("tensor", s_ev, sqev)
            for half in range(2):
                p.on("tensor", lambda t, oc=oc, half=half, sl=sl: t.matmul(
                    psS[0:1, half * 512:(half + 1) * 512], ones_col[:],
                    prebf[sl][:, half * 512:(half + 1) * 512],
                    start=(oc == 0), stop=(oc == DC - 1)))
                p.on("tensor", lambda t, oc=oc, half=half, sl=sl: t.matmul(
                    psQ[0:1, half * 512:(half + 1) * 512], ones_col[:],
                    sqc[sl][:, half * 512:(half + 1) * 512],
                    start=(oc == 0), stop=(oc == DC - 1)))
            wo_ms[("stat", oc)] = p.inc("tensor", s_mm)
            if oc + 2 < DC:
                load_oc2(oc + 2)

        last_stat = wo_ms[("stat", DC - 1)]
        p.wait("vector", s_mm, last_stat)
        p.on("vector", lambda v: v.tensor_scalar(mean_row[:], psS[0:1, :], 1.0 / DIM, None, ALU.mult))
        m1 = p.inc("vector", s_dv)
        p.on("vector", lambda v: v.tensor_scalar(q_row[:], psQ[0:1, :], 1.0 / DIM, None, ALU.mult))
        m2 = p.inc("vector", s_dv)
        p.wait("vector", s_dv, m2)
        p.on("vector", lambda v: v.tensor_mul(t_row[:], mean_row[:], mean_row[:]))
        m3 = p.inc("vector", s_dv)
        p.wait("vector", s_dv, m3)
        p.on("vector", lambda v: v.tensor_sub(q_row[:], q_row[:], t_row[:]))
        m4 = p.inc("vector", s_dv)
        p.wait("vector", s_dv, m4)
        p.on("vector", lambda v: v.tensor_scalar(q_row[:], q_row[:], EPS, None, ALU.add))
        m5 = p.inc("vector", s_dv)
        p.wait("scalar", s_dv, m5)
        p.on("scalar", lambda s: s.activation(t_row[:], q_row[:], AF.Ln))
        e1_ = p.inc("scalar", s_ev)
        p.wait("scalar", s_ev, e1_)
        p.on("scalar", lambda s: s.activation(rstd_row[:], t_row[:], AF.Exp, scale=-0.5))
        e2_ = p.inc("scalar", s_ev)
        p.wait("tensor", s_ev, e2_)
        p.wait("tensor", s_dv, m1)
        for half in range(2):
            p.on("tensor", lambda t, half=half: t.matmul(
                psA2[:, half * 512:(half + 1) * 512], ones_rf[:],
                mean_row[0:1, half * 512:(half + 1) * 512], start=True, stop=True))
            p.on("tensor", lambda t, half=half: t.matmul(
                psB2[:, half * 512:(half + 1) * 512], ones_rf[:],
                rstd_row[0:1, half * 512:(half + 1) * 512], start=True, stop=True))
        mr_ms = p.inc("tensor", s_mm)

        nrm_ms = {}
        for dc in range(DC):
            sl = dc % 2
            p.wait("vector", s_mm, mr_ms)
            p.on("vector", lambda v, dc=dc, sl=sl: v.tensor_sub(
                tmp32[sl][:], pre1[:, dc * TC:(dc + 1) * TC], psA2[:]))
            n1 = p.inc("vector", s_dv)
            p.wait("vector", s_dv, n1)
            p.on("vector", lambda v, dc=dc, sl=sl: v.tensor_mul(
                pre1[:, dc * TC:(dc + 1) * TC], tmp32[sl][:], psB2[:]))
            n2 = p.inc("vector", s_dv)
            p.wait("vector", s_dv, n2)
            p.on("vector", lambda v, dc=dc: v.tensor_copy(
                x2b[:, dc * TC:(dc + 1) * TC], pre1[:, dc * TC:(dc + 1) * TC]))
            n3 = p.inc("vector", s_dv)
            nrm_ms[dc] = n3
            p.wait("sync", s_dv, n3)
            p.on("sync", lambda sy, dc=dc: sy.dma_start(
                out=x2Td.ap()[dc * 128:(dc + 1) * 128, :], in_=pre1[:, dc * TC:(dc + 1) * TC]))
            p.inc("sync", s_x2st, 16)

        p.free(cm_psQ); p.free(cm_psS)
        psL, cm_psL = p.ps_tmp("psL", [128, E])
        p.wait("tensor", s_dv, nrm_ms[DC - 1])
        for tc8 in range(8):
            for dc in range(DC):
                p.on("tensor", lambda t, tc8=tc8, dc=dc: t.matmul(
                    psL[:, :],
                    x2b[:, dc * TC + tc8 * 128: dc * TC + (tc8 + 1) * 128],
                    wg_sb[:, dc * E:(dc + 1) * E],
                    start=(dc == 0), stop=(dc == DC - 1)))
            lg_ = p.inc("tensor", s_mm)
            p.wait("scalar", s_mm, lg_)
            p.on("scalar", lambda s, tc8=tc8: s.activation(
                lstage[:, tc8 * E:(tc8 + 1) * E], psL[:], AF.Identity))
            lev = p.inc("scalar", s_ev)
            p.wait("tensor", s_ev, lev)
        p.wait("sync", s_ev)
        p.on("sync", lambda sy: sy.dma_start(
            out=agl_i.ap().rearrange("(t8 p) e -> p t8 e", p=128),
            in_=lstage[:].rearrange("p (t8 e) -> p t8 e", e=E)))
        p.inc("sync", s_x2st, 16)

        ps_t0b, cm_pt0b = p.ps_tmp("ps_t0b", [128, 128], BF16)
        ps_t1b, cm_pt1b = p.ps_tmp("ps_t1b", [128, 128], BF16)
        pstb = [ps_t0b, ps_t1b]
        xts2_uses = [0, 0]
        for tc8 in range(8):
            sl = tc8 % 2
            if xts2_uses[sl] > 0:
                p.wait("vector", s_xts2[sl], 16 * xts2_uses[sl])
            for dc in range(DC):
                psl2 = pstb[dc % 2]
                p.on("tensor", lambda t, tc8=tc8, dc=dc, psl2=psl2: t.transpose(
                    psl2[:], x2b[:, dc * TC + tc8 * 128: dc * TC + (tc8 + 1) * 128],
                    ident[:]))
                trm = p.inc("tensor", s_mm)
                p.wait("vector", s_mm, trm)
                p.on("vector", lambda v, dc=dc, sl=sl, psl2=psl2: v.tensor_copy(
                    xtok[sl][:, dc * 128:(dc + 1) * 128], psl2[:]))
                cpm = p.inc("vector", s_dv)
                p.wait("tensor", s_dv, cpm)
            p.wait("sync", s_dv)
            p.on("sync", lambda sy, tc8=tc8, sl=sl: sy.dma_start(
                out=agx_i.ap()[tc8 * 128:(tc8 + 1) * 128, :], in_=xtok[sl][:]))
            xts2_uses[sl] += 1
            p.inc("sync", s_xts2[sl], 16)

        for slx in range(2):
            p.wait("gpsimd", s_xts2[slx], 16 * xts2_uses[slx])
        p.wait("gpsimd", s_x2st)
        p.on("gpsimd", lambda g: g.collective_compute(
            "AllGather", ALU.bypass, replica_groups=G18,
            ins=[agx_i.ap().opt()], outs=[agx_o.ap().opt()]))
        p.inc("gpsimd", s_cc)
        p.on("gpsimd", lambda g: g.collective_compute(
            "AllGather", ALU.bypass, replica_groups=G18,
            ins=[agl_i.ap().opt()], outs=[agl_o.ap().opt()]))
        cc_x = p.inc("gpsimd", s_cc)

        p.free(cm_pt1b); p.free(cm_pt0b); p.free(cm_psL)
        p.free(cm_psB2); p.free(cm_psA2)
        p.free(cm_xk1); p.free(cm_xk0)
        p.free(cm_lst)
        p.free(cm_rs); p.free(cm_tr); p.free(cm_qr); p.free(cm_mr)
        p.free(cm_xlo)
        p.free(cm_boti); p.free(cm_atlo); p.free(cm_btlo)
        p.free(cm_woc1); p.free(cm_woc0)
        p.free(cm_xts1); p.free(cm_xts0)
        p.free(cm_sc1); p.free(cm_sc0)
        p.free(cm_pb1); p.free(cm_pb0)
        p.free(cm_t321); p.free(cm_t320)
        p.free(cm_pre1)

        ph2_mm = p.ctr[s_mm]
        ph2_ev = p.ctr[s_ev]
        ph2_dv = p.ctr[s_dv]
        p.wait("gpsimd", s_mm, ph2_mm)
        p.wait("gpsimd", s_ev, ph2_ev)
        p.wait("gpsimd", s_dv, ph2_dv)
        p.wait("gpsimd", s_x2st)

        if stop_phase <= 3:
            p.drain()
            p.free_all()
            p.emit(block)
            return nc
        lg, cm_auto_lg = p.sb_tmp("lg", [128, 512], F32)
        mx, cm_auto_mx = p.sb_tmp("mx", [128, 64], F32)
        exf, cm_auto_exf = p.sb_tmp("exf", [128, 512], F32)
        esum, cm_auto_esum = p.sb_tmp("esum", [128, 64], F32)
        gate, cm_auto_gate = p.sb_tmp("gate", [128, 64], F32)
        ohge, cm_auto_ohge = p.sb_tmp("ohge", [128, 512], F32)
        iotaE, cm_auto_iotaE = p.sb_tmp("iotaE", [128, 512], F32)
        sel, cm_auto_sel = p.sb_tmp("sel", [128, 512], F32)
        eidx, cm_auto_eidx = p.sb_tmp("eidx", [128, 64], F32)
        oh, cm_auto_oh = p.sb_tmp("oh", [128, 512], F32)
        cnt, cm_auto_cnt = p.sb_tmp("cnt", [128, 8], F32)
        cntb, cm_auto_cntb = p.sb_tmp("cntb", [128, 8], BF16)
        offs, cm_auto_offs = p.sb_tmp("offs", [128, 8], F32)
        run, cm_auto_run = p.sb_tmp("run", [128, 8], F32)
        tmp8, cm_auto_tmp8 = p.sb_tmp("tmp8", [128, 8], F32)
        posv, cm_auto_posv = p.sb_tmp("posv", [128, 64], F32)
        keep, cm_auto_keep = p.sb_tmp("keep", [128, 64], F32)
        gk, cm_auto_gk = p.sb_tmp("gk", [128, 64], F32)
        f1, cm_auto_f1 = p.sb_tmp("f1", [128, 64], F32)
        flat_i, cm_auto_flat_i = p.sb_tmp("flat_i", [128, 64], I32)
        mk, cm_auto_mk = p.sb_tmp("mk", [128, 64], F32)
        t2, cm_auto_t2 = p.sb_tmp("t2", [128, 64], F32)
        flloc_i, cm_auto_flloc_i = p.sb_tmp("flloc_i", [128, 64], I32)
        tok80_, cm_auto_tok80 = p.sb_tmp("tok80", [128, 8], I32)
        tok81_, cm_auto_tok81 = p.sb_tmp("tok81", [128, 8], I32)
        tok8 = [tok80_, tok81_]
        tloc, cm_auto_tloc = p.sb_tmp("tloc", [128, 64], F32)
        own, cm_auto_own = p.sb_tmp("own", [128, 64], F32)
        dest_i, cm_auto_dest_i = p.sb_tmp("dest_i", [128, 64], I32)
        rep80_, cm_auto_rep80 = p.sb_tmp("rep80", [128, 8], F32)
        rep81_, cm_auto_rep81 = p.sb_tmp("rep81", [128, 8], F32)
        rep8 = [rep80_, rep81_]
        rep8i0_, cm_auto_rep8i0 = p.sb_tmp("rep8i0", [128, 8], I32)
        rep8i1_, cm_auto_rep8i1 = p.sb_tmp("rep8i1", [128, 8], I32)
        rep8i = [rep8i0_, rep8i1_]
        g_sb, cm_g_sb = p.sb_tmp("g_sb", [128, HC * TC], BF16)

        p.wait("gpsimd", s_cc, cc_x)
        p.on("gpsimd", lambda g: g.dma_start(
            out=lg[:].rearrange("p (a e) -> p a e", e=E),
            in_=agl_o.ap().rearrange("(p a) e -> p a e", p=128)))
        p.inc("gpsimd", s_c, 16)
        p.on("gpsimd", lambda g: g.iota(iotaE[:].rearrange("p (a e) -> p a e", e=E),
                                        pattern=[[0, 64], [1, 8]], base=0,
                                        channel_multiplier=0,
                                        allow_small_or_imprecise_dtypes=True))
        p.inc("gpsimd", s_gp)
        p.on("gpsimd", lambda g: g.iota(tloc[:], pattern=[[1, 64]], base=0,
                                        channel_multiplier=64,
                                        allow_small_or_imprecise_dtypes=True))
        p.inc("gpsimd", s_gp)

        def vstep(fn):
            p.on("vector", fn)
            ms = p.inc("vector", s_dv)
            p.wait("vector", s_dv, ms)
            return ms

        p.wait("vector", s_c)
        p.wait("vector", s_gp, 2)
        vstep(lambda v: v.tensor_reduce(
            mx[:], lg[:].rearrange("p (a e) -> p a e", e=E), axis=AX.X, op=ALU.max))
        for e_ in range(E):
            p.on("vector", lambda v, e_=e_: v.tensor_sub(
                exf[:].rearrange("p (a e) -> p a e", e=E)[:, :, e_],
                lg[:].rearrange("p (a e) -> p a e", e=E)[:, :, e_], mx[:]))
            p.inc("vector", s_dv)
        dsub = p.wait("vector", s_dv)
        p.wait("scalar", s_dv, dsub)
        p.on("scalar", lambda s: s.activation(exf[:], exf[:], AF.Exp))
        eexp = p.inc("scalar", s_ev)
        p.wait("vector", s_ev, eexp)
        vstep(lambda v: v.tensor_reduce(
            esum[:], exf[:].rearrange("p (a e) -> p a e", e=E), axis=AX.X, op=ALU.add))
        vstep(lambda v: v.reciprocal(gate[:], esum[:]))
        for e_ in range(E):
            p.on("vector", lambda v, e_=e_: v.tensor_tensor(
                ohge[:].rearrange("p (a e) -> p a e", e=E)[:, :, e_],
                lg[:].rearrange("p (a e) -> p a e", e=E)[:, :, e_], mx[:], ALU.is_ge))
            p.inc("vector", s_dv)
        p.wait("vector", s_dv)
        vstep(lambda v: v.tensor_scalar(sel[:], iotaE[:], -8.0, None, ALU.add))
        vstep(lambda v: v.tensor_mul(sel[:], sel[:], ohge[:]))
        vstep(lambda v: v.tensor_scalar(sel[:], sel[:], 8.0, None, ALU.add))
        vstep(lambda v: v.tensor_reduce(
            eidx[:], sel[:].rearrange("p (a e) -> p a e", e=E), axis=AX.X, op=ALU.min))
        for e_ in range(E):
            p.on("vector", lambda v, e_=e_: v.tensor_scalar(
                oh[:].rearrange("p (a e) -> p a e", e=E)[:, :, e_],
                eidx[:], float(e_), None, ALU.is_equal))
            p.inc("vector", s_dv)
        p.wait("vector", s_dv)
        vstep(lambda v: v.tensor_reduce(
            cnt[:], oh[:].rearrange("p (a e) -> p e a", e=E), axis=AX.X, op=ALU.add))
        vstep(lambda v: v.tensor_copy(cntb[:], cnt[:]))
        cnt_ms = p.ctr[s_dv]
        ps8, cm_ps8 = p.ps_tmp("ps8", [128, 8])
        psc, cm_psc = p.ps_tmp("psc", [128, 1])
        p.wait("tensor", s_dv, cnt_ms)
        p.on("tensor", lambda t: t.matmul(ps8[:], lt_sb[:], cntb[:], start=True, stop=True))
        p.inc("tensor", s_mm)
        p.on("tensor", lambda t: t.matmul(psc[:], ones_rf[:], myexp_sb[:], start=True, stop=True))
        offmm = p.inc("tensor", s_mm)
        p.wait("vector", s_mm, offmm)
        vstep(lambda v: v.tensor_copy(offs[:], ps8[:]))
        vstep(lambda v: v.tensor_copy(mycol[:], psc[:]))
        vstep(lambda v: v.tensor_scalar(basecol[:], mycol[:], float(CAP), None, ALU.mult))
        vstep(lambda v: v.tensor_copy(run[:], offs[:]))
        ohv = oh[:].rearrange("p (a e) -> p a e", e=E)
        for a in range(64):
            vstep(lambda v, a=a: v.tensor_mul(tmp8[:], run[:], ohv[:, a, :]))
            vstep(lambda v, a=a: v.tensor_reduce(
                posv[:, a:a + 1], tmp8[:], axis=AX.X, op=ALU.add))
            vstep(lambda v, a=a: v.tensor_add(run[:], run[:], ohv[:, a, :]))
        vstep(lambda v: v.tensor_scalar(keep[:], posv[:], float(CAP), None, ALU.is_lt))
        vstep(lambda v: v.tensor_mul(gk[:], gate[:], keep[:]))
        vstep(lambda v: v.tensor_scalar(f1[:], eidx[:], float(CAP), None, ALU.mult))
        vstep(lambda v: v.tensor_add(f1[:], f1[:], posv[:]))
        vstep(lambda v: v.tensor_scalar(f1[:], f1[:], float(T - 1), None, ALU.min))
        vstep(lambda v: v.tensor_copy(flat_i[:], f1[:]))
        vstep(lambda v: v.tensor_scalar(mk[:], eidx[:], mycol[:, 0:1], None, ALU.is_equal))
        vstep(lambda v: v.tensor_mul(mk[:], mk[:], keep[:]))
        vstep(lambda v: v.tensor_scalar(t2[:], posv[:], -float(CAP), None, ALU.add))
        vstep(lambda v: v.tensor_mul(t2[:], t2[:], mk[:]))
        vstep(lambda v: v.tensor_scalar(t2[:], t2[:], float(CAP), None, ALU.add))
        vstep(lambda v: v.tensor_copy(flloc_i[:], t2[:]))
        vstep(lambda v: v.tensor_scalar(tloc[:], tloc[:], basecol[:, 0:1], None, ALU.subtract))
        vstep(lambda v: v.tensor_scalar(own[:], tloc[:], -0.5, None, ALU.is_gt))
        vstep(lambda v: v.tensor_scalar(t2[:], tloc[:], float(TC) - 0.5, None, ALU.is_lt))
        vstep(lambda v: v.tensor_mul(own[:], own[:], t2[:]))
        vstep(lambda v: v.tensor_scalar(t2[:], tloc[:], -float(TC), None, ALU.add))
        vstep(lambda v: v.tensor_mul(t2[:], t2[:], own[:]))
        vstep(lambda v: v.tensor_scalar(t2[:], t2[:], float(TC), None, ALU.add))
        route_dv = vstep(lambda v: v.tensor_copy(dest_i[:], t2[:]))

        zero72, cm_z72 = p.sb_tmp("zero72", [128, 72], I32)
        p.on("gpsimd", lambda g: g.memset(zero72[:], 0))
        zm = p.inc("gpsimd", s_gp)
        p.wait("gpsimd", s_gp, zm)
        p.on("gpsimd", lambda g: g.dma_start(
            out=st8.ap().rearrange("(p a) e -> p a e", p=128),
            in_=zero72[:].rearrange("p (a e) -> p a e", e=E)))
        p.inc("gpsimd", s_z, 16)
        p.wait("gpsimd", s_z, 16)
        p.wait("gpsimd", s_dv, route_dv)
        tok_uses = [0, 0]
        for a in range(64):
            sl = a % 2
            if tok_uses[sl] > 0:
                p.wait("gpsimd", s_sc[sl], 16 * tok_uses[sl])
            p.on("gpsimd", lambda g, a=a, sl=sl: g.iota(
                tok8[sl][:], pattern=[[0, 8]], base=a, channel_multiplier=64))
            p.inc("gpsimd", s_gp)
            p.on("gpsimd", lambda g, a=a, sl=sl: g.indirect_dma_start(
                out=st8.ap(), out_offset=IndirectOffsetOnAxis(flloc_i[:, a:a + 1], 0),
                in_=tok8[sl][:], in_offset=None))
            p.inc("gpsimd", s_sc[sl], 16)
            tok_uses[sl] += 1
        for sl in range(2):
            p.wait("gpsimd", s_sc[sl], 16 * tok_uses[sl])

        def emit_fg_scatters():
            rep_uses = [0, 0]
            for a in range(64):
                sl = a % 2
                if rep_uses[sl] > 0:
                    p.wait("vector", s_sc2[sl], 32 * rep_uses[sl])
                p.on("vector", lambda v, a=a, sl=sl: v.tensor_scalar(
                    rep8[sl][:], ones8[:], gk[:, a:a + 1], None, ALU.mult))
                r1 = p.inc("vector", s_dv)
                p.wait("vector", s_dv, r1)
                p.on("vector", lambda v, a=a, sl=sl: v.tensor_scalar(
                    rep8i[sl][:], ones8[:], f1[:, a:a + 1], None, ALU.mult))
                r2 = p.inc("vector", s_dv)
                p.wait("gpsimd", s_dv, r2)
                p.on("gpsimd", lambda g, a=a, sl=sl: g.indirect_dma_start(
                    out=myg8.ap(), out_offset=IndirectOffsetOnAxis(dest_i[:, a:a + 1], 0),
                    in_=rep8[sl][:], in_offset=None))
                p.inc("gpsimd", s_sc2[sl], 16)
                p.on("gpsimd", lambda g, a=a, sl=sl: g.indirect_dma_start(
                    out=myf8.ap(), out_offset=IndirectOffsetOnAxis(dest_i[:, a:a + 1], 0),
                    in_=rep8i[sl][:], in_offset=None))
                p.inc("gpsimd", s_sc2[sl], 16)
                rep_uses[sl] += 1
            return [32 * rep_uses[0], 32 * rep_uses[1]]

        if stop_phase <= 4:
            p.drain()
            p.free_all()
            p.emit(block)
            return nc
        idxt0, cm_idxt0 = p.sb_tmp("idxt0", [128, E], I32)
        idxt1, cm_idxt1 = p.sb_tmp("idxt1", [128, E], I32)
        idxt = [idxt0, idxt1]
        xdr0, cm_xdr0 = p.sb_tmp("xdrow0", [128, DIM], BF16)
        xdr1, cm_xdr1 = p.sb_tmp("xdrow1", [128, DIM], BF16)
        xdrow = [xdr0, xdr1]
        w1s0_, cm_w1s0 = p.sb_tmp("w1s0", [128, DC * 128], BF16)
        w1s1_, cm_w1s1 = p.sb_tmp("w1s1", [128, DC * 128], BF16)
        w1s = [w1s0_, w1s1_]
        w3s0_, cm_w3s0 = p.sb_tmp("w3s0", [128, DC * 128], BF16)
        w3s1_, cm_w3s1 = p.sb_tmp("w3s1", [128, DC * 128], BF16)
        w3s = [w3s0_, w3s1_]
        slbf0_, cm_slbf0 = p.sb_tmp("slbf0", [128, TC], BF16)
        slbf1_, cm_slbf1 = p.sb_tmp("slbf1", [128, TC], BF16)
        slbf = [slbf0_, slbf1_]
        xdT, cm_xdT = p.sb_tmp("xdT", [128, DC * TC], BF16)
        p.free(cm_psc); p.free(cm_ps8)
        ps_t0c, cm_pt0c = p.ps_tmp("ps_t0c", [128, 128], BF16)
        ps_t1c, cm_pt1c = p.ps_tmp("ps_t1c", [128, 128], BF16)
        pstc = [ps_t0c, ps_t1c]
        ix_uses = [0, 0]
        gx_uses = [0, 0]
        for i in range(8):
            sl = i % 2
            if ix_uses[sl] > 0:
                p.wait("gpsimd", s_gx[sl], 16 * gx_uses[sl])
            p.on("gpsimd", lambda g, i=i, sl=sl: g.dma_start(
                out=idxt[sl][:], in_=st8[i * 128:(i + 1) * 128, :]))
            ix_uses[sl] += 1
            p.inc("gpsimd", s_ix[sl], 16)
            p.wait("gpsimd", s_ix[sl], 16 * ix_uses[sl])
            if i >= 2:
                p.wait("gpsimd", s_mm)
            p.on("gpsimd", lambda g, sl=sl: g.indirect_dma_start(
                out=xdrow[sl][:], out_offset=None,
                in_=agx_o.ap(), in_offset=IndirectOffsetOnAxis(idxt[sl][:, 0:1], 0)))
            gx_uses[sl] += 1
            p.inc("gpsimd", s_gx[sl], 16)
            p.wait("tensor", s_gx[sl], 16 * gx_uses[sl])
            for dc in range(DC):
                psl2 = pstc[dc % 2]
                p.on("tensor", lambda t, i=i, dc=dc, sl=sl, psl2=psl2: t.transpose(
                    psl2[:], xdrow[sl][:, dc * 128:(dc + 1) * 128], ident[:]))
                trm = p.inc("tensor", s_mm)
                p.wait("vector", s_mm, trm)
                p.on("vector", lambda v, i=i, dc=dc, psl2=psl2: v.tensor_copy(
                    xdT[:, dc * TC + i * 128: dc * TC + (i + 1) * 128], psl2[:]))
                cpm = p.inc("vector", s_dv)
                p.wait("tensor", s_dv, cpm)

        w1_uses = [0, 0]

        def h_phase(W1src, W3src, xsrc, xready_sem, xready_val, ph1s, ph3s,
                    ev_gate=None):
            gdv = {}
            for hc in range(HC):
                sl = hc % 2
                if w1_uses[sl] > 0:
                    p.wait("gpsimd", s_mm)
                p.on("gpsimd", lambda g, hc=hc, sl=sl, W1src=W1src: g.dma_start(
                    out=w1s[sl][:].rearrange("p (dc j) -> p dc j", j=128),
                    in_=W1src[:, hc * 128:(hc + 1) * 128].rearrange("(dc p) j -> p dc j", p=128)))
                p.inc("gpsimd", s_w1[sl], 16)
                p.on("gpsimd", lambda g, hc=hc, sl=sl, W3src=W3src: g.dma_start(
                    out=w3s[sl][:].rearrange("p (dc j) -> p dc j", j=128),
                    in_=W3src[:, hc * 128:(hc + 1) * 128].rearrange("(dc p) j -> p dc j", p=128)))
                p.inc("gpsimd", s_w1[sl], 16)
                w1_uses[sl] += 1
                p.wait("tensor", s_w1[sl], 32 * w1_uses[sl])
                if hc == 0:
                    p.wait("tensor", xready_sem, xready_val)
                ph1, ph3 = ph1s[sl], ph3s[sl]
                if hc == 0 and ev_gate is not None:
                    p.wait("tensor", s_ev, ev_gate)
                if hc >= 2:
                    p.wait("tensor", s_ev)
                    p.wait("tensor", s_dv)
                for half in range(2):
                    for dc in range(DC):
                        p.on("tensor", lambda t, dc=dc, half=half, sl=sl, ph1=ph1, xsrc=xsrc: t.matmul(
                            ph1[:, half * 512:(half + 1) * 512],
                            w1s[sl][:, dc * 128:(dc + 1) * 128],
                            xsrc[:, dc * TC + half * 512: dc * TC + (half + 1) * 512],
                            start=(dc == 0), stop=(dc == DC - 1)))
                    for dc in range(DC):
                        p.on("tensor", lambda t, dc=dc, half=half, sl=sl, ph3=ph3, xsrc=xsrc: t.matmul(
                            ph3[:, half * 512:(half + 1) * 512],
                            w3s[sl][:, dc * 128:(dc + 1) * 128],
                            xsrc[:, dc * TC + half * 512: dc * TC + (half + 1) * 512],
                            start=(dc == 0), stop=(dc == DC - 1)))
                hm = p.inc("tensor", s_mm)
                p.wait("scalar", s_mm, hm)
                if hc >= 2:
                    p.wait("scalar", s_dv)
                p.on("scalar", lambda s, sl=sl, ph1=ph1: s.activation(
                    slbf[sl][:], ph1[:], AF.Silu))
                sev = p.inc("scalar", s_ev)
                p.wait("vector", s_ev, sev)
                p.on("vector", lambda v, hc=hc, sl=sl, ph3=ph3: v.tensor_mul(
                    g_sb[:, hc * TC:(hc + 1) * TC], slbf[sl][:], ph3[:]))
                gdv[hc] = p.inc("vector", s_dv)
            return gdv

        p.free(cm_pt1c); p.free(cm_pt0c)
        ph1a, cm_ph1a = p.ps_tmp("ph1a", [128, 1024])
        ph1b, cm_ph1b = p.ps_tmp("ph1b", [128, 1024])
        ph3a, cm_ph3a = p.ps_tmp("ph3a", [128, 1024])
        ph3b, cm_ph3b = p.ps_tmp("ph3b", [128, 1024])
        xd_ready = p.ctr[s_dv]
        gdvE = h_phase(W1c, W3c, xdT, s_dv, xd_ready, [ph1a, ph1b], [ph3a, ph3b])
        glastE = gdvE[HC - 1]
        fg_totals = emit_fg_scatters()

        p.free(cm_ph3b); p.free(cm_ph3a); p.free(cm_ph1b); p.free(cm_ph1a)
        p.free(cm_xdT)
        w2c0_, cm_w2c0 = p.sb_tmp("w2col0", [128, HC * 512], BF16)
        w2c1_, cm_w2c1 = p.sb_tmp("w2col1", [128, HC * 512], BF16)
        w2col = [w2c0_, w2c1_]
        oes0_, cm_oes0 = p.sb_tmp("oest0", [128, 512], BF16)
        oes1_, cm_oes1 = p.sb_tmp("oest1", [128, 512], BF16)
        oest = [oes0_, oes1_]
        psO0, cm_psO0 = p.ps_tmp("psO0", [128, 512])
        psO1, cm_psO1 = p.ps_tmp("psO1", [128, 512])
        psOs = [psO0, psO1]
        w2_uses = [0, 0]
        so_uses = [0, 0]
        ev3 = []
        gidx = [0]
        for on4 in range(4):
            sl = on4 % 2
            if w2_uses[sl] > 0 or on4 == 0:
                p.wait("gpsimd", s_mm)
            p.on("gpsimd", lambda g, on4=on4, sl=sl: g.dma_start(
                out=w2col[sl][:].rearrange("p (hc j) -> p hc j", j=512),
                in_=W2c[:, on4 * 512:(on4 + 1) * 512].rearrange("(hc p) j -> p hc j", p=128)))
            w2_uses[sl] += 1
            p.inc("gpsimd", s_w2[sl], 16)
            p.wait("tensor", s_w2[sl], 16 * w2_uses[sl])
            if on4 == 0:
                p.wait("tensor", s_dv, glastE)
            for tc8 in range(8):
                gi = gidx[0]
                gidx[0] += 1
                ps = psOs[gi % 2]
                stsl = gi % 2
                if gi >= 2:
                    p.wait("tensor", s_ev, ev3[-2])
                for hc in range(HC):
                    p.on("tensor", lambda t, hc=hc, tc8=tc8, sl=sl, ps=ps: t.matmul(
                        ps[:],
                        g_sb[:, hc * TC + tc8 * 128: hc * TC + (tc8 + 1) * 128],
                        w2col[sl][:, hc * 512:(hc + 1) * 512],
                        start=(hc == 0), stop=(hc == HC - 1)))
                om = p.inc("tensor", s_mm)
                p.wait("scalar", s_mm, om)
                if so_uses[stsl] > 0:
                    p.wait("scalar", s_so[stsl], 16 * so_uses[stsl])
                p.on("scalar", lambda s, ps=ps, stsl=stsl: s.activation(
                    oest[stsl][:], ps[:], AF.Identity))
                ev = p.inc("scalar", s_ev)
                ev3.append(ev)
                p.wait("sync", s_ev, ev)
                p.on("sync", lambda sy, tc8=tc8, on4=on4, stsl=stsl: sy.dma_start(
                    out=ago_i.ap()[tc8 * 128:(tc8 + 1) * 128, on4 * 512:(on4 + 1) * 512],
                    in_=oest[stsl][:]))
                so_uses[stsl] += 1
                p.inc("sync", s_so[stsl], 16)

        for sl in range(2):
            p.wait("gpsimd", s_so[sl], 16 * so_uses[sl])
        p.on("gpsimd", lambda g: g.collective_compute(
            "AllGather", ALU.bypass, replica_groups=G18,
            ins=[ago_i.ap().opt()], outs=[ago_o.ap().opt()]))
        cc_oe = p.inc("gpsimd", s_cc)

        p.free(cm_psO1); p.free(cm_psO0)
        if stop_phase <= 5:
            p.drain()
            p.free_all()
            p.emit(block)
            return nc
        ph1a2, cm_ph1a2 = p.ps_tmp("ph1a2", [128, 1024])
        ph1b2, cm_ph1b2 = p.ps_tmp("ph1b2", [128, 1024])
        ph3a2, cm_ph3a2 = p.ps_tmp("ph3a2", [128, 1024])
        ph3b2, cm_ph3b2 = p.ps_tmp("ph3b2", [128, 1024])
        gdvS = h_phase(SW1, SW3, x2b, s_dv, 0,
                       [ph1a2, ph1b2], [ph3a2, ph3b2], ev_gate=ev3[-1])
        glastS = gdvS[HC - 1]

        p.free(cm_oes1); p.free(cm_oes0)
        p.free(cm_w2c1); p.free(cm_w2c0)
        p.wait("gpsimd", s_mm)
        p.wait("gpsimd", s_ev)
        sw0_, cm_sw0 = p.sb_tmp("sw2oc0", [128, HC * 128], BF16)
        sw1_, cm_sw1t = p.sb_tmp("sw2oc1", [128, HC * 128], BF16)
        sw2oc = [sw0_, sw1_]
        x2f0_, cm_x2f0 = p.sb_tmp("x2f0", [128, TC], F32)
        x2f1_, cm_x2f1 = p.sb_tmp("x2f1", [128, TC], F32)
        x2f = [x2f0_, x2f1_]
        p2s0_, cm_p2s0 = p.sb_tmp("p2st0", [128, TC], F32)
        p2s1_, cm_p2s1 = p.sb_tmp("p2st1", [128, TC], F32)
        p2st = [p2s0_, p2s1_]
        p.free(cm_ph3b2); p.free(cm_ph3a2); p.free(cm_ph1b2); p.free(cm_ph1a2)
        psP0, cm_psP0 = p.ps_tmp("psP0", [128, 1024])
        psP1, cm_psP1 = p.ps_tmp("psP1", [128, 1024])
        psPs = [psP0, psP1]
        sw2_uses = [0, 0]
        xf_uses = [0, 0]
        sp_uses = [0, 0]
        dvs_ms = []
        for oc in range(DC):
            sl = oc % 2
            if sw2_uses[sl] > 0:
                p.wait("gpsimd", s_mm)
            p.on("gpsimd", lambda g, oc=oc, sl=sl: g.dma_start(
                out=sw2oc[sl][:].rearrange("p (hc j) -> p hc j", j=128),
                in_=SW2[:, oc * 128:(oc + 1) * 128].rearrange("(hc p) j -> p hc j", p=128)))
            sw2_uses[sl] += 1
            p.inc("gpsimd", s_w2[sl], 16)
            if xf_uses[sl] > 0:
                p.wait("gpsimd", s_dv, dvs_ms[oc - 2])
            p.on("gpsimd", lambda g, oc=oc, sl=sl: g.dma_start(
                out=x2f[sl][:], in_=x2Td.ap()[oc * 128:(oc + 1) * 128, :]))
            xf_uses[sl] += 1
            p.inc("gpsimd", s_xf[sl], 16)
            ps = psPs[oc % 2]
            p.wait("tensor", s_w2[sl], 16 * (w2_uses[sl] + sw2_uses[sl]))
            if oc == 0:
                p.wait("tensor", s_dv, glastS)
            if oc >= 2:
                p.wait("tensor", s_dv, dvs_ms[oc - 2])
            for half in range(2):
                for hc in range(HC):
                    p.on("tensor", lambda t, oc=oc, hc=hc, half=half, sl=sl, ps=ps: t.matmul(
                        ps[:, half * 512:(half + 1) * 512],
                        sw2oc[sl][:, hc * 128:(hc + 1) * 128],
                        g_sb[:, hc * TC + half * 512: hc * TC + (half + 1) * 512],
                        start=(hc == 0), stop=(hc == HC - 1)))
            pm = p.inc("tensor", s_mm)
            p.wait("vector", s_mm, pm)
            p.wait("vector", s_xf[sl], 16 * xf_uses[sl])
            if sp_uses[sl] > 0:
                p.wait("vector", s_sp[sl], 16 * sp_uses[sl])
            p.on("vector", lambda v, sl=sl, ps=ps: v.tensor_add(p2st[sl][:], ps[:], x2f[sl][:]))
            dv_ = p.inc("vector", s_dv)
            dvs_ms.append(dv_)
            p.wait("sync", s_dv, dv_)
            p.on("sync", lambda sy, oc=oc, sl=sl: sy.dma_start(
                out=p2Td.ap()[oc * 128:(oc + 1) * 128, :], in_=p2st[sl][:]))
            sp_uses[sl] += 1
            p.inc("sync", s_sp[sl], 16)

        p.free(cm_psP1); p.free(cm_psP0)
        if stop_phase <= 6:
            p.drain()
            p.free_all()
            p.emit(block)
            return nc
        ph3_mm = p.ctr[s_mm]
        ph3_ev = p.ctr[s_ev]
        ph3_dv = p.ctr[s_dv]
        for cm in (cm_p2s1, cm_p2s0, cm_x2f1, cm_x2f0, cm_sw1t, cm_sw0,
                   cm_slbf1, cm_slbf0, cm_w3s1, cm_w3s0, cm_w1s1, cm_w1s0,
                   cm_xdr1, cm_xdr0, cm_idxt1, cm_idxt0, cm_z72, cm_g_sb):
            p.free(cm)
        p.wait("gpsimd", s_mm, ph3_mm)
        p.wait("gpsimd", s_ev, ph3_ev)
        p.wait("gpsimd", s_dv, ph3_dv)
        for sl in range(2):
            p.wait("gpsimd", s_sp[sl], 16 * sp_uses[sl])
            p.wait("gpsimd", s_so[sl], 16 * so_uses[sl])
        idxf0_, _c1 = p.sb_tmp("idxf0", [128, E], I32)
        idxf1_, _c2 = p.sb_tmp("idxf1", [128, E], I32)
        idxf = [idxf0_, idxf1_]
        gkt0_, _c3 = p.sb_tmp("gkt0", [128, E], F32)
        gkt1_, _c4 = p.sb_tmp("gkt1", [128, E], F32)
        gkt = [gkt0_, gkt1_]
        oeg0_, _c5 = p.sb_tmp("oeg0", [128, DIM], BF16)
        oeg1_, _c6 = p.sb_tmp("oeg1", [128, DIM], BF16)
        oeg = [oeg0_, oeg1_]
        oegs0_, _c7 = p.sb_tmp("oegs0", [128, DIM], BF16)
        oegs1_, _c8 = p.sb_tmp("oegs1", [128, DIM], BF16)
        oegs = [oegs0_, oegs1_]
        pf20_, _c9 = p.sb_tmp("pf20", [128, TC], BF16)
        pf21_, _c10 = p.sb_tmp("pf21", [128, TC], BF16)
        pf2 = [pf20_, pf21_]
        sq20_, _c11 = p.sb_tmp("sq20", [128, TC], BF16)
        sq21_, _c12 = p.sb_tmp("sq21", [128, TC], BF16)
        sq2 = [sq20_, sq21_]
        fstg0_, _c13 = p.sb_tmp("fstg0", [128, TC], F32)
        fstg1_, _c14 = p.sb_tmp("fstg1", [128, TC], F32)
        fstg = [fstg0_, fstg1_]
        m2_row, _c15 = p.sb_tmp("m2_row", [1, TC], F32)
        q2_row, _c16 = p.sb_tmp("q2_row", [1, TC], F32)
        t2_row, _c17 = p.sb_tmp("t2_row", [1, TC], F32)
        r2_row, _c18 = p.sb_tmp("r2_row", [1, TC], F32)
        pre2F, cm_pre2F = p.sb_tmp("pre2F", [128, DC * TC], F32)
        ps_t0d, cm_pt0d = p.ps_tmp("ps_t0d", [128, 128], BF16)
        ps_t1d, cm_pt1d = p.ps_tmp("ps_t1d", [128, 128], BF16)
        pstd = [ps_t0d, ps_t1d]
        psS2, cm_psS2 = p.ps_tmp("psS2", [1, 1024])
        psQ2, cm_psQ2 = p.ps_tmp("psQ2", [1, 1024])

        for sl in range(2):
            p.wait("gpsimd", s_sp[sl], 16 * sp_uses[sl])
        for dc in range(DC):
            p.on("gpsimd", lambda g, dc=dc: g.dma_start(
                out=pre2F[:, dc * TC:(dc + 1) * TC], in_=p2Td.ap()[dc * 128:(dc + 1) * 128, :]))
            p.inc("gpsimd", s_p2l, 16)
        p2l_done = p.ctr[s_p2l]

        if_uses = [0, 0]
        og_uses = [0, 0]
        p.wait("gpsimd", s_cc, cc_oe)
        for sl in range(2):
            p.wait("gpsimd", s_sc2[sl], fg_totals[sl])
        add_ms = {}
        for j in range(8):
            sl = j % 2
            if if_uses[sl] > 0:
                p.wait("gpsimd", s_og[sl], 16 * og_uses[sl])
            p.on("gpsimd", lambda g, j=j, sl=sl: g.dma_start(
                out=idxf[sl][:], in_=myf8[j * 128:(j + 1) * 128, :]))
            if_uses[sl] += 1
            p.inc("gpsimd", s_if[sl], 16)
            p.on("gpsimd", lambda g, j=j, sl=sl: g.dma_start(
                out=gkt[sl][:], in_=myg8[j * 128:(j + 1) * 128, :]))
            p.inc("gpsimd", s_if[sl], 16)
            p.wait("gpsimd", s_if[sl], 32 * if_uses[sl])
            if j >= 2:
                p.wait("gpsimd", s_dv, add_ms[j - 2])
            p.on("gpsimd", lambda g, sl=sl: g.indirect_dma_start(
                out=oeg[sl][:], out_offset=None,
                in_=ago_o.ap(), in_offset=IndirectOffsetOnAxis(idxf[sl][:, 0:1], 0)))
            og_uses[sl] += 1
            p.inc("gpsimd", s_og[sl], 16)
            p.wait("vector", s_og[sl], 16 * og_uses[sl])
            p.on("vector", lambda v, sl=sl: v.tensor_scalar(
                oegs[sl][:], oeg[sl][:], gkt[sl][:, 0:1], None, ALU.mult))
            sc_ms = p.inc("vector", s_dv)
            if j < 2:
                p.wait("tensor", s_p2l, p2l_done)
            for dc in range(DC):
                psl2 = pstd[dc % 2]
                p.wait("tensor", s_dv, sc_ms)
                p.on("tensor", lambda t, dc=dc, sl=sl, psl2=psl2: t.transpose(
                    psl2[:], oegs[sl][:, dc * 128:(dc + 1) * 128], ident[:]))
                trm = p.inc("tensor", s_mm)
                p.wait("vector", s_mm, trm)
                p.on("vector", lambda v, j=j, dc=dc, psl2=psl2: v.tensor_add(
                    pre2F[:, dc * TC + j * 128: dc * TC + (j + 1) * 128],
                    pre2F[:, dc * TC + j * 128: dc * TC + (j + 1) * 128], psl2[:]))
                cpm = p.inc("vector", s_dv)
                p.wait("tensor", s_dv, cpm)
            add_ms[j] = p.ctr[s_dv]

        st2_ms = None
        for dc in range(DC):
            sl = dc % 2
            p.wait("vector", s_dv, add_ms[7])
            if dc >= 2:
                p.wait("vector", s_mm)
            p.on("vector", lambda v, dc=dc, sl=sl: v.tensor_copy(
                pf2[sl][:], pre2F[:, dc * TC:(dc + 1) * TC]))
            c1 = p.inc("vector", s_dv)
            p.wait("scalar", s_dv, c1)
            p.on("scalar", lambda s, sl=sl: s.activation(sq2[sl][:], pf2[sl][:], AF.Square))
            sev = p.inc("scalar", s_ev)
            p.wait("tensor", s_ev, sev)
            for half in range(2):
                p.on("tensor", lambda t, dc=dc, half=half, sl=sl: t.matmul(
                    psS2[0:1, half * 512:(half + 1) * 512], ones_col[:],
                    pf2[sl][:, half * 512:(half + 1) * 512],
                    start=(dc == 0), stop=(dc == DC - 1)))
                p.on("tensor", lambda t, dc=dc, half=half, sl=sl: t.matmul(
                    psQ2[0:1, half * 512:(half + 1) * 512], ones_col[:],
                    sq2[sl][:, half * 512:(half + 1) * 512],
                    start=(dc == 0), stop=(dc == DC - 1)))
            st2_ms = p.inc("tensor", s_mm)

        p.wait("vector", s_mm, st2_ms)
        p.on("vector", lambda v: v.tensor_scalar(m2_row[:], psS2[0:1, :], 1.0 / DIM, None, ALU.mult))
        mm1 = p.inc("vector", s_dv)
        p.on("vector", lambda v: v.tensor_scalar(q2_row[:], psQ2[0:1, :], 1.0 / DIM, None, ALU.mult))
        mm2 = p.inc("vector", s_dv)
        p.wait("vector", s_dv, mm2)
        p.on("vector", lambda v: v.tensor_mul(t2_row[:], m2_row[:], m2_row[:]))
        mm3 = p.inc("vector", s_dv)
        p.wait("vector", s_dv, mm3)
        p.on("vector", lambda v: v.tensor_sub(q2_row[:], q2_row[:], t2_row[:]))
        mm4 = p.inc("vector", s_dv)
        p.wait("vector", s_dv, mm4)
        p.on("vector", lambda v: v.tensor_scalar(q2_row[:], q2_row[:], EPS, None, ALU.add))
        mm5 = p.inc("vector", s_dv)
        p.wait("scalar", s_dv, mm5)
        p.on("scalar", lambda s: s.activation(t2_row[:], q2_row[:], AF.Ln))
        le1 = p.inc("scalar", s_ev)
        p.wait("scalar", s_ev, le1)
        p.on("scalar", lambda s: s.activation(r2_row[:], t2_row[:], AF.Exp, scale=-0.5))
        le2 = p.inc("scalar", s_ev)
        p.free(cm_psQ2); p.free(cm_psS2); p.free(cm_pt1d); p.free(cm_pt0d)
        psM2, cm_psM2 = p.ps_tmp("psM2", [128, 1024])
        psR2, cm_psR2 = p.ps_tmp("psR2", [128, 1024])
        p.wait("tensor", s_ev, le2)
        p.wait("tensor", s_dv, mm1)
        p.wait("tensor", s_dv, mm2)
        for half in range(2):
            p.on("tensor", lambda t, half=half: t.matmul(
                psM2[:, half * 512:(half + 1) * 512], ones_rf[:],
                m2_row[0:1, half * 512:(half + 1) * 512], start=True, stop=True))
            p.on("tensor", lambda t, half=half: t.matmul(
                psR2[:, half * 512:(half + 1) * 512], ones_rf[:],
                r2_row[0:1, half * 512:(half + 1) * 512], start=True, stop=True))
        mr2 = p.inc("tensor", s_mm)

        fst_uses = [0, 0]
        for dc in range(DC):
            sl = dc % 2
            p.wait("vector", s_mm, mr2)
            if fst_uses[sl] > 0:
                p.wait("vector", s_fst[sl], 16 * fst_uses[sl])
            p.on("vector", lambda v, dc=dc, sl=sl: v.tensor_sub(
                fstg[sl][:], pre2F[:, dc * TC:(dc + 1) * TC], psM2[:]))
            f1_ = p.inc("vector", s_dv)
            p.wait("vector", s_dv, f1_)
            p.on("vector", lambda v, dc=dc, sl=sl: v.tensor_mul(
                fstg[sl][:], fstg[sl][:], psR2[:]))
            f2_ = p.inc("vector", s_dv)
            p.wait("sync", s_dv, f2_)
            p.on("sync", lambda sy, dc=dc, sl=sl: sy.dma_start(
                out=OUT_T[dc * 128:(dc + 1) * 128, :], in_=fstg[sl][:]))
            fst_uses[sl] += 1
            p.inc("sync", s_fst[sl], 16)

        for sl in range(2):
            p.wait("sync", s_fst[sl], 16 * fst_uses[sl])
        p.free_all()
        p.emit(block)
    return nc


def _host_fused_inputs(inputs, core):
    x = np.asarray(inputs["L0_attn_x"], np.float32).reshape(T, DIM)
    d = {"xT": np.ascontiguousarray(x[core * TC:(core + 1) * TC, :].T)}
    for pch in "qkvo":
        Wm = np.asarray(inputs[f"L0_attn_W{pch}"], np.float32)
        Bm = np.asarray(inputs[f"L0_attn_B{pch}"], np.float32)
        Am = np.asarray(inputs[f"L0_attn_A{pch}"], np.float32)
        bm = np.asarray(inputs[f"L0_attn_b{pch}"], np.float32)
        d[f"W{pch}T"] = np.ascontiguousarray(Wm.T)
        d[f"B{pch}T"] = np.ascontiguousarray(Bm.T)
        d[f"A{pch}T"] = np.ascontiguousarray(Am.T)
        d[f"b{pch}"] = np.ascontiguousarray(bm.reshape(DC, 128).T)
        d[f"b{pch}row"] = bm.reshape(1, DIM)
    d["Wg"] = np.asarray(inputs["L0_ffn_Wg"], np.float32)
    d["W1c"] = np.ascontiguousarray(np.asarray(inputs["L0_ffn_W1"], np.float32)[core])
    d["W3c"] = np.ascontiguousarray(np.asarray(inputs["L0_ffn_W3"], np.float32)[core])
    d["W2c"] = np.ascontiguousarray(np.asarray(inputs["L0_ffn_W2"], np.float32)[core])
    d["SW1"] = np.asarray(inputs["L0_ffn_sw1"], np.float32)
    d["SW3"] = np.asarray(inputs["L0_ffn_sw3"], np.float32)
    d["SW2"] = np.asarray(inputs["L0_ffn_sw2"], np.float32)
    d["MYEXP"] = np.full((1, 1), float(core), np.float32)
    d["LT"] = np.triu(np.ones((128, 128), np.float32), 1)
    d["IDENT"] = np.eye(128, dtype=np.float32)
    return d


def kernel_fused(**inputs):
    nc = build_fused()
    res = run_bass_kernel_spmd(nc, [_host_fused_inputs(inputs, c) for c in range(NCORE)],
                               core_ids=list(range(NCORE)))
    out = np.concatenate(
        [np.ascontiguousarray(np.asarray(res.results[c]["OUT_T"]).T) for c in range(NCORE)],
        axis=0)
    return out.reshape(2, 4096, DIM)


# version 07
# speedup vs baseline: 1.0135x; 1.0135x over previous
import contextlib
import numpy as np
import ml_dtypes
import concourse.bass as bass
import concourse.mybir as mybir
from concourse.bass import IndirectOffsetOnAxis
from concourse.bass_utils import run_bass_kernel_spmd

F32 = mybir.dt.float32
BF16 = mybir.dt.bfloat16
I32 = mybir.dt.int32
AF = mybir.ActivationFunctionType
ALU = mybir.AluOpType
AX = mybir.AxisListType

BF = ml_dtypes.bfloat16

DIM = 2048
HEADS = 16
HD = 128
RANK = 16
SCALING = 0.5
SCALE = HD ** -0.5
E = 8
H = 2816
HC = H // 128
DC = DIM // 128
T = 8192
TC = 1024
CAP = 1024
NCORE = 8
EPS = 1e-5

ENGINES = ("tensor", "scalar", "vector", "gpsimd", "sync")


class Prog:

    def __init__(self, nc, stack):
        self.nc = nc
        self.stack = stack
        self.ops = {e: [] for e in ENGINES}
        self.ctr = {}

    def sem(self, name):
        h = self.stack.enter_context(self.nc.semaphore(name))
        self.ctr[h] = 0
        return h

    def sb(self, name, shape, dtype):
        return self.stack.enter_context(self.nc.sbuf_tensor(name, shape, dtype))

    def ps(self, name, shape, dtype=F32):
        return self.stack.enter_context(self.nc.psum_tensor(name, shape, dtype))

    def sb_tmp(self, name, shape, dtype):
        cm = self.nc.sbuf_tensor(name, shape, dtype)
        t = cm.__enter__()
        if not hasattr(self, "live"):
            self.live = []
        self.live.append(cm)
        return t, cm

    def ps_tmp(self, name, shape, dtype=F32):
        cm = self.nc.psum_tensor(name, shape, dtype)
        t = cm.__enter__()
        if not hasattr(self, "live"):
            self.live = []
        self.live.append(cm)
        return t, cm

    def free(self, cm):
        cm.__exit__(None, None, None)
        if hasattr(self, "live") and cm in self.live:
            self.live.remove(cm)

    def free_all(self):
        if hasattr(self, "live"):
            for cm in reversed(self.live):
                cm.__exit__(None, None, None)
            self.live = []

    def on(self, eng, fn):
        self.ops[eng].append(fn)

    def inc(self, eng, sem, k=1):
        self.ctr[sem] += k
        prev = self.ops[eng].pop()

        def wrapped(e, prev=prev, sem=sem, k=k):
            return prev(e).then_inc(sem, k)

        self.ops[eng].append(wrapped)
        return self.ctr[sem]

    def wait(self, eng, sem, val=None):
        v = self.ctr[sem] if val is None else val
        if v > 0:
            self.on(eng, lambda e, sem=sem, v=v: e.wait_ge(sem, v))
        return v

    def drain(self):
        for sem, v in list(self.ctr.items()):
            if v > 0:
                self.wait("sync", sem, v)
                self.wait("gpsimd", sem, v)

    def emit(self, block):
        for name in ENGINES:
            ops = self.ops[name]
            if not ops:
                continue

            def body(e, ops=ops):
                for f in ops:
                    f(e)

            getattr(block, name)(body)



_L1_PROJS = ["q", "k", "v"]
_F1_NORMS = True
_F1_KEXT = False
_F1_STORES = True
_F1_PROJS = ["q", "k", "v"]


def build_l1():
    nc = bass.Bass("TRN2", target_bir_lowering=False)
    xT = nc.dram_tensor("xT", [DIM, TC], F32, kind="ExternalInput").ap()
    W = {}
    for pch in "qkv":
        W[pch] = {
            "WT": nc.dram_tensor(f"W{pch}T", [DIM, DIM], F32, kind="ExternalInput").ap(),
            "BT": nc.dram_tensor(f"B{pch}T", [DIM, RANK], F32, kind="ExternalInput").ap(),
            "AT": nc.dram_tensor(f"A{pch}T", [RANK, DIM], F32, kind="ExternalInput").ap(),
            "b": nc.dram_tensor(f"b{pch}", [128, DC], F32, kind="ExternalInput").ap(),
            "brow": nc.dram_tensor(f"b{pch}row", [1, DIM], F32, kind="ExternalInput").ap(),
        }
    QF = nc.dram_tensor("QF", [DIM, TC], BF16, kind="ExternalOutput").ap()
    KF = nc.dram_tensor("KF", [DIM, TC], BF16, kind="ExternalOutput").ap()
    V = nc.dram_tensor("V", [TC, DIM], BF16, kind="ExternalOutput").ap()
    QN = nc.dram_tensor("QN", [1, HEADS], F32, kind="ExternalOutput").ap()
    KN = nc.dram_tensor("KN", [1, HEADS], F32, kind="ExternalOutput").ap()

    NSTG = 4
    with contextlib.ExitStack() as stack:
        p = Prog(nc, stack)
        xb = p.sb("xb", [128, DC * TC], BF16)
        wt0 = p.sb("wt0", [128, DC * DIM], BF16)
        wt1 = p.sb("wt1", [128, DC * DIM], BF16)
        bt0 = p.sb("bt0", [128, DC * RANK], BF16)
        bt1 = p.sb("bt1", [128, DC * RANK], BF16)
        at0 = p.sb("at0", [RANK, DIM], BF16)
        at1 = p.sb("at1", [RANK, DIM], BF16)
        btile0 = p.sb("btile0", [128, DC], F32)
        btile1 = p.sb("btile1", [128, DC], F32)
        brow0 = p.sb("brow0", [1, DIM], BF16)
        brow1 = p.sb("brow1", [1, DIM], BF16)
        xlora = p.sb("xlora", [RANK, TC], BF16)
        stage_ring = [p.sb(f"stg{i}", [128, TC], BF16) for i in range(NSTG)]
        sq = p.sb("sq", [128, TC], BF16)
        ones_col = p.sb("ones_col", [128, 1], BF16)
        ones_row = p.sb("ones_row", [1, 128], BF16)
        qn_sb = p.sb("qn_sb", [1, HEADS], F32)
        kn_sb = p.sb("kn_sb", [1, HEADS], F32)
        psA = p.ps("psA", [128, 1024])
        psB = p.ps("psB", [128, 1024])
        psxb = p.ps("psxb", [RANK, TC])
        psn = p.ps("psn", [1, 1024])
        block = stack.enter_context(nc.Block())
        s_x = p.sem("s_x")
        s_w = [p.sem("s_w0"), p.sem("s_w1")]
        s_mm = p.sem("s_mm")
        s_ev = p.sem("s_ev")
        s_dv = p.sem("s_dv")
        s_stg = [p.sem(f"s_stg{i}") for i in range(NSTG)]
        s_on = p.sem("s_on")

        wts = [wt0, wt1]
        bts = [bt0, bt1]
        ats = [at0, at1]
        btiles = [btile0, btile1]
        brows = [brow0, brow1]
        pss = [psA, psB]
        projs = list(_L1_PROJS)

        for dc in range(DC):
            p.on("gpsimd", lambda g, dc=dc: g.dma_start(
                out=xb[:, dc * TC:(dc + 1) * TC], in_=xT[dc * 128:(dc + 1) * 128, :]))
            p.inc("gpsimd", s_x, 16)
        p.on("vector", lambda v: v.memset(ones_col[:], 1.0))
        p.inc("vector", s_dv)
        p.on("vector", lambda v: v.memset(ones_row[:], 1.0))
        v_const = p.inc("vector", s_dv)

        def load_w(pi):
            name = projs[pi]
            k = pi % 2
            if pi >= 2:
                p.wait("gpsimd", s_mm)
            for dc in range(DC):
                p.on("gpsimd", lambda g, dc=dc, k=k, name=name: g.dma_start(
                    out=wts[k][:, dc * DIM:(dc + 1) * DIM],
                    in_=W[name]["WT"][dc * 128:(dc + 1) * 128, :]))
                p.inc("gpsimd", s_w[k], 16)
            p.on("gpsimd", lambda g, k=k, name=name: g.dma_start(
                out=bts[k][:].rearrange("p (dc r) -> p dc r", r=RANK),
                in_=W[name]["BT"].rearrange("(dc p) r -> p dc r", p=128)))
            p.inc("gpsimd", s_w[k], 16)
            p.on("gpsimd", lambda g, k=k, name=name: g.dma_start(out=ats[k][:], in_=W[name]["AT"][:, :]))
            p.inc("gpsimd", s_w[k], 16)
            p.on("gpsimd", lambda g, k=k, name=name: g.dma_start(out=btiles[k][:], in_=W[name]["b"][:, :]))
            p.inc("gpsimd", s_w[k], 16)
            p.on("gpsimd", lambda g, k=k, name=name: g.dma_start(out=brows[k][:], in_=W[name]["brow"][:, :]))
            return p.inc("gpsimd", s_w[k], 16)

        ld_done = [load_w(0)]
        outdst = {"q": QF, "k": KF}
        ndst = {"q": qn_sb, "k": kn_sb}
        ev_ms = []
        stg_uses = [0] * NSTG
        gflat = [0]

        def group(emit_mms, emit_store, do_norm, name, oc):
            gi = gflat[0]
            gflat[0] += 1
            ps = pss[gi % 2]
            slot = gi % NSTG
            stg = stage_ring[slot]
            if gi >= 2:
                p.wait("tensor", s_ev, ev_ms[gi - 2])
            emit_mms(ps)
            mm_grp = p.inc("tensor", s_mm)
            p.wait("scalar", s_mm, mm_grp)
            if stg_uses[slot] > 0:
                p.wait("scalar", s_stg[slot], 16 * stg_uses[slot])
            emit_evac = emit_store[0]
            emit_evac(ps, stg)
            ev = p.inc("scalar", s_ev)
            ev_ms.append(ev)
            p.wait("sync", s_ev, ev)
            emit_store[1](stg)
            stg_uses[slot] += 1
            p.inc("sync", s_stg[slot], 16)
            if do_norm:
                p.wait("scalar", s_ev, ev)
                p.on("scalar", lambda s, stg=stg: s.activation(sq[:], stg[:], AF.Square))
                ev_sq = p.inc("scalar", s_ev)
                p.wait("tensor", s_ev, ev_sq)
                p.wait("tensor", s_dv)
                for half in range(2):
                    p.on("tensor", lambda t, half=half: t.matmul(
                        psn[:, half * 512:(half + 1) * 512], ones_col[:],
                        sq[:, half * 512:(half + 1) * 512],
                        start=True, stop=True))
                mm_n = p.inc("tensor", s_mm)
                p.wait("vector", s_mm, mm_n)
                p.on("vector", lambda v, oc=oc, name=name: v.tensor_reduce(
                    ndst[name][0:1, oc:oc + 1], psn[0:1, :], axis=AX.X, op=ALU.max))
                p.inc("vector", s_dv)

        for pi, name in enumerate(projs):
            k = pi % 2
            wbuf, btb, atb, btlb, brb = wts[k], bts[k], ats[k], btiles[k], brows[k]
            p.wait("tensor", s_w[k], ld_done[pi])
            if pi == 0:
                p.wait("tensor", s_x)
            if pi + 1 < len(projs):
                ld_done.append(load_w(pi + 1))

            for half in range(2):
                for dc in range(DC):
                    p.on("tensor", lambda t, dc=dc, half=half, btb=btb: t.matmul(
                        psxb[:, half * 512:(half + 1) * 512],
                        btb[:, dc * RANK:(dc + 1) * RANK],
                        xb[:, dc * TC + half * 512: dc * TC + (half + 1) * 512],
                        start=(dc == 0), stop=(dc == DC - 1)))
            p.inc("tensor", s_mm)
            p.wait("scalar", s_mm)
            p.on("scalar", lambda s: s.activation(xlora[:], psxb[:], AF.Identity, scale=SCALING))
            ev_lora = p.inc("scalar", s_ev)

            if name in ("q", "k"):
                for oc in range(DC):
                    def mms(ps, oc=oc, wbuf=wbuf, atb=atb):
                        if oc == 0:
                            p.wait("tensor", s_ev, ev_lora)
                        for half in range(2):
                            for dc in range(DC):
                                p.on("tensor", lambda t, oc=oc, dc=dc, half=half, ps=ps, wbuf=wbuf: t.matmul(
                                    ps[:, half * 512:(half + 1) * 512],
                                    wbuf[:, dc * DIM + oc * 128: dc * DIM + (oc + 1) * 128],
                                    xb[:, dc * TC + half * 512: dc * TC + (half + 1) * 512],
                                    start=(dc == 0), stop=False))
                            p.on("tensor", lambda t, oc=oc, half=half, ps=ps, atb=atb: t.matmul(
                                ps[:, half * 512:(half + 1) * 512],
                                atb[:, oc * 128:(oc + 1) * 128],
                                xlora[:, half * 512:(half + 1) * 512],
                                start=False, stop=True))

                    def evac(ps, stg, oc=oc, btlb=btlb):
                        p.on("scalar", lambda s, ps=ps, stg=stg, oc=oc, btlb=btlb: s.activation(
                            stg[:], ps[:], AF.Identity, bias=btlb[:, oc:oc + 1]))

                    def store(stg, oc=oc, name=name):
                        p.on("sync", lambda sy, oc=oc, stg=stg, name=name: sy.dma_start(
                            out=outdst[name][oc * 128:(oc + 1) * 128, :], in_=stg[:]))

                    group(mms, (evac, store), True, name, oc)
            else:
                for tc8 in range(8):
                    for on4 in range(4):
                        def mms(ps, tc8=tc8, on4=on4, wbuf=wbuf, atb=atb, brb=brb):
                            if tc8 == 0 and on4 == 0:
                                p.wait("tensor", s_ev, ev_lora)
                                p.wait("tensor", s_dv, v_const)
                            for dc in range(DC):
                                p.on("tensor", lambda t, tc8=tc8, on4=on4, dc=dc, ps=ps, wbuf=wbuf: t.matmul(
                                    ps[:, 0:512],
                                    xb[:, dc * TC + tc8 * 128: dc * TC + (tc8 + 1) * 128],
                                    wbuf[:, dc * DIM + on4 * 512: dc * DIM + (on4 + 1) * 512],
                                    start=(dc == 0), stop=False))
                            p.on("tensor", lambda t, tc8=tc8, on4=on4, ps=ps, atb=atb: t.matmul(
                                ps[:, 0:512],
                                xlora[:, tc8 * 128:(tc8 + 1) * 128],
                                atb[:, on4 * 512:(on4 + 1) * 512],
                                start=False, stop=False))
                            p.on("tensor", lambda t, on4=on4, ps=ps, brb=brb: t.matmul(
                                ps[:, 0:512], ones_row[:],
                                brb[0:1, on4 * 512:(on4 + 1) * 512],
                                start=False, stop=True))

                        def evac(ps, stg):
                            p.on("scalar", lambda s, ps=ps, stg=stg: s.activation(
                                stg[:, 0:512], ps[:, 0:512], AF.Identity))

                        def store(stg, tc8=tc8, on4=on4):
                            p.on("sync", lambda sy, tc8=tc8, on4=on4, stg=stg: sy.dma_start(
                                out=V[tc8 * 128:(tc8 + 1) * 128, on4 * 512:(on4 + 1) * 512],
                                in_=stg[:, 0:512]))

                        group(mms, (evac, store), False, name, 0)

        p.wait("sync", s_dv)
        p.on("sync", lambda sy: sy.dma_start(out=QN[:, :], in_=qn_sb[:]))
        p.inc("sync", s_on, 16)
        p.on("sync", lambda sy: sy.dma_start(out=KN[:, :], in_=kn_sb[:]))
        p.inc("sync", s_on, 16)
        for i in range(NSTG):
            p.wait("sync", s_stg[i], 16 * stg_uses[i])
        p.wait("sync", s_on)
        p.emit(block)
    return nc


def _host_l1_inputs(inputs, core):
    x = np.asarray(inputs["L0_attn_x"], np.float32).reshape(T, DIM)
    xT = np.ascontiguousarray(x[core * TC:(core + 1) * TC, :].T)
    d = {"xT": xT}
    for pch in "qkv":
        Wm = np.asarray(inputs[f"L0_attn_W{pch}"], np.float32)
        Bm = np.asarray(inputs[f"L0_attn_B{pch}"], np.float32)
        Am = np.asarray(inputs[f"L0_attn_A{pch}"], np.float32)
        bm = np.asarray(inputs[f"L0_attn_b{pch}"], np.float32)
        d[f"W{pch}T"] = np.ascontiguousarray(Wm.T)
        d[f"B{pch}T"] = np.ascontiguousarray(Bm.T)
        d[f"A{pch}T"] = np.ascontiguousarray(Am.T)
        d[f"b{pch}"] = np.ascontiguousarray(bm.reshape(DC, 128).T)
        d[f"b{pch}row"] = bm.reshape(1, DIM)
    return d






def build_l2():
    nc = bass.Bass("TRN2", target_bir_lowering=False)
    QFi = nc.dram_tensor("QFi", [DIM, TC], BF16, kind="ExternalInput").ap()
    KFall = nc.dram_tensor("KFall", [DIM, 4 * TC], BF16, kind="ExternalInput").ap()
    Vall = nc.dram_tensor("Vall", [4 * TC, DIM], BF16, kind="ExternalInput").ap()
    QNi = nc.dram_tensor("QNi", [1, HEADS], F32, kind="ExternalInput").ap()
    KN4 = nc.dram_tensor("KN4", [1, 4 * HEADS], F32, kind="ExternalInput").ap()
    xTi = nc.dram_tensor("xTi", [DIM, TC], F32, kind="ExternalInput").ap()
    WoT = nc.dram_tensor("WoT", [DIM, DIM], F32, kind="ExternalInput").ap()
    BoT = nc.dram_tensor("BoT", [DIM, RANK], F32, kind="ExternalInput").ap()
    AoT = nc.dram_tensor("AoT", [RANK, DIM], F32, kind="ExternalInput").ap()
    bo = nc.dram_tensor("bo", [128, DC], F32, kind="ExternalInput").ap()
    Wg = nc.dram_tensor("Wg", [DIM, E], F32, kind="ExternalInput").ap()
    IDENT = nc.dram_tensor("IDENT", [128, 128], F32, kind="ExternalInput").ap()
    X2T = nc.dram_tensor("X2T", [DIM, TC], F32, kind="ExternalOutput").ap()
    X2TOK = nc.dram_tensor("X2TOK", [TC, DIM], BF16, kind="ExternalOutput").ap()
    LOGITS = nc.dram_tensor("LOGITS", [TC, E], F32, kind="ExternalOutput").ap()

    NKC = 4 * TC // 128

    with contextlib.ExitStack() as stack:
        p = Prog(nc, stack)
        cat = p.sb("cat", [128, HEADS * TC], BF16)
        ones_col = p.sb("ones_col", [128, 1], BF16)
        ones_rf = p.sb("ones_rf", [1, 128], F32)
        negb = p.sb("negb", [128, HEADS], F32)
        nrow = p.sb("nrow", [1, HEADS], F32)
        krow = p.sb("krow", [1, 4 * HEADS], F32)
        kmax = p.sb("kmax", [1, HEADS], F32)
        r_row = p.sb("r_row", [1, 512], F32)
        rt = p.sb("rt", [128, 512], F32)
        wg_sb = p.sb("wg_sb", [128, DC * E], BF16)
        ident = p.sb("ident", [128, 128], BF16)
        botile = p.sb("botile", [128, DC], F32)
        btlora = p.sb("btlora", [128, DC * RANK], BF16)
        atlora = p.sb("atlora", [RANK, DIM], BF16)
        qf, cm_qf = p.sb_tmp("qf", [128, HEADS * TC], BF16)
        kh0, cm_kh0 = p.sb_tmp("kh0", [128, 4 * TC], BF16)
        kh1, cm_kh1 = p.sb_tmp("kh1", [128, 4 * TC], BF16)
        vh0, cm_vh0 = p.sb_tmp("vh0", [128, 4 * TC], BF16)
        vh1, cm_vh1 = p.sb_tmp("vh1", [128, 4 * TC], BF16)
        e0, cm_e0 = p.sb_tmp("e0", [128, 512], BF16)
        e1, cm_e1 = p.sb_tmp("e1", [128, 512], BF16)
        ps_s0, cm_ps_s0 = p.ps_tmp("ps_s0", [128, 512])
        ps_s1, cm_ps_s1 = p.ps_tmp("ps_s1", [128, 512])
        ps_o, cm_ps_o = p.ps_tmp("ps_o", [128, 512])
        ps_g, cm_ps_g = p.ps_tmp("ps_g", [1, 512])
        ps_r, cm_ps_r = p.ps_tmp("ps_r", [128, 512])

        block = stack.enter_context(nc.Block())
        s_qf = p.sem("s_qf")
        s_kh = [p.sem("s_kh0"), p.sem("s_kh1")]
        s_vh = [p.sem("s_vh0"), p.sem("s_vh1")]
        s_const = p.sem("s_const")
        s_s = p.sem("s_s")
        s_e = p.sem("s_e")
        s_av = p.sem("s_av")
        s_mm = p.sem("s_mm")
        s_ev = p.sem("s_ev")
        s_dv = p.sem("s_dv")
        s_xt = [p.sem("s_xt0"), p.sem("s_xt1")]
        s_wo = [p.sem("s_wo0"), p.sem("s_wo1")]
        s_sto = p.sem("s_sto")
        s_xts = [p.sem("s_xts0"), p.sem("s_xts1")]

        khs, vhs, es = [kh0, kh1], [vh0, vh1], [e0, e1]

        for dc in range(DC):
            p.on("gpsimd", lambda g, dc=dc: g.dma_start(
                out=qf[:, dc * TC:(dc + 1) * TC], in_=QFi[dc * 128:(dc + 1) * 128, :]))
            p.inc("gpsimd", s_qf, 16)
        for t, src in ((nrow, QNi), (krow, KN4)):
            p.on("gpsimd", lambda g, t=t, src=src: g.dma_start(out=t[:], in_=src[:, :]))
            p.inc("gpsimd", s_const, 16)
        p.on("gpsimd", lambda g: g.dma_start(
            out=wg_sb[:].rearrange("p (dc e) -> p dc e", e=E),
            in_=Wg.rearrange("(dc p) e -> p dc e", p=128)))
        p.inc("gpsimd", s_const, 16)
        p.on("gpsimd", lambda g: g.dma_start(out=ident[:], in_=IDENT[:, :]))
        p.inc("gpsimd", s_const, 16)
        p.on("gpsimd", lambda g: g.dma_start(out=botile[:], in_=bo[:, :]))
        p.inc("gpsimd", s_const, 16)
        p.on("gpsimd", lambda g: g.dma_start(
            out=btlora[:].rearrange("p (dc r) -> p dc r", r=RANK),
            in_=BoT.rearrange("(dc p) r -> p dc r", p=128)))
        p.inc("gpsimd", s_const, 16)
        p.on("gpsimd", lambda g: g.dma_start(out=atlora[:], in_=AoT[:, :]))
        p.inc("gpsimd", s_const, 16)
        p.on("vector", lambda v: v.memset(ones_col[:], 1.0))
        p.inc("vector", s_dv)
        p.on("vector", lambda v: v.memset(ones_rf[:], 1.0))
        p.inc("vector", s_dv)

        p.wait("vector", s_const)
        p.on("vector", lambda v: v.tensor_reduce(
            kmax[0:1, :],
            krow[:].rearrange("o (r h) -> o h r", h=HEADS),
            axis=AX.X, op=ALU.max))
        kn_red = p.inc("vector", s_dv)
        p.wait("vector", s_dv, kn_red)
        p.on("vector", lambda v: v.tensor_mul(nrow[0:1, :], nrow[0:1, :], kmax[0:1, :]))
        nmul = p.inc("vector", s_dv)
        p.wait("scalar", s_dv, nmul)
        p.on("scalar", lambda s: s.activation(nrow[0:1, :], nrow[0:1, :], AF.Ln))
        ev1 = p.inc("scalar", s_ev)
        p.wait("scalar", s_ev, ev1)
        p.on("scalar", lambda s: s.activation(nrow[0:1, :], nrow[0:1, :], AF.Exp, scale=0.5))
        ev2 = p.inc("scalar", s_ev)
        p.wait("vector", s_ev, ev2)
        p.on("vector", lambda v: v.tensor_scalar(
            nrow[0:1, :], nrow[0:1, :], -SCALE, None, ALU.mult))
        nneg = p.inc("vector", s_dv)
        p.wait("tensor", s_dv, nneg)
        p.on("tensor", lambda t: t.matmul(ps_r[:, 0:HEADS], ones_rf[:], nrow[0:1, :],
                                          start=True, stop=True))
        bmm = p.inc("tensor", s_mm)
        p.wait("vector", s_mm, bmm)
        p.on("vector", lambda v: v.tensor_copy(negb[:], ps_r[:, 0:HEADS]))
        negb_done = p.inc("vector", s_dv)

        kh_uses = [0, 0]
        vh_uses = [0, 0]
        s_cnt = [0]
        av_cnt = [0]
        kh_ms = {}
        vh_ms = {}

        def load_head(h):
            sl = h % 2
            if h >= 2:
                p.wait("gpsimd", s_s, 64 * (h - 1))
                p.wait("gpsimd", s_av, 64 * (h - 1))
            p.on("gpsimd", lambda g, h=h, sl=sl: g.dma_start(
                out=khs[sl][:], in_=KFall[h * 128:(h + 1) * 128, :]))
            kh_uses[sl] += 1
            kh_ms[h] = 16 * kh_uses[sl]
            p.inc("gpsimd", s_kh[sl], 16)
            for qtr in range(4):
                p.on("gpsimd", lambda g, h=h, sl=sl, qtr=qtr: g.dma_start(
                    out=vhs[sl][:].rearrange("p (c j) -> p c j", j=128)[:, qtr * 8:(qtr + 1) * 8, :],
                    in_=Vall[:, h * 128:(h + 1) * 128].rearrange(
                        "(c p) j -> p c j", p=128)[:, qtr * 8:(qtr + 1) * 8, :]))
                p.inc("gpsimd", s_vh[sl], 16)
            vh_uses[sl] += 1
            vh_ms[h] = 64 * vh_uses[sl]

        load_head(0)
        load_head(1)

        for h in range(HEADS):
            sl = h % 2
            if h + 2 < HEADS:
                load_head(h + 2)
            p.wait("tensor", s_kh[sl], kh_ms[h])
            p.wait("tensor", s_vh[sl], vh_ms[h])
            if h == 0:
                p.wait("tensor", s_qf)
                p.wait("tensor", s_dv, negb_done)
            for qh in range(2):
                s_base = s_cnt[0]
                e_base = av_cnt[0]
                e_sem_base = s_cnt[0]
                prev_exps = s_cnt[0]
                p.wait("tensor", s_e, prev_exps)
                p.wait("tensor", s_dv)
                qsl = slice(h * TC + qh * 512, h * TC + (qh + 1) * 512)
                for c in range(2):
                    p.on("tensor", lambda t, c=c, sl=sl, qsl=qsl: t.matmul(
                        (ps_s0 if c % 2 == 0 else ps_s1)[:, :],
                        khs[sl][:, c * 128:(c + 1) * 128],
                        qf[:, qsl], start=True, stop=True))
                    s_cnt[0] += 1
                    p.inc("tensor", s_s)
                for c in range(NKC):
                    p.wait("scalar", s_s, s_base + c + 1)
                    if c >= 2:
                        p.wait("scalar", s_av, av_cnt[0] - 1)
                    p.on("scalar", lambda s, c=c, h=h: s.activation(
                        es[c % 2][:], (ps_s0 if c % 2 == 0 else ps_s1)[:],
                        AF.Exp, bias=negb[:, h:h + 1], scale=SCALE))
                    p.inc("scalar", s_e)
                    p.wait("tensor", s_e, s_base + c + 1)
                    p.on("tensor", lambda t, c=c, sl=sl: t.matmul(
                        ps_o[:, :], vhs[sl][:, c * 128:(c + 1) * 128], es[c % 2][:],
                        start=(c == 0), stop=(c == NKC - 1)))
                    p.on("tensor", lambda t, c=c: t.matmul(
                        ps_g[:, :], ones_col[:], es[c % 2][:],
                        start=(c == 0), stop=(c == NKC - 1)))
                    av_cnt[0] += 1
                    p.inc("tensor", s_av)
                    if c + 2 < NKC:
                        p.on("tensor", lambda t, c=c, sl=sl, qsl=qsl: t.matmul(
                            (ps_s0 if c % 2 == 0 else ps_s1)[:, :],
                            khs[sl][:, (c + 2) * 128:(c + 3) * 128],
                            qf[:, qsl], start=True, stop=True))
                        s_cnt[0] += 1
                        p.inc("tensor", s_s)
                p.wait("vector", s_av, av_cnt[0])
                p.on("vector", lambda v: v.reciprocal(r_row[:], ps_g[0:1, :]))
                rc = p.inc("vector", s_dv)
                p.wait("tensor", s_dv, rc)
                p.on("tensor", lambda t: t.matmul(ps_r[:, :], ones_rf[:], r_row[:],
                                                  start=True, stop=True))
                rmm = p.inc("tensor", s_mm)
                p.wait("vector", s_mm, rmm)
                p.on("vector", lambda v: v.tensor_copy(rt[:], ps_r[:, :]))
                rcp = p.inc("vector", s_dv)
                p.wait("vector", s_dv, rcp)
                p.on("vector", lambda v, qsl=qsl: v.tensor_mul(cat[:, qsl], ps_o[:], rt[:]))
                p.inc("vector", s_dv)

        att_s_total = s_cnt[0]
        att_av_total = av_cnt[0]
        att_dv_after = p.ctr[s_dv]

        for cm in (cm_ps_r, cm_ps_g, cm_ps_o, cm_ps_s1, cm_ps_s0):
            p.free(cm)
        for cm in (cm_e1, cm_e0, cm_vh1, cm_vh0, cm_kh1, cm_kh0, cm_qf):
            p.free(cm)
        pre1 = p.sb("pre1", [128, DC * TC], F32)
        x2b = None
        tmp32 = [p.sb("tmp32_0", [128, TC], F32), p.sb("tmp32_1", [128, TC], F32)]
        prebf = [p.sb("prebf0", [128, TC], BF16), p.sb("prebf1", [128, TC], BF16)]
        sqc = [p.sb("sqc0", [128, TC], BF16), p.sb("sqc1", [128, TC], BF16)]
        xts = [p.sb("xts0", [128, TC], F32), p.sb("xts1", [128, TC], F32)]
        wocs = [p.sb("woc0", [128, DC * 128], BF16), p.sb("woc1", [128, DC * 128], BF16)]
        xlo = p.sb("xlo", [RANK, TC], BF16)
        mean_row = p.sb("mean_row", [1, TC], F32)
        q_row = p.sb("q_row", [1, TC], F32)
        t_row = p.sb("t_row", [1, TC], F32)
        rstd_row = p.sb("rstd_row", [1, TC], F32)
        psA, cm_psA = p.ps_tmp("psA", [128, 1024])
        psB, cm_psB = p.ps_tmp("psB", [128, 1024])
        psS, cm_psS = p.ps_tmp("psS", [1, 1024])
        psQ, cm_psQ = p.ps_tmp("psQ", [1, 1024])
        pss = [psA, psB]

        p.wait("gpsimd", s_s, att_s_total)
        p.wait("gpsimd", s_av, att_av_total)
        p.wait("gpsimd", s_e, att_s_total)
        p.wait("gpsimd", s_dv, att_dv_after)

        p.wait("tensor", s_dv, att_dv_after)
        p.wait("tensor", s_const)
        for half in range(2):
            for dc in range(DC):
                p.on("tensor", lambda t, dc=dc, half=half: t.matmul(
                    psA[0:RANK, half * 512:(half + 1) * 512],
                    btlora[:, dc * RANK:(dc + 1) * RANK],
                    cat[:, dc * TC + half * 512: dc * TC + (half + 1) * 512],
                    start=(dc == 0), stop=(dc == DC - 1)))
        lomm = p.inc("tensor", s_mm)
        p.wait("scalar", s_mm, lomm)
        p.on("scalar", lambda s: s.activation(xlo[:], psA[0:RANK, :], AF.Identity, scale=SCALING))
        ev_lo = p.inc("scalar", s_ev)

        xt_ms = {}
        wo_ms = {}
        xt_uses = [0, 0]
        wo_uses = [0, 0]
        dv_tt_ms = {}
        ov_ms = {}

        def load_oc(oc):
            sl = oc % 2
            if oc >= 2:
                p.wait("gpsimd", s_dv, dv_tt_ms[oc - 2])
                p.wait("gpsimd", s_mm, ov_ms[oc - 2])
            p.on("gpsimd", lambda g, oc=oc, sl=sl: g.dma_start(
                out=xts[sl][:], in_=xTi[oc * 128:(oc + 1) * 128, :]))
            xt_uses[sl] += 1
            xt_ms[oc] = 16 * xt_uses[sl]
            p.inc("gpsimd", s_xt[sl], 16)
            p.on("gpsimd", lambda g, oc=oc, sl=sl: g.dma_start(
                out=wocs[sl][:].rearrange("p (dc j) -> p dc j", j=128),
                in_=WoT[:, oc * 128:(oc + 1) * 128].rearrange("(dc p) j -> p dc j", p=128)))
            wo_uses[sl] += 1
            wo_ms[oc] = 16 * wo_uses[sl]
            p.inc("gpsimd", s_wo[sl], 16)

        load_oc(0)
        load_oc(1)
        ev_ms2 = []
        for oc in range(DC):
            sl = oc % 2
            ps = pss[oc % 2]
            p.wait("tensor", s_wo[sl], wo_ms[oc])
            if oc >= 2:
                p.wait("tensor", s_ev, ev_ms2[oc - 2])
            if oc == 0:
                p.wait("tensor", s_ev, ev_lo)
            for half in range(2):
                for dc in range(DC):
                    p.on("tensor", lambda t, oc=oc, dc=dc, half=half, ps=ps, sl=sl: t.matmul(
                        ps[:, half * 512:(half + 1) * 512],
                        wocs[sl][:, dc * 128:(dc + 1) * 128],
                        cat[:, dc * TC + half * 512: dc * TC + (half + 1) * 512],
                        start=(dc == 0), stop=False))
                p.on("tensor", lambda t, oc=oc, half=half, ps=ps: t.matmul(
                    ps[:, half * 512:(half + 1) * 512],
                    atlora[:, oc * 128:(oc + 1) * 128],
                    xlo[:, half * 512:(half + 1) * 512],
                    start=False, stop=True))
            ov_ms[oc] = p.inc("tensor", s_mm)
            p.wait("scalar", s_mm, ov_ms[oc])
            if oc >= 2:
                p.wait("scalar", s_dv, dv_tt_ms[oc - 2])
            p.on("scalar", lambda s, oc=oc, ps=ps, sl=sl: s.activation(
                tmp32[sl][:], ps[:], AF.Identity, bias=botile[:, oc:oc + 1]))
            ev = p.inc("scalar", s_ev)
            ev_ms2.append(ev)
            p.wait("vector", s_ev, ev)
            p.wait("vector", s_xt[sl], xt_ms[oc])
            p.on("vector", lambda v, oc=oc, sl=sl: v.tensor_add(
                pre1[:, oc * TC:(oc + 1) * TC], tmp32[sl][:], xts[sl][:]))
            dv_tt_ms[oc] = p.inc("vector", s_dv)
            p.wait("vector", s_dv, dv_tt_ms[oc])
            p.on("vector", lambda v, oc=oc, sl=sl: v.tensor_copy(
                prebf[sl][:], pre1[:, oc * TC:(oc + 1) * TC]))
            cpms = p.inc("vector", s_dv)
            p.wait("scalar", s_dv, cpms)
            p.on("scalar", lambda s, sl=sl: s.activation(sqc[sl][:], prebf[sl][:], AF.Square))
            sqev = p.inc("scalar", s_ev)
            p.wait("tensor", s_ev, sqev)
            for half in range(2):
                p.on("tensor", lambda t, oc=oc, half=half, sl=sl: t.matmul(
                    psS[0:1, half * 512:(half + 1) * 512], ones_col[:],
                    prebf[sl][:, half * 512:(half + 1) * 512],
                    start=(oc == 0), stop=(oc == DC - 1)))
                p.on("tensor", lambda t, oc=oc, half=half, sl=sl: t.matmul(
                    psQ[0:1, half * 512:(half + 1) * 512], ones_col[:],
                    sqc[sl][:, half * 512:(half + 1) * 512],
                    start=(oc == 0), stop=(oc == DC - 1)))
            stat_ms = p.inc("tensor", s_mm)
            if oc + 2 < DC:
                load_oc(oc + 2)
            dv_tt_ms[oc] = dv_tt_ms[oc]
            if oc >= 2:
                pass
            wo_ms[("stat", oc)] = stat_ms


        last_stat = wo_ms[("stat", DC - 1)]
        p.wait("vector", s_mm, last_stat)
        p.on("vector", lambda v: v.tensor_scalar(mean_row[:], psS[0:1, :], 1.0 / DIM, None, ALU.mult))
        m1 = p.inc("vector", s_dv)
        p.on("vector", lambda v: v.tensor_scalar(q_row[:], psQ[0:1, :], 1.0 / DIM, None, ALU.mult))
        m2 = p.inc("vector", s_dv)
        p.wait("vector", s_dv, m2)
        p.on("vector", lambda v: v.tensor_mul(t_row[:], mean_row[:], mean_row[:]))
        m3 = p.inc("vector", s_dv)
        p.wait("vector", s_dv, m3)
        p.on("vector", lambda v: v.tensor_sub(q_row[:], q_row[:], t_row[:]))
        m4 = p.inc("vector", s_dv)
        p.wait("vector", s_dv, m4)
        p.on("vector", lambda v: v.tensor_scalar(q_row[:], q_row[:], EPS, None, ALU.add))
        m5 = p.inc("vector", s_dv)
        p.wait("scalar", s_dv, m5)
        p.on("scalar", lambda s: s.activation(t_row[:], q_row[:], AF.Ln))
        e1_ = p.inc("scalar", s_ev)
        p.wait("scalar", s_ev, e1_)
        p.on("scalar", lambda s: s.activation(rstd_row[:], t_row[:], AF.Exp, scale=-0.5))
        e2_ = p.inc("scalar", s_ev)
        p.wait("tensor", s_ev, e2_)
        p.wait("tensor", s_dv, m1)
        for half in range(2):
            p.on("tensor", lambda t, half=half: t.matmul(
                psA[:, half * 512:(half + 1) * 512], ones_rf[:],
                mean_row[0:1, half * 512:(half + 1) * 512], start=True, stop=True))
            p.on("tensor", lambda t, half=half: t.matmul(
                psB[:, half * 512:(half + 1) * 512], ones_rf[:],
                rstd_row[0:1, half * 512:(half + 1) * 512], start=True, stop=True))
        mr_ms = p.inc("tensor", s_mm)

        p.free(cm_psQ)
        p.free(cm_psS)
        x2b = p.sb("x2b", [128, DC * TC], BF16)
        nrm_ms = {}
        for dc in range(DC):
            sl = dc % 2
            p.wait("vector", s_mm, mr_ms)
            p.on("vector", lambda v, dc=dc, sl=sl: v.tensor_sub(
                tmp32[sl][:], pre1[:, dc * TC:(dc + 1) * TC], psA[:]))
            n1 = p.inc("vector", s_dv)
            p.wait("vector", s_dv, n1)
            p.on("vector", lambda v, dc=dc, sl=sl: v.tensor_mul(
                pre1[:, dc * TC:(dc + 1) * TC], tmp32[sl][:], psB[:]))
            n2 = p.inc("vector", s_dv)
            p.wait("vector", s_dv, n2)
            p.on("vector", lambda v, dc=dc: v.tensor_copy(
                x2b[:, dc * TC:(dc + 1) * TC], pre1[:, dc * TC:(dc + 1) * TC]))
            n3 = p.inc("vector", s_dv)
            nrm_ms[dc] = n3
            p.wait("sync", s_dv, n3)
            p.on("sync", lambda sy, dc=dc: sy.dma_start(
                out=X2T[dc * 128:(dc + 1) * 128, :], in_=pre1[:, dc * TC:(dc + 1) * TC]))
            p.inc("sync", s_sto, 16)

        lstage = p.sb("lstage", [128, 8 * E], F32)
        psL, cm_psL = p.ps_tmp("psL", [128, E])
        p.wait("tensor", s_dv, nrm_ms[DC - 1])
        for tc8 in range(8):
            for dc in range(DC):
                p.on("tensor", lambda t, tc8=tc8, dc=dc: t.matmul(
                    psL[:, :],
                    x2b[:, dc * TC + tc8 * 128: dc * TC + (tc8 + 1) * 128],
                    wg_sb[:, dc * E:(dc + 1) * E],
                    start=(dc == 0), stop=(dc == DC - 1)))
            lg = p.inc("tensor", s_mm)
            p.wait("scalar", s_mm, lg)
            p.on("scalar", lambda s, tc8=tc8: s.activation(
                lstage[:, tc8 * E:(tc8 + 1) * E], psL[:], AF.Identity))
            lev = p.inc("scalar", s_ev)
            p.wait("tensor", s_ev, lev)
        p.wait("sync", s_ev)
        p.on("sync", lambda sy: sy.dma_start(
            out=LOGITS.rearrange("(t8 p) e -> p t8 e", p=128),
            in_=lstage[:].rearrange("p (t8 e) -> p t8 e", e=E)))
        p.inc("sync", s_sto, 16)

        xtok = [p.sb("xtok0", [128, DIM], BF16), p.sb("xtok1", [128, DIM], BF16)]
        ps_t0, cm_pst0 = p.ps_tmp("ps_t0", [128, 128], BF16)
        ps_t1, cm_pst1 = p.ps_tmp("ps_t1", [128, 128], BF16)
        pst = [ps_t0, ps_t1]
        xts_uses = [0, 0]
        for tc8 in range(8):
            sl = tc8 % 2
            if xts_uses[sl] > 0:
                p.wait("vector", s_xts[sl], 16 * xts_uses[sl])
            for dc in range(DC):
                psl2 = pst[dc % 2]
                p.on("tensor", lambda t, tc8=tc8, dc=dc, psl2=psl2: t.transpose(
                    psl2[:], x2b[:, dc * TC + tc8 * 128: dc * TC + (tc8 + 1) * 128],
                    ident[:]))
                trm = p.inc("tensor", s_mm)
                p.wait("vector", s_mm, trm)
                p.on("vector", lambda v, dc=dc, sl=sl, psl2=psl2: v.tensor_copy(
                    xtok[sl][:, dc * 128:(dc + 1) * 128], psl2[:]))
                cpm = p.inc("vector", s_dv)
                p.wait("tensor", s_dv, cpm)
            p.wait("sync", s_dv)
            p.on("sync", lambda sy, tc8=tc8, sl=sl: sy.dma_start(
                out=X2TOK[tc8 * 128:(tc8 + 1) * 128, :], in_=xtok[sl][:]))
            xts_uses[sl] += 1
            p.inc("sync", s_xts[sl], 16)

        p.wait("sync", s_sto)
        for i in range(2):
            p.wait("sync", s_xts[i], 16 * xts_uses[i])
        p.emit(block)
    return nc


def _host_l2_inputs(inputs, core, l1res):
    x = np.asarray(inputs["L0_attn_x"], np.float32).reshape(T, DIM)
    seq = core // 4
    d = {
        "QFi": np.asarray(l1res[core]["QF"]),
        "KFall": np.concatenate([np.asarray(l1res[4 * seq + i]["KF"]) for i in range(4)], axis=1),
        "Vall": np.concatenate([np.asarray(l1res[4 * seq + i]["V"]) for i in range(4)], axis=0),
        "QNi": np.asarray(l1res[core]["QN"]),
        "KN4": np.concatenate([np.asarray(l1res[4 * seq + i]["KN"]) for i in range(4)], axis=1),
        "xTi": np.ascontiguousarray(x[core * TC:(core + 1) * TC, :].T),
        "WoT": np.ascontiguousarray(np.asarray(inputs["L0_attn_Wo"], np.float32).T),
        "BoT": np.ascontiguousarray(np.asarray(inputs["L0_attn_Bo"], np.float32).T),
        "AoT": np.ascontiguousarray(np.asarray(inputs["L0_attn_Ao"], np.float32).T),
        "bo": np.ascontiguousarray(np.asarray(inputs["L0_attn_bo"], np.float32).reshape(DC, 128).T),
        "Wg": np.asarray(inputs["L0_ffn_Wg"], np.float32),
        "IDENT": np.eye(128, dtype=np.float32),
    }
    return d



def build_l3():
    nc = bass.Bass("TRN2", target_bir_lowering=False)
    X2Ti = nc.dram_tensor("X2Ti", [DIM, TC], F32, kind="ExternalInput").ap()
    X2ALL = nc.dram_tensor("X2ALL", [T, DIM], BF16, kind="ExternalInput").ap()
    LOGALL = nc.dram_tensor("LOGALL", [T, E], F32, kind="ExternalInput").ap()
    W1c = nc.dram_tensor("W1c", [DIM, H], F32, kind="ExternalInput").ap()
    W3c = nc.dram_tensor("W3c", [DIM, H], F32, kind="ExternalInput").ap()
    W2c = nc.dram_tensor("W2c", [H, DIM], F32, kind="ExternalInput").ap()
    SW1 = nc.dram_tensor("SW1", [DIM, H], F32, kind="ExternalInput").ap()
    SW3 = nc.dram_tensor("SW3", [DIM, H], F32, kind="ExternalInput").ap()
    SW2 = nc.dram_tensor("SW2", [H, DIM], F32, kind="ExternalInput").ap()
    MYEXP = nc.dram_tensor("MYEXP", [1, 1], F32, kind="ExternalInput").ap()
    LT = nc.dram_tensor("LT", [128, 128], F32, kind="ExternalInput").ap()
    IDENT = nc.dram_tensor("IDENT", [128, 128], F32, kind="ExternalInput").ap()
    OE = nc.dram_tensor("OE", [CAP, DIM], BF16, kind="ExternalOutput").ap()
    PRE2AT = nc.dram_tensor("PRE2AT", [DIM, TC], F32, kind="ExternalOutput").ap()
    FLAT = nc.dram_tensor("FLAT", [T, 1], I32, kind="ExternalOutput").ap()
    GK = nc.dram_tensor("GK", [T, 1], F32, kind="ExternalOutput").ap()
    st8 = nc.dram_tensor("st8", [1152, E], I32)

    with contextlib.ExitStack() as stack:
        p = Prog(nc, stack)
        lg = p.sb("lg", [128, 512], F32)
        mx = p.sb("mx", [128, 64], F32)
        exf = p.sb("exf", [128, 512], F32)
        esum = p.sb("esum", [128, 64], F32)
        gate = p.sb("gate", [128, 64], F32)
        ohge = p.sb("ohge", [128, 512], F32)
        iotaE = p.sb("iotaE", [128, 512], F32)
        sel = p.sb("sel", [128, 512], F32)
        eidx = p.sb("eidx", [128, 64], F32)
        oh = p.sb("oh", [128, 512], F32)
        cnt = p.sb("cnt", [128, 8], F32)
        cntb = p.sb("cntb", [128, 8], BF16)
        offs = p.sb("offs", [128, 8], F32)
        run = p.sb("run", [128, 8], F32)
        tmp8 = p.sb("tmp8", [128, 8], F32)
        posv = p.sb("posv", [128, 64], F32)
        keep = p.sb("keep", [128, 64], F32)
        gk = p.sb("gk", [128, 64], F32)
        f1 = p.sb("f1", [128, 64], F32)
        flat_i = p.sb("flat_i", [128, 64], I32)
        mycol = p.sb("mycol", [128, 1], F32)
        myexp_sb = p.sb("myexp_sb", [1, 1], F32)
        mk = p.sb("mk", [128, 64], F32)
        t2 = p.sb("t2", [128, 64], F32)
        flloc_i = p.sb("flloc_i", [128, 64], I32)
        lt_sb = p.sb("lt_sb", [128, 128], BF16)
        ident = p.sb("ident", [128, 128], BF16)
        ones_rf = p.sb("ones_rf", [1, 128], F32)
        zero72 = p.sb("zero72", [128, 72], I32)
        tok8 = [p.sb("tok80", [128, 8], I32), p.sb("tok81", [128, 8], I32)]
        g_sb, cm_g_sb = p.sb_tmp("g_sb", [128, HC * TC], BF16)
        x2tb = p.sb("x2tb", [128, DC * TC], BF16)

        block = stack.enter_context(nc.Block())
        s_c = p.sem("s_c")
        s_mm = p.sem("s_mm")
        s_ev = p.sem("s_ev")
        s_dv = p.sem("s_dv")
        s_gp = p.sem("s_gp")
        s_z = p.sem("s_z")
        s_sc = [p.sem("s_sc0"), p.sem("s_sc1")]
        s_ix = [p.sem("s_ix0"), p.sem("s_ix1")]
        s_gx = [p.sem("s_gx0"), p.sem("s_gx1")]
        s_w1 = [p.sem("s_w10"), p.sem("s_w11")]
        s_w2 = [p.sem("s_w20"), p.sem("s_w21")]
        s_x2 = p.sem("s_x2")
        s_xf = [p.sem("s_xf0"), p.sem("s_xf1")]
        s_so = [p.sem("s_so0"), p.sem("s_so1")]
        s_sp = [p.sem("s_sp0"), p.sem("s_sp1")]
        s_sfg = p.sem("s_sfg")

        p.on("gpsimd", lambda g: g.dma_start(
            out=lg[:].rearrange("p (a e) -> p a e", e=E),
            in_=LOGALL.rearrange("(p a) e -> p a e", p=128)))
        p.inc("gpsimd", s_c, 16)
        p.on("gpsimd", lambda g: g.dma_start(out=lt_sb[:], in_=LT[:, :]))
        p.inc("gpsimd", s_c, 16)
        p.on("gpsimd", lambda g: g.dma_start(out=ident[:], in_=IDENT[:, :]))
        p.inc("gpsimd", s_c, 16)
        p.on("gpsimd", lambda g: g.dma_start(out=myexp_sb[:], in_=MYEXP[:, :]))
        p.inc("gpsimd", s_c, 16)
        p.on("gpsimd", lambda g: g.iota(iotaE[:].rearrange("p (a e) -> p a e", e=E),
                                        pattern=[[0, 64], [1, 8]], base=0,
                                        channel_multiplier=0,
                                        allow_small_or_imprecise_dtypes=True))
        p.inc("gpsimd", s_gp)
        p.on("gpsimd", lambda g: g.memset(zero72[:], 0))
        p.inc("gpsimd", s_gp)
        p.on("vector", lambda v: v.memset(ones_rf[:], 1.0))
        v0 = p.inc("vector", s_dv)

        def vstep(fn):
            p.on("vector", fn)
            ms = p.inc("vector", s_dv)
            p.wait("vector", s_dv, ms)
            return ms

        p.wait("vector", s_c)
        p.wait("vector", s_gp, 1)
        vstep(lambda v: v.tensor_reduce(
            mx[:], lg[:].rearrange("p (a e) -> p a e", e=E), axis=AX.X, op=ALU.max))
        for e_ in range(E):
            p.on("vector", lambda v, e_=e_: v.tensor_sub(
                exf[:].rearrange("p (a e) -> p a e", e=E)[:, :, e_],
                lg[:].rearrange("p (a e) -> p a e", e=E)[:, :, e_], mx[:]))
            p.inc("vector", s_dv)
        dsub = p.wait("vector", s_dv)
        p.wait("scalar", s_dv, dsub)
        p.on("scalar", lambda s: s.activation(exf[:], exf[:], AF.Exp))
        eexp = p.inc("scalar", s_ev)
        p.wait("vector", s_ev, eexp)
        vstep(lambda v: v.tensor_reduce(
            esum[:], exf[:].rearrange("p (a e) -> p a e", e=E), axis=AX.X, op=ALU.add))
        vstep(lambda v: v.reciprocal(gate[:], esum[:]))
        for e_ in range(E):
            p.on("vector", lambda v, e_=e_: v.tensor_tensor(
                ohge[:].rearrange("p (a e) -> p a e", e=E)[:, :, e_],
                lg[:].rearrange("p (a e) -> p a e", e=E)[:, :, e_], mx[:], ALU.is_ge))
            p.inc("vector", s_dv)
        p.wait("vector", s_dv)
        vstep(lambda v: v.tensor_scalar(sel[:], iotaE[:], -8.0, None, ALU.add))
        vstep(lambda v: v.tensor_mul(sel[:], sel[:], ohge[:]))
        vstep(lambda v: v.tensor_scalar(sel[:], sel[:], 8.0, None, ALU.add))
        vstep(lambda v: v.tensor_reduce(
            eidx[:], sel[:].rearrange("p (a e) -> p a e", e=E), axis=AX.X, op=ALU.min))
        for e_ in range(E):
            p.on("vector", lambda v, e_=e_: v.tensor_scalar(
                oh[:].rearrange("p (a e) -> p a e", e=E)[:, :, e_],
                eidx[:], float(e_), None, ALU.is_equal))
            p.inc("vector", s_dv)
        p.wait("vector", s_dv)
        vstep(lambda v: v.tensor_reduce(
            cnt[:], oh[:].rearrange("p (a e) -> p e a", e=E), axis=AX.X, op=ALU.add))
        vstep(lambda v: v.tensor_copy(cntb[:], cnt[:]))
        cnt_ms = p.ctr[s_dv]
        p.wait("tensor", s_dv, cnt_ms)
        p.wait("tensor", s_c)
        ps8, cm_ps8 = p.ps_tmp("ps8", [128, 8])
        psc, cm_psc = p.ps_tmp("psc", [128, 1])
        p.on("tensor", lambda t: t.matmul(ps8[:], lt_sb[:], cntb[:], start=True, stop=True))
        p.inc("tensor", s_mm)
        p.on("tensor", lambda t: t.matmul(psc[:], ones_rf[:], myexp_sb[:], start=True, stop=True))
        offmm = p.inc("tensor", s_mm)
        p.wait("vector", s_mm, offmm)
        vstep(lambda v: v.tensor_copy(offs[:], ps8[:]))
        vstep(lambda v: v.tensor_copy(mycol[:], psc[:]))
        vstep(lambda v: v.tensor_copy(run[:], offs[:]))
        ohv = oh[:].rearrange("p (a e) -> p a e", e=E)
        for a in range(64):
            vstep(lambda v, a=a: v.tensor_mul(tmp8[:], run[:], ohv[:, a, :]))
            vstep(lambda v, a=a: v.tensor_reduce(
                posv[:, a:a + 1], tmp8[:], axis=AX.X, op=ALU.add))
            vstep(lambda v, a=a: v.tensor_add(run[:], run[:], ohv[:, a, :]))
        vstep(lambda v: v.tensor_scalar(keep[:], posv[:], float(CAP), None, ALU.is_lt))
        vstep(lambda v: v.tensor_mul(gk[:], gate[:], keep[:]))
        vstep(lambda v: v.tensor_scalar(f1[:], eidx[:], float(CAP), None, ALU.mult))
        vstep(lambda v: v.tensor_add(f1[:], f1[:], posv[:]))
        vstep(lambda v: v.tensor_scalar(f1[:], f1[:], float(T - 1), None, ALU.min))
        vstep(lambda v: v.tensor_copy(flat_i[:], f1[:]))
        vstep(lambda v: v.tensor_scalar(mk[:], eidx[:], mycol[:, 0:1], None, ALU.is_equal))
        vstep(lambda v: v.tensor_mul(mk[:], mk[:], keep[:]))
        vstep(lambda v: v.tensor_scalar(t2[:], posv[:], -float(CAP), None, ALU.add))
        vstep(lambda v: v.tensor_mul(t2[:], t2[:], mk[:]))
        vstep(lambda v: v.tensor_scalar(t2[:], t2[:], float(CAP), None, ALU.add))
        route_dv = vstep(lambda v: v.tensor_copy(flloc_i[:], t2[:]))
        p.wait("sync", s_dv, route_dv)
        p.on("sync", lambda sy: sy.dma_start(
            out=FLAT.rearrange("(p a) o -> p (a o)", p=128), in_=flat_i[:]))
        p.inc("sync", s_sfg, 16)
        p.on("sync", lambda sy: sy.dma_start(
            out=GK.rearrange("(p a) o -> p (a o)", p=128), in_=gk[:]))
        p.inc("sync", s_sfg, 16)

        p.wait("gpsimd", s_gp, 2)
        p.on("gpsimd", lambda g: g.dma_start(
            out=st8.ap().rearrange("(p a) e -> p a e", p=128),
            in_=zero72[:].rearrange("p (a e) -> p a e", e=E)))
        p.inc("gpsimd", s_z, 16)
        p.wait("gpsimd", s_z, 16)
        p.wait("gpsimd", s_dv, route_dv)
        tok_uses = [0, 0]
        for a in range(64):
            sl = a % 2
            if tok_uses[sl] > 0:
                p.wait("gpsimd", s_sc[sl], 16 * tok_uses[sl])
            p.on("gpsimd", lambda g, a=a, sl=sl: g.iota(
                tok8[sl][:], pattern=[[0, 8]], base=a, channel_multiplier=64))
            p.inc("gpsimd", s_gp)
            p.on("gpsimd", lambda g, a=a, sl=sl: g.indirect_dma_start(
                out=st8.ap(), out_offset=IndirectOffsetOnAxis(flloc_i[:, a:a + 1], 0),
                in_=tok8[sl][:], in_offset=None))
            tok_uses[sl] += 1
            p.inc("gpsimd", s_sc[sl], 16)
        for sl in range(2):
            p.wait("gpsimd", s_sc[sl], 16 * tok_uses[sl])

        p.free(cm_psc)
        p.free(cm_ps8)
        idxt0, cm_idxt0 = p.sb_tmp("idxt0", [128, E], I32)
        idxt1, cm_idxt1 = p.sb_tmp("idxt1", [128, E], I32)
        idxt = [idxt0, idxt1]
        xdr0, cm_xdr0 = p.sb_tmp("xdrow0", [128, DIM], BF16)
        xdr1, cm_xdr1 = p.sb_tmp("xdrow1", [128, DIM], BF16)
        xdrow = [xdr0, xdr1]
        w1s0_, cm_w1s0 = p.sb_tmp("w1s0", [128, DC * 128], BF16)
        w1s1_, cm_w1s1 = p.sb_tmp("w1s1", [128, DC * 128], BF16)
        w1s = [w1s0_, w1s1_]
        w3s0_, cm_w3s0 = p.sb_tmp("w3s0", [128, DC * 128], BF16)
        w3s1_, cm_w3s1 = p.sb_tmp("w3s1", [128, DC * 128], BF16)
        w3s = [w3s0_, w3s1_]
        slbf0_, cm_slbf0 = p.sb_tmp("slbf0", [128, TC], BF16)
        slbf1_, cm_slbf1 = p.sb_tmp("slbf1", [128, TC], BF16)
        slbf = [slbf0_, slbf1_]
        xdT, cm_xdT = p.sb_tmp("xdT", [128, DC * TC], BF16)
        ps_t0, cm_pst0 = p.ps_tmp("ps_t0", [128, 128], BF16)
        ps_t1, cm_pst1 = p.ps_tmp("ps_t1", [128, 128], BF16)
        pst = [ps_t0, ps_t1]
        ix_uses = [0, 0]
        gx_uses = [0, 0]
        for i in range(8):
            sl = i % 2
            if ix_uses[sl] > 0:
                p.wait("gpsimd", s_gx[sl], 16 * gx_uses[sl])
            p.on("gpsimd", lambda g, i=i, sl=sl: g.dma_start(
                out=idxt[sl][:], in_=st8[i * 128:(i + 1) * 128, :]))
            ix_uses[sl] += 1
            p.inc("gpsimd", s_ix[sl], 16)
            p.wait("gpsimd", s_ix[sl], 16 * ix_uses[sl])
            if i >= 2:
                p.wait("gpsimd", s_mm)
            p.on("gpsimd", lambda g, sl=sl: g.indirect_dma_start(
                out=xdrow[sl][:], out_offset=None,
                in_=X2ALL, in_offset=IndirectOffsetOnAxis(idxt[sl][:, 0:1], 0)))
            gx_uses[sl] += 1
            p.inc("gpsimd", s_gx[sl], 16)
            p.wait("tensor", s_gx[sl], 16 * gx_uses[sl])
            for dc in range(DC):
                psl2 = pst[dc % 2]
                p.on("tensor", lambda t, i=i, dc=dc, sl=sl, psl2=psl2: t.transpose(
                    psl2[:], xdrow[sl][:, dc * 128:(dc + 1) * 128], ident[:]))
                trm = p.inc("tensor", s_mm)
                p.wait("vector", s_mm, trm)
                p.on("vector", lambda v, i=i, dc=dc, psl2=psl2: v.tensor_copy(
                    xdT[:, dc * TC + i * 128: dc * TC + (i + 1) * 128], psl2[:]))
                cpm = p.inc("vector", s_dv)
                p.wait("tensor", s_dv, cpm)

        w1_uses = [0, 0]

        def h_phase(W1src, W3src, xsrc, xready_sem, xready_val, ph1s, ph3s,
                    ev_gate=None):
            gdv = {}
            for hc in range(HC):
                sl = hc % 2
                if w1_uses[sl] > 0:
                    p.wait("gpsimd", s_mm)
                p.on("gpsimd", lambda g, hc=hc, sl=sl, W1src=W1src: g.dma_start(
                    out=w1s[sl][:].rearrange("p (dc j) -> p dc j", j=128),
                    in_=W1src[:, hc * 128:(hc + 1) * 128].rearrange("(dc p) j -> p dc j", p=128)))
                p.inc("gpsimd", s_w1[sl], 16)
                p.on("gpsimd", lambda g, hc=hc, sl=sl, W3src=W3src: g.dma_start(
                    out=w3s[sl][:].rearrange("p (dc j) -> p dc j", j=128),
                    in_=W3src[:, hc * 128:(hc + 1) * 128].rearrange("(dc p) j -> p dc j", p=128)))
                p.inc("gpsimd", s_w1[sl], 16)
                w1_uses[sl] += 1
                p.wait("tensor", s_w1[sl], 32 * w1_uses[sl])
                if hc == 0:
                    p.wait("tensor", xready_sem, xready_val)
                ph1, ph3 = ph1s[sl], ph3s[sl]
                if hc == 0 and ev_gate is not None:
                    p.wait("tensor", s_ev, ev_gate)
                if hc >= 2:
                    p.wait("tensor", s_ev)
                    p.wait("tensor", s_dv)
                for half in range(2):
                    for dc in range(DC):
                        p.on("tensor", lambda t, hc=hc, dc=dc, half=half, sl=sl, ph1=ph1, xsrc=xsrc: t.matmul(
                            ph1[:, half * 512:(half + 1) * 512],
                            w1s[sl][:, dc * 128:(dc + 1) * 128],
                            xsrc[:, dc * TC + half * 512: dc * TC + (half + 1) * 512],
                            start=(dc == 0), stop=(dc == DC - 1)))
                    for dc in range(DC):
                        p.on("tensor", lambda t, hc=hc, dc=dc, half=half, sl=sl, ph3=ph3, xsrc=xsrc: t.matmul(
                            ph3[:, half * 512:(half + 1) * 512],
                            w3s[sl][:, dc * 128:(dc + 1) * 128],
                            xsrc[:, dc * TC + half * 512: dc * TC + (half + 1) * 512],
                            start=(dc == 0), stop=(dc == DC - 1)))
                hm = p.inc("tensor", s_mm)
                p.wait("scalar", s_mm, hm)
                if hc >= 2:
                    p.wait("scalar", s_dv)
                p.on("scalar", lambda s, sl=sl, ph1=ph1: s.activation(
                    slbf[sl][:], ph1[:], AF.Silu))
                sev = p.inc("scalar", s_ev)
                p.wait("vector", s_ev, sev)
                p.on("vector", lambda v, hc=hc, sl=sl, ph3=ph3: v.tensor_mul(
                    g_sb[:, hc * TC:(hc + 1) * TC], slbf[sl][:], ph3[:]))
                gdv[hc] = p.inc("vector", s_dv)
            return gdv

        p.free(cm_pst1)
        p.free(cm_pst0)
        ph1a, cm_ph1a = p.ps_tmp("ph1a", [128, 1024])
        ph1b, cm_ph1b = p.ps_tmp("ph1b", [128, 1024])
        ph3a, cm_ph3a = p.ps_tmp("ph3a", [128, 1024])
        ph3b, cm_ph3b = p.ps_tmp("ph3b", [128, 1024])
        xd_ready = p.ctr[s_dv]
        gdvE = h_phase(W1c, W3c, xdT, s_dv, xd_ready, [ph1a, ph1b], [ph3a, ph3b])
        glastE = gdvE[HC - 1]

        p.free(cm_ph3b)
        p.free(cm_ph3a)
        p.free(cm_ph1b)
        p.free(cm_ph1a)
        p.free(cm_xdT)
        w2c0_, cm_w2c0 = p.sb_tmp("w2col0", [128, HC * 512], BF16)
        w2c1_, cm_w2c1 = p.sb_tmp("w2col1", [128, HC * 512], BF16)
        w2col = [w2c0_, w2c1_]
        oes0_, cm_oes0 = p.sb_tmp("oest0", [128, 512], BF16)
        oes1_, cm_oes1 = p.sb_tmp("oest1", [128, 512], BF16)
        oest = [oes0_, oes1_]
        psO0, cm_psO0 = p.ps_tmp("psO0", [128, 512])
        psO1, cm_psO1 = p.ps_tmp("psO1", [128, 512])
        psOs = [psO0, psO1]
        w2_uses = [0, 0]
        so_uses = [0, 0]
        ev3 = []

        def w2_pass(W2src, outdst, gready, tag):
            gidx = [0]
            for on4 in range(4):
                sl = on4 % 2
                if w2_uses[sl] > 0 or on4 == 0:
                    p.wait("gpsimd", s_mm)
                p.on("gpsimd", lambda g, on4=on4, sl=sl, W2src=W2src: g.dma_start(
                    out=w2col[sl][:].rearrange("p (hc j) -> p hc j", j=512),
                    in_=W2src[:, on4 * 512:(on4 + 1) * 512].rearrange("(hc p) j -> p hc j", p=128)))
                w2_uses[sl] += 1
                p.inc("gpsimd", s_w2[sl], 16)
                p.wait("tensor", s_w2[sl], 16 * w2_uses[sl])
                if on4 == 0:
                    p.wait("tensor", s_dv, gready)
                for tc8 in range(8):
                    gi = gidx[0]
                    gidx[0] += 1
                    ps = psOs[gi % 2]
                    stsl = gi % 2
                    if gi >= 2:
                        p.wait("tensor", s_ev, ev3[-2])
                    for hc in range(HC):
                        p.on("tensor", lambda t, hc=hc, tc8=tc8, on4=on4, sl=sl, ps=ps: t.matmul(
                            ps[:],
                            g_sb[:, hc * TC + tc8 * 128: hc * TC + (tc8 + 1) * 128],
                            w2col[sl][:, hc * 512:(hc + 1) * 512],
                            start=(hc == 0), stop=(hc == HC - 1)))
                    om = p.inc("tensor", s_mm)
                    p.wait("scalar", s_mm, om)
                    if so_uses[stsl] > 0:
                        p.wait("scalar", s_so[stsl], 16 * so_uses[stsl])
                    p.on("scalar", lambda s, ps=ps, stsl=stsl: s.activation(
                        oest[stsl][:], ps[:], AF.Identity))
                    ev = p.inc("scalar", s_ev)
                    ev3.append(ev)
                    p.wait("sync", s_ev, ev)
                    p.on("sync", lambda sy, tc8=tc8, on4=on4, stsl=stsl, outdst=outdst: sy.dma_start(
                        out=outdst[tc8 * 128:(tc8 + 1) * 128, on4 * 512:(on4 + 1) * 512],
                        in_=oest[stsl][:]))
                    so_uses[stsl] += 1
                    p.inc("sync", s_so[stsl], 16)

        w2_pass(W2c, OE, glastE, "E")

        p.free(cm_psO1)
        p.free(cm_psO0)
        ph1a2, cm_ph1a2 = p.ps_tmp("ph1a2", [128, 1024])
        ph1b2, cm_ph1b2 = p.ps_tmp("ph1b2", [128, 1024])
        ph3a2, cm_ph3a2 = p.ps_tmp("ph3a2", [128, 1024])
        ph3b2, cm_ph3b2 = p.ps_tmp("ph3b2", [128, 1024])
        for dc in range(DC):
            p.on("gpsimd", lambda g, dc=dc: g.dma_start(
                out=x2tb[:, dc * TC:(dc + 1) * TC], in_=X2Ti[dc * 128:(dc + 1) * 128, :]))
            p.inc("gpsimd", s_x2, 16)
        x2_ready_val = p.ctr[s_x2]
        gdvS = h_phase(SW1, SW3, x2tb, s_x2, x2_ready_val,
                       [ph1a2, ph1b2], [ph3a2, ph3b2], ev_gate=ev3[-1])
        glastS = gdvS[HC - 1]

        sw0_, cm_sw0 = p.sb_tmp("sw2oc0", [128, HC * 128], BF16)
        sw1_, cm_sw1t = p.sb_tmp("sw2oc1", [128, HC * 128], BF16)
        sw2oc = [sw0_, sw1_]
        x2f0_, cm_x2f0 = p.sb_tmp("x2f0", [128, TC], F32)
        x2f1_, cm_x2f1 = p.sb_tmp("x2f1", [128, TC], F32)
        x2f = [x2f0_, x2f1_]
        p2s0_, cm_p2s0 = p.sb_tmp("p2st0", [128, TC], F32)
        p2s1_, cm_p2s1 = p.sb_tmp("p2st1", [128, TC], F32)
        p2st = [p2s0_, p2s1_]
        p.free(cm_ph3b2)
        p.free(cm_ph3a2)
        p.free(cm_ph1b2)
        p.free(cm_ph1a2)
        psP0, cm_psP0 = p.ps_tmp("psP0", [128, 1024])
        psP1, cm_psP1 = p.ps_tmp("psP1", [128, 1024])
        psPs = [psP0, psP1]
        sw2_uses = [0, 0]
        xf_uses = [0, 0]
        sp_uses = [0, 0]
        dvs_ms = []
        for oc in range(DC):
            sl = oc % 2
            if sw2_uses[sl] > 0:
                p.wait("gpsimd", s_mm)
            p.on("gpsimd", lambda g, oc=oc, sl=sl: g.dma_start(
                out=sw2oc[sl][:].rearrange("p (hc j) -> p hc j", j=128),
                in_=SW2[:, oc * 128:(oc + 1) * 128].rearrange("(hc p) j -> p hc j", p=128)))
            sw2_uses[sl] += 1
            p.inc("gpsimd", s_w2[sl], 16)
            if xf_uses[sl] > 0:
                p.wait("gpsimd", s_dv, dvs_ms[oc - 2])
            p.on("gpsimd", lambda g, oc=oc, sl=sl: g.dma_start(
                out=x2f[sl][:], in_=X2Ti[oc * 128:(oc + 1) * 128, :]))
            xf_uses[sl] += 1
            p.inc("gpsimd", s_xf[sl], 16)
            ps = psPs[oc % 2]
            p.wait("tensor", s_w2[sl], 16 * (w2_uses[sl] + sw2_uses[sl]))
            if oc == 0:
                p.wait("tensor", s_dv, glastS)
            if oc >= 2:
                p.wait("tensor", s_dv, dvs_ms[oc - 2])
            for half in range(2):
                for hc in range(HC):
                    p.on("tensor", lambda t, oc=oc, hc=hc, half=half, sl=sl, ps=ps: t.matmul(
                        ps[:, half * 512:(half + 1) * 512],
                        sw2oc[sl][:, hc * 128:(hc + 1) * 128],
                        g_sb[:, hc * TC + half * 512: hc * TC + (half + 1) * 512],
                        start=(hc == 0), stop=(hc == HC - 1)))
            pm = p.inc("tensor", s_mm)
            p.wait("vector", s_mm, pm)
            p.wait("vector", s_xf[sl], 16 * xf_uses[sl])
            if sp_uses[sl] > 0:
                p.wait("vector", s_sp[sl], 16 * sp_uses[sl])
            p.on("vector", lambda v, sl=sl, ps=ps: v.tensor_add(p2st[sl][:], ps[:], x2f[sl][:]))
            dv_ = p.inc("vector", s_dv)
            dvs_ms.append(dv_)
            p.wait("sync", s_dv, dv_)
            p.on("sync", lambda sy, oc=oc, sl=sl: sy.dma_start(
                out=PRE2AT[oc * 128:(oc + 1) * 128, :], in_=p2st[sl][:]))
            sp_uses[sl] += 1
            p.inc("sync", s_sp[sl], 16)

        for sl in range(2):
            p.wait("sync", s_so[sl], 16 * so_uses[sl])
            p.wait("sync", s_sp[sl], 16 * sp_uses[sl])
        p.wait("sync", s_sfg)
        p.emit(block)
    return nc


def _host_l3_inputs(inputs, core, l2res):
    x2all = np.concatenate([np.asarray(l2res[c]["X2TOK"]) for c in range(NCORE)], axis=0)
    logall = np.concatenate([np.asarray(l2res[c]["LOGITS"]) for c in range(NCORE)], axis=0)
    d = {
        "X2Ti": np.asarray(l2res[core]["X2T"]),
        "X2ALL": x2all,
        "LOGALL": logall,
        "W1c": np.ascontiguousarray(np.asarray(inputs["L0_ffn_W1"], np.float32)[core]),
        "W3c": np.ascontiguousarray(np.asarray(inputs["L0_ffn_W3"], np.float32)[core]),
        "W2c": np.ascontiguousarray(np.asarray(inputs["L0_ffn_W2"], np.float32)[core]),
        "SW1": np.asarray(inputs["L0_ffn_sw1"], np.float32),
        "SW3": np.asarray(inputs["L0_ffn_sw3"], np.float32),
        "SW2": np.asarray(inputs["L0_ffn_sw2"], np.float32),
        "MYEXP": np.full((1, 1), float(core), np.float32),
        "LT": np.triu(np.ones((128, 128), np.float32), 1),
        "IDENT": np.eye(128, dtype=np.float32),
    }
    return d



def build_l4():
    nc = bass.Bass("TRN2", target_bir_lowering=False)
    OEALL = nc.dram_tensor("OEALL", [T, DIM], BF16, kind="ExternalInput").ap()
    PRE2A = nc.dram_tensor("PRE2A", [TC, DIM], F32, kind="ExternalInput").ap()
    FLATO = nc.dram_tensor("FLATO", [128, 8], I32, kind="ExternalInput").ap()
    GKO = nc.dram_tensor("GKO", [128, 8], F32, kind="ExternalInput").ap()
    OUT = nc.dram_tensor("OUT", [TC, DIM], F32, kind="ExternalOutput").ap()

    with contextlib.ExitStack() as stack:
        p = Prog(nc, stack)
        flato = p.sb("flato", [128, 8], I32)
        gko = p.sb("gko", [128, 8], F32)
        oeg = [p.sb("oeg0", [128, DIM], BF16), p.sb("oeg1", [128, DIM], BF16)]
        pre2 = [p.sb("pre20", [128, DIM], F32), p.sb("pre21", [128, DIM], F32)]
        pfin = [p.sb("pfin0", [128, DIM], F32), p.sb("pfin1", [128, DIM], F32)]
        sqf = p.sb("sqf", [128, DIM], F32)
        srow = p.sb("srow", [128, 1], F32)
        qrow = p.sb("qrow", [128, 1], F32)
        mrow = p.sb("mrow", [128, 1], F32)
        trow = p.sb("trow", [128, 1], F32)
        rrow = p.sb("rrow", [128, 1], F32)
        stg = [p.sb("stg0", [128, DIM], F32), p.sb("stg1", [128, DIM], F32)]
        block = stack.enter_context(nc.Block())
        s_c = p.sem("s_c")
        s_g = [p.sem("s_g0"), p.sem("s_g1")]
        s_p = [p.sem("s_p0"), p.sem("s_p1")]
        s_dv = p.sem("s_dv")
        s_ev = p.sem("s_ev")
        s_st = [p.sem("s_st0"), p.sem("s_st1")]

        p.on("gpsimd", lambda g: g.dma_start(out=flato[:], in_=FLATO[:, :]))
        p.inc("gpsimd", s_c, 16)
        p.on("gpsimd", lambda g: g.dma_start(out=gko[:], in_=GKO[:, :]))
        p.inc("gpsimd", s_c, 16)
        p.wait("gpsimd", s_c)
        g_uses = [0, 0]
        p_uses = [0, 0]
        st_uses = [0, 0]
        cons_ms = {}

        def vstep(fn):
            p.on("vector", fn)
            ms = p.inc("vector", s_dv)
            p.wait("vector", s_dv, ms)
            return ms

        for tc in range(8):
            sl = tc % 2
            if tc >= 2:
                p.wait("gpsimd", s_dv, cons_ms[tc - 2])
            p.on("gpsimd", lambda g, tc=tc, sl=sl: g.indirect_dma_start(
                out=oeg[sl][:], out_offset=None,
                in_=OEALL, in_offset=IndirectOffsetOnAxis(flato[:, tc:tc + 1], 0)))
            g_uses[sl] += 1
            p.inc("gpsimd", s_g[sl], 16)
            p.on("gpsimd", lambda g, tc=tc, sl=sl: g.dma_start(
                out=pre2[sl][:], in_=PRE2A[tc * 128:(tc + 1) * 128, :]))
            p_uses[sl] += 1
            p.inc("gpsimd", s_p[sl], 16)
            p.wait("vector", s_g[sl], 16 * g_uses[sl])
            p.wait("vector", s_p[sl], 16 * p_uses[sl])
            p.on("vector", lambda v, tc=tc, sl=sl: v.scalar_tensor_tensor(
                pfin[sl][:], oeg[sl][:], gko[:, tc:tc + 1], pre2[sl][:],
                ALU.mult, ALU.add))
            cm = p.inc("vector", s_dv)
            cons_ms[tc] = cm
            p.wait("vector", s_dv, cm)
            vstep(lambda v, sl=sl: v.reduce_sum(srow[:], pfin[sl][:], axis=AX.X))
            p.wait("scalar", s_dv)
            p.on("scalar", lambda s, sl=sl: s.activation(sqf[:], pfin[sl][:], AF.Square))
            sq_ev = p.inc("scalar", s_ev)
            p.wait("vector", s_ev, sq_ev)
            vstep(lambda v: v.reduce_sum(qrow[:], sqf[:], axis=AX.X))
            vstep(lambda v: v.tensor_scalar(mrow[:], srow[:], 1.0 / DIM, None, ALU.mult))
            vstep(lambda v: v.tensor_mul(trow[:], mrow[:], mrow[:]))
            vstep(lambda v: v.tensor_scalar(qrow[:], qrow[:], 1.0 / DIM, None, ALU.mult))
            vstep(lambda v: v.tensor_sub(qrow[:], qrow[:], trow[:]))
            veps = vstep(lambda v: v.tensor_scalar(qrow[:], qrow[:], EPS, None, ALU.add))
            p.wait("scalar", s_dv, veps)
            p.on("scalar", lambda s: s.activation(trow[:], qrow[:], AF.Ln))
            ln_ev = p.inc("scalar", s_ev)
            p.wait("scalar", s_ev, ln_ev)
            p.on("scalar", lambda s: s.activation(rrow[:], trow[:], AF.Exp, scale=-0.5))
            ex_ev = p.inc("scalar", s_ev)
            p.wait("vector", s_ev, ex_ev)
            if st_uses[sl] > 0:
                p.wait("vector", s_st[sl], 16 * st_uses[sl])
            fin = vstep(lambda v, sl=sl: v.tensor_scalar(
                stg[sl][:], pfin[sl][:], mrow[:], rrow[:], ALU.subtract, ALU.mult))
            p.wait("sync", s_dv, fin)
            p.on("sync", lambda sy, tc=tc, sl=sl: sy.dma_start(
                out=OUT[tc * 128:(tc + 1) * 128, :], in_=stg[sl][:]))
            st_uses[sl] += 1
            p.inc("sync", s_st[sl], 16)

        for sl in range(2):
            p.wait("sync", s_st[sl], 16 * st_uses[sl])
        p.emit(block)
    return nc


def _host_l4_inputs(core, l3res):
    oeall = np.concatenate([np.asarray(l3res[c]["OE"]) for c in range(NCORE)], axis=0)
    flat = np.asarray(l3res[0]["FLAT"]).astype(np.int32)[:, 0]
    gk = np.asarray(l3res[0]["GK"]).astype(np.float32)[:, 0]
    fo = flat[core * TC:(core + 1) * TC].reshape(8, 128).T.copy()
    go = gk[core * TC:(core + 1) * TC].reshape(8, 128).T.copy()
    pre2a = np.ascontiguousarray(np.asarray(l3res[core]["PRE2AT"]).T)
    return {"OEALL": oeall, "PRE2A": pre2a, "FLATO": fo, "GKO": go}


def kernel(**inputs):
    return kernel_fused(**inputs)


def kernel_multilaunch(**inputs):
    nc1 = build_l1()
    res1 = run_bass_kernel_spmd(nc1, [_host_l1_inputs(inputs, c) for c in range(NCORE)],
                                core_ids=list(range(NCORE)))
    nc2 = build_l2()
    res2 = run_bass_kernel_spmd(nc2, [_host_l2_inputs(inputs, c, res1.results) for c in range(NCORE)],
                                core_ids=list(range(NCORE)))
    nc3 = build_l3()
    res3 = run_bass_kernel_spmd(nc3, [_host_l3_inputs(inputs, c, res2.results) for c in range(NCORE)],
                                core_ids=list(range(NCORE)))
    nc4 = build_l4()
    res4 = run_bass_kernel_spmd(nc4, [_host_l4_inputs(c, res3.results) for c in range(NCORE)],
                                core_ids=list(range(NCORE)))
    out = np.concatenate([np.asarray(res4.results[c]["OUT"]) for c in range(NCORE)], axis=0)
    return out.reshape(2, 4096, DIM)



def build_fused(stop_phase=99):
    nc = bass.Bass("TRN2", target_bir_lowering=False)
    xT = nc.dram_tensor("xT", [DIM, TC], F32, kind="ExternalInput").ap()
    W = {}
    for pch in "qkvo":
        W[pch] = {
            "WT": nc.dram_tensor(f"W{pch}T", [DIM, DIM], F32, kind="ExternalInput").ap(),
            "BT": nc.dram_tensor(f"B{pch}T", [DIM, RANK], F32, kind="ExternalInput").ap(),
            "AT": nc.dram_tensor(f"A{pch}T", [RANK, DIM], F32, kind="ExternalInput").ap(),
            "b": nc.dram_tensor(f"b{pch}", [128, DC], F32, kind="ExternalInput").ap(),
            "brow": nc.dram_tensor(f"b{pch}row", [1, DIM], F32, kind="ExternalInput").ap(),
        }
    Wg = nc.dram_tensor("Wg", [DIM, E], F32, kind="ExternalInput").ap()
    W1c = nc.dram_tensor("W1c", [DIM, H], F32, kind="ExternalInput").ap()
    W3c = nc.dram_tensor("W3c", [DIM, H], F32, kind="ExternalInput").ap()
    W2c = nc.dram_tensor("W2c", [H, DIM], F32, kind="ExternalInput").ap()
    SW1 = nc.dram_tensor("SW1", [DIM, H], F32, kind="ExternalInput").ap()
    SW3 = nc.dram_tensor("SW3", [DIM, H], F32, kind="ExternalInput").ap()
    SW2 = nc.dram_tensor("SW2", [H, DIM], F32, kind="ExternalInput").ap()
    MYEXP = nc.dram_tensor("MYEXP", [1, 1], F32, kind="ExternalInput").ap()
    LT = nc.dram_tensor("LT", [128, 128], F32, kind="ExternalInput").ap()
    IDENT = nc.dram_tensor("IDENT", [128, 128], F32, kind="ExternalInput").ap()
    OUT_T = nc.dram_tensor("OUT_T", [DIM, TC], F32, kind="ExternalOutput").ap()
    KDBG = nc.dram_tensor("KDBG", [DIM, TC], BF16, kind="ExternalOutput").ap()
    agk_i = (nc.dram_tensor("agk_i", [DIM, TC], BF16) if not _F1_KEXT else None)
    if _F1_KEXT:
        agk_i = KDBG
    agk_o = nc.dram_tensor("agk_o", [4 * DIM, TC], BF16)
    agv_i = nc.dram_tensor("agv_i", [TC, DIM], BF16)
    agv_o = nc.dram_tensor("agv_o", [4 * TC, DIM], BF16)
    agn_i = nc.dram_tensor("agn_i", [1, HEADS], F32)
    agn_o = nc.dram_tensor("agn_o", [4, HEADS], F32)
    agx_i = nc.dram_tensor("agx_i", [TC, DIM], BF16)
    agx_o = nc.dram_tensor("agx_o", [T, DIM], BF16)
    agl_i = nc.dram_tensor("agl_i", [TC, E], F32)
    agl_o = nc.dram_tensor("agl_o", [T, E], F32)
    ago_i = nc.dram_tensor("ago_i", [CAP, DIM], BF16)
    ago_o = nc.dram_tensor("ago_o", [T, DIM], BF16)
    x2Td = nc.dram_tensor("x2Td", [DIM, TC], F32)
    p2Td = nc.dram_tensor("p2Td", [DIM, TC], F32)
    st8 = nc.dram_tensor("st8", [1152, E], I32)
    myf8 = nc.dram_tensor("myf8", [1152, E], I32)
    myg8 = nc.dram_tensor("myg8", [1152, E], F32)
    G14 = [[0, 1, 2, 3], [4, 5, 6, 7]]
    G18 = [list(range(8))]

    with contextlib.ExitStack() as stack:
        p = Prog(nc, stack)
        ones_col = p.sb("ones_col", [128, 1], BF16)
        ones_row = p.sb("ones_row", [1, 128], BF16)
        ones_rf = p.sb("ones_rf", [1, 128], F32)
        ident = p.sb("ident", [128, 128], BF16)
        qn_sb = p.sb("qn_sb", [1, HEADS], F32)
        negb = p.sb("negb", [128, HEADS], F32)
        krow = p.sb("krow", [1, 4 * HEADS], F32)
        kmax = p.sb("kmax", [1, HEADS], F32)
        wg_sb = p.sb("wg_sb", [128, DC * E], BF16)
        lt_sb = p.sb("lt_sb", [128, 128], BF16)
        myexp_sb = p.sb("myexp_sb", [1, 1], F32)
        mycol = p.sb("mycol", [128, 1], F32)
        basecol = p.sb("basecol", [128, 1], F32)
        ones8 = p.sb("ones8", [128, E], F32)

        block = stack.enter_context(nc.Block())
        s_st1 = [p.sem(f"s_st1_{i}") for i in range(4)]
        s_x = p.sem("s_x")
        s_w = [p.sem("s_w0"), p.sem("s_w1")]
        s_lc = [p.sem("s_lc0"), p.sem("s_lc1")]
        s_mm = p.sem("s_mm")
        s_ev = p.sem("s_ev")
        s_dv = p.sem("s_dv")
        s_gp = p.sem("s_gp")
        s_kst = p.sem("s_kst")
        s_cc = p.sem("s_cc")
        s_kh = [p.sem("s_kh0"), p.sem("s_kh1")]
        s_vh = [p.sem("s_vh0"), p.sem("s_vh1")]
        s_c = p.sem("s_c")
        s_xt = [p.sem("s_xt0"), p.sem("s_xt1")]
        s_x2st = p.sem("s_x2st")
        s_sc = [p.sem("s_sc0"), p.sem("s_sc1")]
        s_ix = [p.sem("s_ix0"), p.sem("s_ix1")]
        s_gx = [p.sem("s_gx0"), p.sem("s_gx1")]
        s_w1 = [p.sem("s_w10"), p.sem("s_w11")]
        s_w2 = [p.sem("s_w20"), p.sem("s_w21")]
        s_xf = [p.sem("s_xf0"), p.sem("s_xf1")]
        s_so = [p.sem("s_so0"), p.sem("s_so1")]
        s_sp = [p.sem("s_sp0"), p.sem("s_sp1")]
        s_z = p.sem("s_z")
        s_fo = p.sem("s_fo")
        s_s = p.sem("s_s")
        s_e = p.sem("s_e")
        s_av = p.sem("s_av")
        s_xts2 = [p.sem("s_xts20"), p.sem("s_xts21")]
        s_p2l = p.sem("s_p2l")
        s_if = [p.sem("s_if0"), p.sem("s_if1")]
        s_og = [p.sem("s_og0"), p.sem("s_og1")]
        s_fst = [p.sem("s_fst0"), p.sem("s_fst1")]
        s_sc2 = [p.sem("s_sc2_0"), p.sem("s_sc2_1")]

        NSTG = 4
        cat, cm_cat = p.sb_tmp("cat", [128, HEADS * TC], BF16)
        x2b, cm_x2b = p.sb_tmp("x2b", [128, DC * TC], BF16)
        qf, cm_qf = p.sb_tmp("qf", [128, HEADS * TC], BF16)
        xb, cm_xb = p.sb_tmp("xb", [128, DC * TC], BF16)
        wt0, cm_wt0 = p.sb_tmp("wt0", [128, DC * 1024], BF16)
        bts = [None, None]
        bts[0], cm_bt0 = p.sb_tmp("bt0", [128, DC * RANK], BF16)
        bts[1], cm_bt1 = p.sb_tmp("bt1", [128, DC * RANK], BF16)
        ats = [None, None]
        ats[0], cm_at0 = p.sb_tmp("at0", [RANK, DIM], BF16)
        ats[1], cm_at1 = p.sb_tmp("at1", [RANK, DIM], BF16)
        btils = [None, None]
        btils[0], cm_bl0 = p.sb_tmp("btile0", [128, DC], F32)
        btils[1], cm_bl1 = p.sb_tmp("btile1", [128, DC], F32)
        brws = [None, None]
        brws[0], cm_br0 = p.sb_tmp("brow0", [1, DIM], BF16)
        brws[1], cm_br1 = p.sb_tmp("brow1", [1, DIM], BF16)
        xlora, cm_xlora = p.sb_tmp("xlora", [RANK, TC], BF16)
        stg1 = [None] * NSTG
        cm_stg1 = [None] * NSTG
        for i in range(NSTG):
            stg1[i], cm_stg1[i] = p.sb_tmp(f"stg1_{i}", [128, TC], BF16)
        sq1, cm_sq1 = p.sb_tmp("sq1", [128, TC], BF16)
        kn_own, cm_kn_own = p.sb_tmp("kn_own", [1, HEADS], F32)
        psA, cm_psA = p.ps_tmp("psA", [128, 1024])
        psB, cm_psB = p.ps_tmp("psB", [128, 1024])
        psxb, cm_psxb = p.ps_tmp("psxb", [RANK, TC])
        psn, cm_psn = p.ps_tmp("psn", [1, 1024])
        pss1 = [psA, psB]

        for dc in range(DC):
            p.on("gpsimd", lambda g, dc=dc: g.dma_start(
                out=xb[:, dc * TC:(dc + 1) * TC], in_=xT[dc * 128:(dc + 1) * 128, :]))
            p.inc("gpsimd", s_x, 16)
        p.on("vector", lambda v: v.memset(ones_col[:], 1.0))
        p.inc("vector", s_dv)
        p.on("vector", lambda v: v.memset(ones_row[:], 1.0))
        p.inc("vector", s_dv)
        p.on("vector", lambda v: v.memset(ones_rf[:], 1.0))
        p.inc("vector", s_dv)
        p.on("vector", lambda v: v.memset(ones8[:], 1.0))
        v_const = p.inc("vector", s_dv)
        p.on("gpsimd", lambda g: g.dma_start(out=ident[:], in_=IDENT[:, :]))
        p.inc("gpsimd", s_c, 16)
        p.on("gpsimd", lambda g: g.dma_start(out=lt_sb[:], in_=LT[:, :]))
        p.inc("gpsimd", s_c, 16)
        p.on("gpsimd", lambda g: g.dma_start(out=myexp_sb[:], in_=MYEXP[:, :]))
        p.inc("gpsimd", s_c, 16)
        p.on("gpsimd", lambda g: g.dma_start(
            out=wg_sb[:].rearrange("p (dc e) -> p dc e", e=E),
            in_=Wg.rearrange("(dc p) e -> p dc e", p=128)))
        p.inc("gpsimd", s_c, 16)

        projs1 = list(_F1_PROJS)

        wbufs = [wt0, x2b]
        wload_n = [0]

        def load_w1(pi, half):
            name = projs1[pi]
            k = pi % 2
            buf = (2 * pi + half) % 2
            if wload_n[0] >= 2:
                p.wait("gpsimd", s_mm)
            wload_n[0] += 1
            for dc in range(DC):
                p.on("gpsimd", lambda g, dc=dc, name=name, half=half, buf=buf: g.dma_start(
                    out=wbufs[buf][:, dc * 1024:(dc + 1) * 1024],
                    in_=W[name]["WT"][dc * 128:(dc + 1) * 128,
                                      half * 1024:(half + 1) * 1024]))
                p.inc("gpsimd", s_w[buf], 16)
            if half == 0:
                p.on("gpsimd", lambda g, k=k, name=name, buf=buf: g.dma_start(
                    out=bts[k][:].rearrange("p (dc r) -> p dc r", r=RANK),
                    in_=W[name]["BT"].rearrange("(dc p) r -> p dc r", p=128)))
                p.inc("gpsimd", s_w[buf], 16)
                p.on("gpsimd", lambda g, k=k, name=name, buf=buf: g.dma_start(out=ats[k][:], in_=W[name]["AT"][:, :]))
                p.inc("gpsimd", s_w[buf], 16)
                p.on("gpsimd", lambda g, k=k, name=name, buf=buf: g.dma_start(out=btils[k][:], in_=W[name]["b"][:, :]))
                p.inc("gpsimd", s_w[buf], 16)
                p.on("gpsimd", lambda g, k=k, name=name, buf=buf: g.dma_start(out=brws[k][:], in_=W[name]["brow"][:, :]))
                p.inc("gpsimd", s_w[buf], 16)
            return buf, p.ctr[s_w[buf]]

        ev1_ms = []
        stg1_uses = [0] * NSTG
        gi1 = [0]

        def proj_group(emit_mms, evac_fn, store_fn, norm_oc=None):
            gi = gi1[0]
            gi1[0] += 1
            ps = pss1[gi % 2]
            slot = gi % NSTG
            stg = stg1[slot]
            if gi >= 2:
                p.wait("tensor", s_ev, ev1_ms[gi - 2])
            emit_mms(ps)
            mm_grp = p.inc("tensor", s_mm)
            p.wait("scalar", s_mm, mm_grp)
            if store_fn is not None and stg1_uses[slot] > 0:
                p.wait("scalar", s_st1[slot], 16 * stg1_uses[slot])
            evac_fn(ps, stg)
            ev = p.inc("scalar", s_ev)
            ev1_ms.append(ev)
            if store_fn is not None and _F1_STORES:
                p.wait("sync", s_ev, ev)
                store_fn(stg)
                stg1_uses[slot] += 1
                p.inc("sync", s_st1[slot], 16)
            if norm_oc is not None:
                oc, ndst = norm_oc
                p.wait("scalar", s_ev, ev)
                p.on("scalar", lambda s, stg=stg: s.activation(sq1[:], stg[:], AF.Square))
                ev_sq = p.inc("scalar", s_ev)
                ev1_ms[-1] = ev_sq
                p.wait("tensor", s_ev, ev_sq)
                p.wait("tensor", s_dv)
                for half in range(2):
                    p.on("tensor", lambda t, half=half: t.matmul(
                        psn[:, half * 512:(half + 1) * 512], ones_col[:],
                        sq1[:, half * 512:(half + 1) * 512], start=True, stop=True))
                mm_n = p.inc("tensor", s_mm)
                p.wait("vector", s_mm, mm_n)
                p.on("vector", lambda v, oc=oc, ndst=ndst: v.tensor_reduce(
                    ndst[0:1, oc:oc + 1], psn[0:1, :], axis=AX.X, op=ALU.max))
                p.inc("vector", s_dv)
            return ev

        wq = {}
        wq[(0, 0)] = load_w1(0, 0)
        wq[(0, 1)] = load_w1(0, 1)
        for pi, name in enumerate(projs1):
            k = pi % 2
            buf, wv = wq[(pi, 0)]
            p.wait("tensor", s_w[buf], wv)
            wcur = [wbufs[buf]]
            if (pi, 1) not in wq:
                wq[(pi, 1)] = load_w1(pi, 1)
            if pi == 0:
                p.wait("tensor", s_x)

            for half in range(2):
                for dc in range(DC):
                    p.on("tensor", lambda t, dc=dc, half=half, k=k: t.matmul(
                        psxb[:, half * 512:(half + 1) * 512],
                        bts[k][:, dc * RANK:(dc + 1) * RANK],
                        xb[:, dc * TC + half * 512: dc * TC + (half + 1) * 512],
                        start=(dc == 0), stop=(dc == DC - 1)))
            lomm = p.inc("tensor", s_mm)
            p.wait("scalar", s_mm, lomm)
            p.on("scalar", lambda s: s.activation(xlora[:], psxb[:], AF.Identity, scale=SCALING))
            ev_lora = p.inc("scalar", s_ev)

            if name in ("q", "k"):
                for oc in range(DC):
                    if oc == 8:
                        buf, wv = wq[(pi, 1)]
                        p.wait("tensor", s_w[buf], wv)
                        wcur[0] = wbufs[buf]
                        nxt = (pi + 1, 0)
                        if nxt[0] < len(projs1) and nxt not in wq:
                            wq[nxt] = load_w1(*nxt)

                    def mms(ps, oc=oc, k=k, ev_lora=ev_lora):
                        if oc == 0:
                            p.wait("tensor", s_ev, ev_lora)
                        for half in range(2):
                            for dc in range(DC):
                                p.on("tensor", lambda t, oc=oc, dc=dc, half=half, ps=ps, wb=wcur[0]: t.matmul(
                                    ps[:, half * 512:(half + 1) * 512],
                                    wb[:, dc * 1024 + (oc % 8) * 128: dc * 1024 + ((oc % 8) + 1) * 128],
                                    xb[:, dc * TC + half * 512: dc * TC + (half + 1) * 512],
                                    start=(dc == 0), stop=False))
                            p.on("tensor", lambda t, oc=oc, half=half, ps=ps, k=k: t.matmul(
                                ps[:, half * 512:(half + 1) * 512],
                                ats[k][:, oc * 128:(oc + 1) * 128],
                                xlora[:, half * 512:(half + 1) * 512],
                                start=False, stop=True))

                    if name == "q":
                        def evac(ps, stg, oc=oc, k=k):
                            p.on("scalar", lambda s, ps=ps, oc=oc, k=k: s.activation(
                                qf[:, oc * TC:(oc + 1) * TC], ps[:], AF.Identity,
                                bias=btils[k][:, oc:oc + 1]))
                        proj_group(mms, evac, None,
                                   norm_oc=None)
                        if _F1_NORMS:
                            p.wait("scalar", s_ev)
                            p.on("scalar", lambda s, oc=oc: s.activation(
                                sq1[:], qf[:, oc * TC:(oc + 1) * TC], AF.Square))
                            ev_sq = p.inc("scalar", s_ev)
                            ev1_ms[-1] = ev_sq
                            p.wait("tensor", s_ev, ev_sq)
                            p.wait("tensor", s_dv)
                            for half in range(2):
                                p.on("tensor", lambda t, half=half: t.matmul(
                                    psn[:, half * 512:(half + 1) * 512], ones_col[:],
                                    sq1[:, half * 512:(half + 1) * 512], start=True, stop=True))
                            mm_n = p.inc("tensor", s_mm)
                            p.wait("vector", s_mm, mm_n)
                            p.on("vector", lambda v, oc=oc: v.tensor_reduce(
                                qn_sb[0:1, oc:oc + 1], psn[0:1, :], axis=AX.X, op=ALU.max))
                            p.inc("vector", s_dv)
                    else:
                        def evac(ps, stg, oc=oc, k=k):
                            p.on("scalar", lambda s, ps=ps, stg=stg, oc=oc, k=k: s.activation(
                                stg[:], ps[:], AF.Identity, bias=btils[k][:, oc:oc + 1]))

                        def store(stg, oc=oc):
                            p.on("sync", lambda sy, oc=oc, stg=stg: sy.dma_start(
                                out=agk_i[oc * 128:(oc + 1) * 128, :], in_=stg[:]))
                        proj_group(mms, evac, store,
                                   norm_oc=(oc, kn_own) if _F1_NORMS else None)
            else:
                for on4 in range(4):
                    if on4 == 2:
                        buf, wv = wq[(pi, 1)]
                        p.wait("tensor", s_w[buf], wv)
                        wcur[0] = wbufs[buf]
                    for tc8 in range(8):
                        def mms(ps, tc8=tc8, on4=on4, k=k, ev_lora=ev_lora):
                            if tc8 == 0 and on4 == 0:
                                p.wait("tensor", s_ev, ev_lora)
                                p.wait("tensor", s_dv, v_const)
                            for dc in range(DC):
                                p.on("tensor", lambda t, tc8=tc8, on4=on4, dc=dc, ps=ps, wb=wcur[0]: t.matmul(
                                    ps[:, 0:512],
                                    xb[:, dc * TC + tc8 * 128: dc * TC + (tc8 + 1) * 128],
                                    wb[:, dc * 1024 + (on4 % 2) * 512: dc * 1024 + ((on4 % 2) + 1) * 512],
                                    start=(dc == 0), stop=False))
                            p.on("tensor", lambda t, tc8=tc8, ps=ps, k=k: t.matmul(
                                ps[:, 0:512],
                                xlora[:, tc8 * 128:(tc8 + 1) * 128],
                                ats[k][:, on4 * 512:(on4 + 1) * 512],
                                start=False, stop=False))
                            p.on("tensor", lambda t, on4=on4, ps=ps, k=k: t.matmul(
                                ps[:, 0:512], ones_row[:],
                                brws[k][0:1, on4 * 512:(on4 + 1) * 512],
                                start=False, stop=True))

                        def evac(ps, stg):
                            p.on("scalar", lambda s, ps=ps, stg=stg: s.activation(
                                stg[:, 0:512], ps[:, 0:512], AF.Identity))

                        def store(stg, tc8=tc8, on4=on4):
                            p.on("sync", lambda sy, tc8=tc8, on4=on4, stg=stg: sy.dma_start(
                                out=agv_i[tc8 * 128:(tc8 + 1) * 128, on4 * 512:(on4 + 1) * 512],
                                in_=stg[:, 0:512]))
                        proj_group(mms, evac, store, norm_oc=None)

        p.wait("sync", s_dv)
        p.on("sync", lambda sy: sy.dma_start(out=agn_i[:, :], in_=kn_own[:]))
        p.inc("sync", s_kst, 16)

        if stop_phase <= 0:
            p.drain()
            p.free_all()
            p.emit(block)
            return nc
        p.wait("gpsimd", s_kst)
        p.on("gpsimd", lambda g: g.collective_compute(
            "AllGather", ALU.bypass, replica_groups=G14,
            ins=[agk_i.ap().opt()], outs=[agk_o.ap().opt()]))
        p.inc("gpsimd", s_cc)
        p.on("gpsimd", lambda g: g.collective_compute(
            "AllGather", ALU.bypass, replica_groups=G14,
            ins=[agv_i.ap().opt()], outs=[agv_o.ap().opt()]))
        p.inc("gpsimd", s_cc)
        p.on("gpsimd", lambda g: g.collective_compute(
            "AllGather", ALU.bypass, replica_groups=G14,
            ins=[agn_i.ap().opt()], outs=[agn_o.ap().opt()]))
        cc_kv = p.inc("gpsimd", s_cc)

        p.free(cm_psn); p.free(cm_psxb); p.free(cm_psB); p.free(cm_psA)
        p.free(cm_kn_own)
        p.free(cm_sq1)
        for i in (3, 2, 1, 0):
            p.free(cm_stg1[i])
        p.free(cm_xlora)
        p.free(cm_br1); p.free(cm_br0)
        p.free(cm_bl1); p.free(cm_bl0)
        p.free(cm_at1); p.free(cm_at0)
        p.free(cm_bt1); p.free(cm_bt0)
        p.free(cm_wt0)
        p.free(cm_xb)

        if stop_phase <= 1:
            p.drain()
            p.free_all()
            p.emit(block)
            return nc
        nrow, cm_nrow = p.sb_tmp("nrow", [1, HEADS], F32)
        r_row, cm_rrow = p.sb_tmp("r_row", [1, 512], F32)
        rt, cm_rt = p.sb_tmp("rt", [128, 512], F32)
        kh0, cm_kh0 = p.sb_tmp("kh0", [128, 4 * TC], BF16)
        kh1, cm_kh1 = p.sb_tmp("kh1", [128, 4 * TC], BF16)
        vh0, cm_vh0 = p.sb_tmp("vh0", [128, 4 * TC], BF16)
        vh1, cm_vh1 = p.sb_tmp("vh1", [128, 4 * TC], BF16)
        e0, cm_e0 = p.sb_tmp("e0", [128, 512], BF16)
        e1, cm_e1 = p.sb_tmp("e1", [128, 512], BF16)
        ps_s0, cm_ps_s0 = p.ps_tmp("ps_s0", [128, 512])
        ps_s1, cm_ps_s1 = p.ps_tmp("ps_s1", [128, 512])
        ps_o, cm_ps_o = p.ps_tmp("ps_o", [128, 512])
        ps_g, cm_ps_g = p.ps_tmp("ps_g", [1, 512])
        ps_r, cm_ps_r = p.ps_tmp("ps_r", [128, 512])
        khs, vhs, es = [kh0, kh1], [vh0, vh1], [e0, e1]

        ph1_mm = p.ctr[s_mm]
        ph1_ev = p.ctr[s_ev]
        ph1_dv = p.ctr[s_dv]
        for eng in ("gpsimd",):
            p.wait(eng, s_mm, ph1_mm)
            p.wait(eng, s_ev, ph1_ev)
            p.wait(eng, s_kst)

        p.wait("gpsimd", s_cc, cc_kv)
        p.on("gpsimd", lambda g: g.dma_start(
            out=krow[:].rearrange("o (r h) -> o r h", h=HEADS), in_=agn_o[:, :]))
        p.inc("gpsimd", s_c, 16)
        p.wait("vector", s_c)
        p.on("vector", lambda v: v.tensor_reduce(
            kmax[0:1, :], krow[:].rearrange("o (r h) -> o h r", h=HEADS),
            axis=AX.X, op=ALU.max))
        knred = p.inc("vector", s_dv)
        p.wait("vector", s_dv, knred)
        p.on("vector", lambda v: v.tensor_mul(nrow[0:1, :], qn_sb[0:1, :], kmax[0:1, :]))
        nmul = p.inc("vector", s_dv)
        p.wait("scalar", s_dv, nmul)
        p.on("scalar", lambda s: s.activation(nrow[0:1, :], nrow[0:1, :], AF.Ln))
        eln = p.inc("scalar", s_ev)
        p.wait("scalar", s_ev, eln)
        p.on("scalar", lambda s: s.activation(nrow[0:1, :], nrow[0:1, :], AF.Exp, scale=0.5))
        eex = p.inc("scalar", s_ev)
        p.wait("vector", s_ev, eex)
        p.on("vector", lambda v: v.tensor_scalar(nrow[0:1, :], nrow[0:1, :], -SCALE, None, ALU.mult))
        nneg = p.inc("vector", s_dv)
        p.wait("tensor", s_dv, nneg)
        p.on("tensor", lambda t: t.matmul(ps_r[:, 0:HEADS], ones_rf[:], nrow[0:1, :],
                                          start=True, stop=True))
        bmm = p.inc("tensor", s_mm)
        p.wait("vector", s_mm, bmm)
        p.on("vector", lambda v: v.tensor_copy(negb[:], ps_r[:, 0:HEADS]))
        negb_done = p.inc("vector", s_dv)

        kh_uses = [0, 0]
        vh_uses = [0, 0]
        s_cnt = [0]
        av_cnt = [0]
        kh_ms = {}
        vh_ms = {}
        NKC = 4 * TC // 128

        def load_head(h):
            sl = h % 2
            if h >= 2:
                p.wait("gpsimd", s_s, 64 * (h - 1))
                p.wait("gpsimd", s_av, 64 * (h - 1))
            for r in range(4):
                p.on("gpsimd", lambda g, h=h, sl=sl, r=r: g.dma_start(
                    out=khs[sl][:, r * TC:(r + 1) * TC],
                    in_=agk_o[r * DIM + h * 128: r * DIM + (h + 1) * 128, :]))
                p.inc("gpsimd", s_kh[sl], 16)
            kh_uses[sl] += 1
            kh_ms[h] = 64 * kh_uses[sl]
            for qtr in range(4):
                p.on("gpsimd", lambda g, h=h, sl=sl, qtr=qtr: g.dma_start(
                    out=vhs[sl][:].rearrange("p (c j) -> p c j", j=128)[:, qtr * 8:(qtr + 1) * 8, :],
                    in_=agv_o.ap()[:, h * 128:(h + 1) * 128].rearrange(
                        "(c p) j -> p c j", p=128)[:, qtr * 8:(qtr + 1) * 8, :]))
                p.inc("gpsimd", s_vh[sl], 16)
            vh_uses[sl] += 1
            vh_ms[h] = 64 * vh_uses[sl]

        p.wait("gpsimd", s_cc, cc_kv)
        load_head(0)
        load_head(1)
        for h in range(HEADS):
            sl = h % 2
            if h + 2 < HEADS:
                load_head(h + 2)
            p.wait("tensor", s_kh[sl], kh_ms[h])
            p.wait("tensor", s_vh[sl], vh_ms[h])
            if h == 0:
                p.wait("tensor", s_dv, negb_done)
            for qh in range(2):
                s_base = s_cnt[0]
                prev_exps = s_cnt[0]
                p.wait("tensor", s_e, prev_exps)
                p.wait("tensor", s_dv)
                qsl = slice(h * TC + qh * 512, h * TC + (qh + 1) * 512)
                for c in range(2):
                    p.on("tensor", lambda t, c=c, sl=sl, qsl=qsl: t.matmul(
                        (ps_s0 if c % 2 == 0 else ps_s1)[:, :],
                        khs[sl][:, c * 128:(c + 1) * 128],
                        qf[:, qsl], start=True, stop=True))
                    s_cnt[0] += 1
                    p.inc("tensor", s_s)
                for c in range(NKC):
                    p.wait("scalar", s_s, s_base + c + 1)
                    if c >= 2:
                        p.wait("scalar", s_av, av_cnt[0] - 1)
                    p.on("scalar", lambda s, c=c, h=h: s.activation(
                        es[c % 2][:], (ps_s0 if c % 2 == 0 else ps_s1)[:],
                        AF.Exp, bias=negb[:, h:h + 1], scale=SCALE))
                    p.inc("scalar", s_e)
                    p.wait("tensor", s_e, s_base + c + 1)
                    p.on("tensor", lambda t, c=c, sl=sl: t.matmul(
                        ps_o[:, :], vhs[sl][:, c * 128:(c + 1) * 128], es[c % 2][:],
                        start=(c == 0), stop=(c == NKC - 1)))
                    p.on("tensor", lambda t, c=c: t.matmul(
                        ps_g[:, :], ones_col[:], es[c % 2][:],
                        start=(c == 0), stop=(c == NKC - 1)))
                    av_cnt[0] += 1
                    p.inc("tensor", s_av)
                    if c + 2 < NKC:
                        p.on("tensor", lambda t, c=c, sl=sl, qsl=qsl: t.matmul(
                            (ps_s0 if c % 2 == 0 else ps_s1)[:, :],
                            khs[sl][:, (c + 2) * 128:(c + 3) * 128],
                            qf[:, qsl], start=True, stop=True))
                        s_cnt[0] += 1
                        p.inc("tensor", s_s)
                p.wait("vector", s_av, av_cnt[0])
                p.on("vector", lambda v: v.reciprocal(r_row[:], ps_g[0:1, :]))
                rc_ = p.inc("vector", s_dv)
                p.wait("tensor", s_dv, rc_)
                p.on("tensor", lambda t: t.matmul(ps_r[:, :], ones_rf[:], r_row[:],
                                                  start=True, stop=True))
                rmm = p.inc("tensor", s_mm)
                p.wait("vector", s_mm, rmm)
                p.on("vector", lambda v: v.tensor_copy(rt[:], ps_r[:, :]))
                rcp = p.inc("vector", s_dv)
                p.wait("vector", s_dv, rcp)
                p.on("vector", lambda v, qsl=qsl: v.tensor_mul(cat[:, qsl], ps_o[:], rt[:]))
                p.inc("vector", s_dv)

        att_s_total = s_cnt[0]
        att_av_total = av_cnt[0]
        att_dv_after = p.ctr[s_dv]

        p.free(cm_ps_r); p.free(cm_ps_g); p.free(cm_ps_o)
        p.free(cm_ps_s1); p.free(cm_ps_s0)
        p.free(cm_e1); p.free(cm_e0)
        p.free(cm_vh1); p.free(cm_vh0)
        p.free(cm_kh1); p.free(cm_kh0)
        p.free(cm_rt); p.free(cm_rrow); p.free(cm_nrow)
        p.free(cm_qf)
        if stop_phase <= 2:
            p.drain()
            p.free_all()
            p.emit(block)
            return nc
        pre1, cm_pre1 = p.sb_tmp("pre1", [128, DC * TC], F32)
        tmp32 = [None, None]
        tmp32[0], cm_t320 = p.sb_tmp("tmp32_0", [128, TC], F32)
        tmp32[1], cm_t321 = p.sb_tmp("tmp32_1", [128, TC], F32)
        prebf = [None, None]
        prebf[0], cm_pb0 = p.sb_tmp("prebf0", [128, TC], BF16)
        prebf[1], cm_pb1 = p.sb_tmp("prebf1", [128, TC], BF16)
        sqc = [None, None]
        sqc[0], cm_sc0 = p.sb_tmp("sqc0", [128, TC], BF16)
        sqc[1], cm_sc1 = p.sb_tmp("sqc1", [128, TC], BF16)
        xts = [None, None]
        xts[0], cm_xts0 = p.sb_tmp("xts0", [128, TC], F32)
        xts[1], cm_xts1 = p.sb_tmp("xts1", [128, TC], F32)
        wocs = [None, None]
        wocs[0], cm_woc0 = p.sb_tmp("woc0", [128, DC * 128], BF16)
        wocs[1], cm_woc1 = p.sb_tmp("woc1", [128, DC * 128], BF16)
        btlo, cm_btlo = p.sb_tmp("btlo", [128, DC * RANK], BF16)
        atlo, cm_atlo = p.sb_tmp("atlo", [RANK, DIM], BF16)
        botile, cm_boti = p.sb_tmp("botile", [128, DC], F32)
        xlo, cm_xlo = p.sb_tmp("xlo", [RANK, TC], BF16)
        mean_row, cm_mr = p.sb_tmp("mean_row", [1, TC], F32)
        q_row, cm_qr = p.sb_tmp("q_row", [1, TC], F32)
        t_row, cm_tr = p.sb_tmp("t_row", [1, TC], F32)
        rstd_row, cm_rs = p.sb_tmp("rstd_row", [1, TC], F32)
        lstage, cm_lst = p.sb_tmp("lstage", [128, 8 * E], F32)
        xtok = [None, None]
        xtok[0], cm_xk0 = p.sb_tmp("xtok0", [128, DIM], BF16)
        xtok[1], cm_xk1 = p.sb_tmp("xtok1", [128, DIM], BF16)
        psA2, cm_psA2 = p.ps_tmp("psA2", [128, 1024])
        psB2, cm_psB2 = p.ps_tmp("psB2", [128, 1024])
        psS, cm_psS = p.ps_tmp("psS", [1, 1024])
        psQ, cm_psQ = p.ps_tmp("psQ", [1, 1024])
        pss2 = [psA2, psB2]

        p.wait("gpsimd", s_s, att_s_total)
        p.wait("gpsimd", s_av, att_av_total)
        p.wait("gpsimd", s_e, att_s_total)
        p.wait("gpsimd", s_dv, att_dv_after)
        p.on("gpsimd", lambda g: g.dma_start(
            out=btlo[:].rearrange("p (dc r) -> p dc r", r=RANK),
            in_=W["o"]["BT"].rearrange("(dc p) r -> p dc r", p=128)))
        p.inc("gpsimd", s_c, 16)
        p.on("gpsimd", lambda g: g.dma_start(out=atlo[:], in_=W["o"]["AT"][:, :]))
        p.inc("gpsimd", s_c, 16)
        p.on("gpsimd", lambda g: g.dma_start(out=botile[:], in_=W["o"]["b"][:, :]))
        o_c = p.inc("gpsimd", s_c, 16)

        p.wait("tensor", s_dv, att_dv_after)
        p.wait("tensor", s_c, o_c)
        for half in range(2):
            for dc in range(DC):
                p.on("tensor", lambda t, dc=dc, half=half: t.matmul(
                    psA2[0:RANK, half * 512:(half + 1) * 512],
                    btlo[:, dc * RANK:(dc + 1) * RANK],
                    cat[:, dc * TC + half * 512: dc * TC + (half + 1) * 512],
                    start=(dc == 0), stop=(dc == DC - 1)))
        lomm = p.inc("tensor", s_mm)
        p.wait("scalar", s_mm, lomm)
        p.on("scalar", lambda s: s.activation(xlo[:], psA2[0:RANK, :], AF.Identity, scale=SCALING))
        ev_lo = p.inc("scalar", s_ev)

        xt_ms = {}
        wo_ms = {}
        xt_uses2 = [0, 0]
        wo_uses2 = [0, 0]
        dv_tt_ms = {}
        ov_ms = {}

        def load_oc2(oc):
            sl = oc % 2
            if oc >= 2:
                p.wait("gpsimd", s_dv, dv_tt_ms[oc - 2])
                p.wait("gpsimd", s_mm, ov_ms[oc - 2])
            p.on("gpsimd", lambda g, oc=oc, sl=sl: g.dma_start(
                out=xts[sl][:], in_=xT[oc * 128:(oc + 1) * 128, :]))
            xt_uses2[sl] += 1
            xt_ms[oc] = 16 * xt_uses2[sl]
            p.inc("gpsimd", s_xt[sl], 16)
            p.on("gpsimd", lambda g, oc=oc, sl=sl: g.dma_start(
                out=wocs[sl][:].rearrange("p (dc j) -> p dc j", j=128),
                in_=W["o"]["WT"][:, oc * 128:(oc + 1) * 128].rearrange("(dc p) j -> p dc j", p=128)))
            wo_uses2[sl] += 1
            wo_ms[oc] = 16 * wo_uses2[sl]
            p.inc("gpsimd", s_w[sl], 16)

        wbase = [p.ctr[s_w[0]], p.ctr[s_w[1]]]
        load_oc2(0)
        load_oc2(1)
        ev_ms2 = []
        for oc in range(DC):
            sl = oc % 2
            ps = pss2[oc % 2]
            p.wait("tensor", s_w[sl], wbase[sl] + wo_ms[oc])
            if oc >= 2:
                p.wait("tensor", s_ev, ev_ms2[oc - 2])
            if oc == 0:
                p.wait("tensor", s_ev, ev_lo)
            for half in range(2):
                for dc in range(DC):
                    p.on("tensor", lambda t, oc=oc, dc=dc, half=half, ps=ps, sl=sl: t.matmul(
                        ps[:, half * 512:(half + 1) * 512],
                        wocs[sl][:, dc * 128:(dc + 1) * 128],
                        cat[:, dc * TC + half * 512: dc * TC + (half + 1) * 512],
                        start=(dc == 0), stop=False))
                p.on("tensor", lambda t, oc=oc, half=half, ps=ps: t.matmul(
                    ps[:, half * 512:(half + 1) * 512],
                    atlo[:, oc * 128:(oc + 1) * 128],
                    xlo[:, half * 512:(half + 1) * 512],
                    start=False, stop=True))
            ov_ms[oc] = p.inc("tensor", s_mm)
            p.wait("scalar", s_mm, ov_ms[oc])
            if oc >= 2:
                p.wait("scalar", s_dv, dv_tt_ms[oc - 2])
            p.on("scalar", lambda s, oc=oc, ps=ps, sl=sl: s.activation(
                tmp32[sl][:], ps[:], AF.Identity, bias=botile[:, oc:oc + 1]))
            ev = p.inc("scalar", s_ev)
            ev_ms2.append(ev)
            p.wait("vector", s_ev, ev)
            p.wait("vector", s_xt[sl], wbase[0] * 0 + xt_ms[oc])
            p.on("vector", lambda v, oc=oc, sl=sl: v.tensor_add(
                pre1[:, oc * TC:(oc + 1) * TC], tmp32[sl][:], xts[sl][:]))
            dv_tt_ms[oc] = p.inc("vector", s_dv)
            p.wait("vector", s_dv, dv_tt_ms[oc])
            p.on("vector", lambda v, oc=oc, sl=sl: v.tensor_copy(
                prebf[sl][:], pre1[:, oc * TC:(oc + 1) * TC]))
            cpms = p.inc("vector", s_dv)
            p.wait("scalar", s_dv, cpms)
            p.on("scalar", lambda s, sl=sl: s.activation(sqc[sl][:], prebf[sl][:], AF.Square))
            sqev = p.inc("scalar", s_ev)
            p.wait("tensor", s_ev, sqev)
            for half in range(2):
                p.on("tensor", lambda t, oc=oc, half=half, sl=sl: t.matmul(
                    psS[0:1, half * 512:(half + 1) * 512], ones_col[:],
                    prebf[sl][:, half * 512:(half + 1) * 512],
                    start=(oc == 0), stop=(oc == DC - 1)))
                p.on("tensor", lambda t, oc=oc, half=half, sl=sl: t.matmul(
                    psQ[0:1, half * 512:(half + 1) * 512], ones_col[:],
                    sqc[sl][:, half * 512:(half + 1) * 512],
                    start=(oc == 0), stop=(oc == DC - 1)))
            wo_ms[("stat", oc)] = p.inc("tensor", s_mm)
            if oc + 2 < DC:
                load_oc2(oc + 2)

        last_stat = wo_ms[("stat", DC - 1)]
        p.wait("vector", s_mm, last_stat)
        p.on("vector", lambda v: v.tensor_scalar(mean_row[:], psS[0:1, :], 1.0 / DIM, None, ALU.mult))
        m1 = p.inc("vector", s_dv)
        p.on("vector", lambda v: v.tensor_scalar(q_row[:], psQ[0:1, :], 1.0 / DIM, None, ALU.mult))
        m2 = p.inc("vector", s_dv)
        p.wait("vector", s_dv, m2)
        p.on("vector", lambda v: v.tensor_mul(t_row[:], mean_row[:], mean_row[:]))
        m3 = p.inc("vector", s_dv)
        p.wait("vector", s_dv, m3)
        p.on("vector", lambda v: v.tensor_sub(q_row[:], q_row[:], t_row[:]))
        m4 = p.inc("vector", s_dv)
        p.wait("vector", s_dv, m4)
        p.on("vector", lambda v: v.tensor_scalar(q_row[:], q_row[:], EPS, None, ALU.add))
        m5 = p.inc("vector", s_dv)
        p.wait("scalar", s_dv, m5)
        p.on("scalar", lambda s: s.activation(t_row[:], q_row[:], AF.Ln))
        e1_ = p.inc("scalar", s_ev)
        p.wait("scalar", s_ev, e1_)
        p.on("scalar", lambda s: s.activation(rstd_row[:], t_row[:], AF.Exp, scale=-0.5))
        e2_ = p.inc("scalar", s_ev)
        p.wait("tensor", s_ev, e2_)
        p.wait("tensor", s_dv, m1)
        for half in range(2):
            p.on("tensor", lambda t, half=half: t.matmul(
                psA2[:, half * 512:(half + 1) * 512], ones_rf[:],
                mean_row[0:1, half * 512:(half + 1) * 512], start=True, stop=True))
            p.on("tensor", lambda t, half=half: t.matmul(
                psB2[:, half * 512:(half + 1) * 512], ones_rf[:],
                rstd_row[0:1, half * 512:(half + 1) * 512], start=True, stop=True))
        mr_ms = p.inc("tensor", s_mm)

        nrm_ms = {}
        for dc in range(DC):
            sl = dc % 2
            p.wait("vector", s_mm, mr_ms)
            p.on("vector", lambda v, dc=dc, sl=sl: v.tensor_sub(
                tmp32[sl][:], pre1[:, dc * TC:(dc + 1) * TC], psA2[:]))
            n1 = p.inc("vector", s_dv)
            p.wait("vector", s_dv, n1)
            p.on("vector", lambda v, dc=dc, sl=sl: v.tensor_mul(
                pre1[:, dc * TC:(dc + 1) * TC], tmp32[sl][:], psB2[:]))
            n2 = p.inc("vector", s_dv)
            p.wait("vector", s_dv, n2)
            p.on("vector", lambda v, dc=dc: v.tensor_copy(
                x2b[:, dc * TC:(dc + 1) * TC], pre1[:, dc * TC:(dc + 1) * TC]))
            n3 = p.inc("vector", s_dv)
            nrm_ms[dc] = n3
            p.wait("sync", s_dv, n3)
            p.on("sync", lambda sy, dc=dc: sy.dma_start(
                out=x2Td.ap()[dc * 128:(dc + 1) * 128, :], in_=pre1[:, dc * TC:(dc + 1) * TC]))
            p.inc("sync", s_x2st, 16)

        p.free(cm_psQ); p.free(cm_psS)
        psL, cm_psL = p.ps_tmp("psL", [128, E])
        p.wait("tensor", s_dv, nrm_ms[DC - 1])
        for tc8 in range(8):
            for dc in range(DC):
                p.on("tensor", lambda t, tc8=tc8, dc=dc: t.matmul(
                    psL[:, :],
                    x2b[:, dc * TC + tc8 * 128: dc * TC + (tc8 + 1) * 128],
                    wg_sb[:, dc * E:(dc + 1) * E],
                    start=(dc == 0), stop=(dc == DC - 1)))
            lg_ = p.inc("tensor", s_mm)
            p.wait("scalar", s_mm, lg_)
            p.on("scalar", lambda s, tc8=tc8: s.activation(
                lstage[:, tc8 * E:(tc8 + 1) * E], psL[:], AF.Identity))
            lev = p.inc("scalar", s_ev)
            p.wait("tensor", s_ev, lev)
        p.wait("sync", s_ev)
        p.on("sync", lambda sy: sy.dma_start(
            out=agl_i.ap().rearrange("(t8 p) e -> p t8 e", p=128),
            in_=lstage[:].rearrange("p (t8 e) -> p t8 e", e=E)))
        p.inc("sync", s_x2st, 16)

        ps_t0b, cm_pt0b = p.ps_tmp("ps_t0b", [128, 128], BF16)
        ps_t1b, cm_pt1b = p.ps_tmp("ps_t1b", [128, 128], BF16)
        pstb = [ps_t0b, ps_t1b]
        xts2_uses = [0, 0]
        for tc8 in range(8):
            sl = tc8 % 2
            if xts2_uses[sl] > 0:
                p.wait("vector", s_xts2[sl], 16 * xts2_uses[sl])
            for dc in range(DC):
                psl2 = pstb[dc % 2]
                p.on("tensor", lambda t, tc8=tc8, dc=dc, psl2=psl2: t.transpose(
                    psl2[:], x2b[:, dc * TC + tc8 * 128: dc * TC + (tc8 + 1) * 128],
                    ident[:]))
                trm = p.inc("tensor", s_mm)
                p.wait("vector", s_mm, trm)
                p.on("vector", lambda v, dc=dc, sl=sl, psl2=psl2: v.tensor_copy(
                    xtok[sl][:, dc * 128:(dc + 1) * 128], psl2[:]))
                cpm = p.inc("vector", s_dv)
                p.wait("tensor", s_dv, cpm)
            p.wait("sync", s_dv)
            p.on("sync", lambda sy, tc8=tc8, sl=sl: sy.dma_start(
                out=agx_i.ap()[tc8 * 128:(tc8 + 1) * 128, :], in_=xtok[sl][:]))
            xts2_uses[sl] += 1
            p.inc("sync", s_xts2[sl], 16)

        for slx in range(2):
            p.wait("gpsimd", s_xts2[slx], 16 * xts2_uses[slx])
        p.wait("gpsimd", s_x2st)
        p.on("gpsimd", lambda g: g.collective_compute(
            "AllGather", ALU.bypass, replica_groups=G18,
            ins=[agx_i.ap().opt()], outs=[agx_o.ap().opt()]))
        p.inc("gpsimd", s_cc)
        p.on("gpsimd", lambda g: g.collective_compute(
            "AllGather", ALU.bypass, replica_groups=G18,
            ins=[agl_i.ap().opt()], outs=[agl_o.ap().opt()]))
        cc_x = p.inc("gpsimd", s_cc)

        p.free(cm_pt1b); p.free(cm_pt0b); p.free(cm_psL)
        p.free(cm_psB2); p.free(cm_psA2)
        p.free(cm_xk1); p.free(cm_xk0)
        p.free(cm_lst)
        p.free(cm_rs); p.free(cm_tr); p.free(cm_qr); p.free(cm_mr)
        p.free(cm_xlo)
        p.free(cm_boti); p.free(cm_atlo); p.free(cm_btlo)
        p.free(cm_woc1); p.free(cm_woc0)
        p.free(cm_xts1); p.free(cm_xts0)
        p.free(cm_sc1); p.free(cm_sc0)
        p.free(cm_pb1); p.free(cm_pb0)
        p.free(cm_t321); p.free(cm_t320)
        p.free(cm_pre1)

        ph2_mm = p.ctr[s_mm]
        ph2_ev = p.ctr[s_ev]
        ph2_dv = p.ctr[s_dv]
        p.wait("gpsimd", s_mm, ph2_mm)
        p.wait("gpsimd", s_ev, ph2_ev)
        p.wait("gpsimd", s_dv, ph2_dv)
        p.wait("gpsimd", s_x2st)

        if stop_phase <= 3:
            p.drain()
            p.free_all()
            p.emit(block)
            return nc
        lg, cm_auto_lg = p.sb_tmp("lg", [128, 512], F32)
        mx, cm_auto_mx = p.sb_tmp("mx", [128, 64], F32)
        exf, cm_auto_exf = p.sb_tmp("exf", [128, 512], F32)
        esum, cm_auto_esum = p.sb_tmp("esum", [128, 64], F32)
        gate, cm_auto_gate = p.sb_tmp("gate", [128, 64], F32)
        ohge, cm_auto_ohge = p.sb_tmp("ohge", [128, 512], F32)
        iotaE, cm_auto_iotaE = p.sb_tmp("iotaE", [128, 512], F32)
        sel, cm_auto_sel = p.sb_tmp("sel", [128, 512], F32)
        eidx, cm_auto_eidx = p.sb_tmp("eidx", [128, 64], F32)
        oh, cm_auto_oh = p.sb_tmp("oh", [128, 512], F32)
        cnt, cm_auto_cnt = p.sb_tmp("cnt", [128, 8], F32)
        cntb, cm_auto_cntb = p.sb_tmp("cntb", [128, 8], BF16)
        offs, cm_auto_offs = p.sb_tmp("offs", [128, 8], F32)
        run, cm_auto_run = p.sb_tmp("run", [128, 8], F32)
        tmp8, cm_auto_tmp8 = p.sb_tmp("tmp8", [128, 8], F32)
        posv, cm_auto_posv = p.sb_tmp("posv", [128, 64], F32)
        keep, cm_auto_keep = p.sb_tmp("keep", [128, 64], F32)
        gk, cm_auto_gk = p.sb_tmp("gk", [128, 64], F32)
        f1, cm_auto_f1 = p.sb_tmp("f1", [128, 64], F32)
        flat_i, cm_auto_flat_i = p.sb_tmp("flat_i", [128, 64], I32)
        mk, cm_auto_mk = p.sb_tmp("mk", [128, 64], F32)
        t2, cm_auto_t2 = p.sb_tmp("t2", [128, 64], F32)
        flloc_i, cm_auto_flloc_i = p.sb_tmp("flloc_i", [128, 64], I32)
        tok80_, cm_auto_tok80 = p.sb_tmp("tok80", [128, 8], I32)
        tok81_, cm_auto_tok81 = p.sb_tmp("tok81", [128, 8], I32)
        tok8 = [tok80_, tok81_]
        tloc, cm_auto_tloc = p.sb_tmp("tloc", [128, 64], F32)
        own, cm_auto_own = p.sb_tmp("own", [128, 64], F32)
        dest_i, cm_auto_dest_i = p.sb_tmp("dest_i", [128, 64], I32)
        rep80_, cm_auto_rep80 = p.sb_tmp("rep80", [128, 8], F32)
        rep81_, cm_auto_rep81 = p.sb_tmp("rep81", [128, 8], F32)
        rep8 = [rep80_, rep81_]
        rep8i0_, cm_auto_rep8i0 = p.sb_tmp("rep8i0", [128, 8], I32)
        rep8i1_, cm_auto_rep8i1 = p.sb_tmp("rep8i1", [128, 8], I32)
        rep8i = [rep8i0_, rep8i1_]
        g_sb, cm_g_sb = p.sb_tmp("g_sb", [128, HC * TC], BF16)

        p.wait("gpsimd", s_cc, cc_x)
        p.on("gpsimd", lambda g: g.dma_start(
            out=lg[:].rearrange("p (a e) -> p a e", e=E),
            in_=agl_o.ap().rearrange("(p a) e -> p a e", p=128)))
        p.inc("gpsimd", s_c, 16)
        p.on("gpsimd", lambda g: g.iota(iotaE[:].rearrange("p (a e) -> p a e", e=E),
                                        pattern=[[0, 64], [1, 8]], base=0,
                                        channel_multiplier=0,
                                        allow_small_or_imprecise_dtypes=True))
        p.inc("gpsimd", s_gp)
        p.on("gpsimd", lambda g: g.iota(tloc[:], pattern=[[1, 64]], base=0,
                                        channel_multiplier=64,
                                        allow_small_or_imprecise_dtypes=True))
        p.inc("gpsimd", s_gp)

        def vstep(fn):
            p.on("vector", fn)
            ms = p.inc("vector", s_dv)
            p.wait("vector", s_dv, ms)
            return ms

        p.wait("vector", s_c)
        p.wait("vector", s_gp, 2)
        vstep(lambda v: v.tensor_reduce(
            mx[:], lg[:].rearrange("p (a e) -> p a e", e=E), axis=AX.X, op=ALU.max))
        for e_ in range(E):
            p.on("vector", lambda v, e_=e_: v.tensor_sub(
                exf[:].rearrange("p (a e) -> p a e", e=E)[:, :, e_],
                lg[:].rearrange("p (a e) -> p a e", e=E)[:, :, e_], mx[:]))
            p.inc("vector", s_dv)
        dsub = p.wait("vector", s_dv)
        p.wait("scalar", s_dv, dsub)
        p.on("scalar", lambda s: s.activation(exf[:], exf[:], AF.Exp))
        eexp = p.inc("scalar", s_ev)
        p.wait("vector", s_ev, eexp)
        vstep(lambda v: v.tensor_reduce(
            esum[:], exf[:].rearrange("p (a e) -> p a e", e=E), axis=AX.X, op=ALU.add))
        vstep(lambda v: v.reciprocal(gate[:], esum[:]))
        for e_ in range(E):
            p.on("vector", lambda v, e_=e_: v.tensor_tensor(
                ohge[:].rearrange("p (a e) -> p a e", e=E)[:, :, e_],
                lg[:].rearrange("p (a e) -> p a e", e=E)[:, :, e_], mx[:], ALU.is_ge))
            p.inc("vector", s_dv)
        p.wait("vector", s_dv)
        vstep(lambda v: v.tensor_scalar(sel[:], iotaE[:], -8.0, None, ALU.add))
        vstep(lambda v: v.tensor_mul(sel[:], sel[:], ohge[:]))
        vstep(lambda v: v.tensor_scalar(sel[:], sel[:], 8.0, None, ALU.add))
        vstep(lambda v: v.tensor_reduce(
            eidx[:], sel[:].rearrange("p (a e) -> p a e", e=E), axis=AX.X, op=ALU.min))
        for e_ in range(E):
            p.on("vector", lambda v, e_=e_: v.tensor_scalar(
                oh[:].rearrange("p (a e) -> p a e", e=E)[:, :, e_],
                eidx[:], float(e_), None, ALU.is_equal))
            p.inc("vector", s_dv)
        p.wait("vector", s_dv)
        vstep(lambda v: v.tensor_reduce(
            cnt[:], oh[:].rearrange("p (a e) -> p e a", e=E), axis=AX.X, op=ALU.add))
        vstep(lambda v: v.tensor_copy(cntb[:], cnt[:]))
        cnt_ms = p.ctr[s_dv]
        ps8, cm_ps8 = p.ps_tmp("ps8", [128, 8])
        psc, cm_psc = p.ps_tmp("psc", [128, 1])
        p.wait("tensor", s_dv, cnt_ms)
        p.on("tensor", lambda t: t.matmul(ps8[:], lt_sb[:], cntb[:], start=True, stop=True))
        p.inc("tensor", s_mm)
        p.on("tensor", lambda t: t.matmul(psc[:], ones_rf[:], myexp_sb[:], start=True, stop=True))
        offmm = p.inc("tensor", s_mm)
        p.wait("vector", s_mm, offmm)
        vstep(lambda v: v.tensor_copy(offs[:], ps8[:]))
        vstep(lambda v: v.tensor_copy(mycol[:], psc[:]))
        vstep(lambda v: v.tensor_scalar(basecol[:], mycol[:], float(CAP), None, ALU.mult))
        vstep(lambda v: v.tensor_copy(run[:], offs[:]))
        ohv = oh[:].rearrange("p (a e) -> p a e", e=E)
        for a in range(64):
            vstep(lambda v, a=a: v.tensor_mul(tmp8[:], run[:], ohv[:, a, :]))
            vstep(lambda v, a=a: v.tensor_reduce(
                posv[:, a:a + 1], tmp8[:], axis=AX.X, op=ALU.add))
            vstep(lambda v, a=a: v.tensor_add(run[:], run[:], ohv[:, a, :]))
        vstep(lambda v: v.tensor_scalar(keep[:], posv[:], float(CAP), None, ALU.is_lt))
        vstep(lambda v: v.tensor_mul(gk[:], gate[:], keep[:]))
        vstep(lambda v: v.tensor_scalar(f1[:], eidx[:], float(CAP), None, ALU.mult))
        vstep(lambda v: v.tensor_add(f1[:], f1[:], posv[:]))
        vstep(lambda v: v.tensor_scalar(f1[:], f1[:], float(T - 1), None, ALU.min))
        vstep(lambda v: v.tensor_copy(flat_i[:], f1[:]))
        vstep(lambda v: v.tensor_scalar(mk[:], eidx[:], mycol[:, 0:1], None, ALU.is_equal))
        vstep(lambda v: v.tensor_mul(mk[:], mk[:], keep[:]))
        vstep(lambda v: v.tensor_scalar(t2[:], posv[:], -float(CAP), None, ALU.add))
        vstep(lambda v: v.tensor_mul(t2[:], t2[:], mk[:]))
        vstep(lambda v: v.tensor_scalar(t2[:], t2[:], float(CAP), None, ALU.add))
        vstep(lambda v: v.tensor_copy(flloc_i[:], t2[:]))
        vstep(lambda v: v.tensor_scalar(tloc[:], tloc[:], basecol[:, 0:1], None, ALU.subtract))
        vstep(lambda v: v.tensor_scalar(own[:], tloc[:], -0.5, None, ALU.is_gt))
        vstep(lambda v: v.tensor_scalar(t2[:], tloc[:], float(TC) - 0.5, None, ALU.is_lt))
        vstep(lambda v: v.tensor_mul(own[:], own[:], t2[:]))
        vstep(lambda v: v.tensor_scalar(t2[:], tloc[:], -float(TC), None, ALU.add))
        vstep(lambda v: v.tensor_mul(t2[:], t2[:], own[:]))
        vstep(lambda v: v.tensor_scalar(t2[:], t2[:], float(TC), None, ALU.add))
        route_dv = vstep(lambda v: v.tensor_copy(dest_i[:], t2[:]))

        zero72, cm_z72 = p.sb_tmp("zero72", [128, 72], I32)
        p.on("gpsimd", lambda g: g.memset(zero72[:], 0))
        zm = p.inc("gpsimd", s_gp)
        p.wait("gpsimd", s_gp, zm)
        p.on("gpsimd", lambda g: g.dma_start(
            out=st8.ap().rearrange("(p a) e -> p a e", p=128),
            in_=zero72[:].rearrange("p (a e) -> p a e", e=E)))
        p.inc("gpsimd", s_z, 16)
        p.wait("gpsimd", s_z, 16)
        p.wait("gpsimd", s_dv, route_dv)
        tok_uses = [0, 0]
        for a in range(64):
            sl = a % 2
            if tok_uses[sl] > 0:
                p.wait("gpsimd", s_sc[sl], 16 * tok_uses[sl])
            p.on("gpsimd", lambda g, a=a, sl=sl: g.iota(
                tok8[sl][:], pattern=[[0, 8]], base=a, channel_multiplier=64))
            p.inc("gpsimd", s_gp)
            p.on("gpsimd", lambda g, a=a, sl=sl: g.indirect_dma_start(
                out=st8.ap(), out_offset=IndirectOffsetOnAxis(flloc_i[:, a:a + 1], 0),
                in_=tok8[sl][:], in_offset=None))
            p.inc("gpsimd", s_sc[sl], 16)
            tok_uses[sl] += 1
        for sl in range(2):
            p.wait("gpsimd", s_sc[sl], 16 * tok_uses[sl])

        def emit_fg_scatters():
            rep_uses = [0, 0]
            for a in range(64):
                sl = a % 2
                if rep_uses[sl] > 0:
                    p.wait("vector", s_sc2[sl], 32 * rep_uses[sl])
                p.on("vector", lambda v, a=a, sl=sl: v.tensor_scalar(
                    rep8[sl][:], ones8[:], gk[:, a:a + 1], None, ALU.mult))
                r1 = p.inc("vector", s_dv)
                p.wait("vector", s_dv, r1)
                p.on("vector", lambda v, a=a, sl=sl: v.tensor_scalar(
                    rep8i[sl][:], ones8[:], f1[:, a:a + 1], None, ALU.mult))
                r2 = p.inc("vector", s_dv)
                p.wait("gpsimd", s_dv, r2)
                p.on("gpsimd", lambda g, a=a, sl=sl: g.indirect_dma_start(
                    out=myg8.ap(), out_offset=IndirectOffsetOnAxis(dest_i[:, a:a + 1], 0),
                    in_=rep8[sl][:], in_offset=None))
                p.inc("gpsimd", s_sc2[sl], 16)
                p.on("gpsimd", lambda g, a=a, sl=sl: g.indirect_dma_start(
                    out=myf8.ap(), out_offset=IndirectOffsetOnAxis(dest_i[:, a:a + 1], 0),
                    in_=rep8i[sl][:], in_offset=None))
                p.inc("gpsimd", s_sc2[sl], 16)
                rep_uses[sl] += 1
            return [32 * rep_uses[0], 32 * rep_uses[1]]

        if stop_phase <= 4:
            p.drain()
            p.free_all()
            p.emit(block)
            return nc
        idxt0, cm_idxt0 = p.sb_tmp("idxt0", [128, E], I32)
        idxt1, cm_idxt1 = p.sb_tmp("idxt1", [128, E], I32)
        idxt = [idxt0, idxt1]
        xdr0, cm_xdr0 = p.sb_tmp("xdrow0", [128, DIM], BF16)
        xdr1, cm_xdr1 = p.sb_tmp("xdrow1", [128, DIM], BF16)
        xdrow = [xdr0, xdr1]
        w1s0_, cm_w1s0 = p.sb_tmp("w1s0", [128, DC * 128], BF16)
        w1s1_, cm_w1s1 = p.sb_tmp("w1s1", [128, DC * 128], BF16)
        w1s = [w1s0_, w1s1_]
        w3s0_, cm_w3s0 = p.sb_tmp("w3s0", [128, DC * 128], BF16)
        w3s1_, cm_w3s1 = p.sb_tmp("w3s1", [128, DC * 128], BF16)
        w3s = [w3s0_, w3s1_]
        slbf0_, cm_slbf0 = p.sb_tmp("slbf0", [128, TC], BF16)
        slbf1_, cm_slbf1 = p.sb_tmp("slbf1", [128, TC], BF16)
        slbf = [slbf0_, slbf1_]
        xdT, cm_xdT = p.sb_tmp("xdT", [128, DC * TC], BF16)
        p.free(cm_psc); p.free(cm_ps8)
        ps_t0c, cm_pt0c = p.ps_tmp("ps_t0c", [128, 128], BF16)
        ps_t1c, cm_pt1c = p.ps_tmp("ps_t1c", [128, 128], BF16)
        pstc = [ps_t0c, ps_t1c]
        ix_uses = [0, 0]
        gx_uses = [0, 0]
        for i in range(8):
            sl = i % 2
            if ix_uses[sl] > 0:
                p.wait("gpsimd", s_gx[sl], 16 * gx_uses[sl])
            p.on("gpsimd", lambda g, i=i, sl=sl: g.dma_start(
                out=idxt[sl][:], in_=st8[i * 128:(i + 1) * 128, :]))
            ix_uses[sl] += 1
            p.inc("gpsimd", s_ix[sl], 16)
            p.wait("gpsimd", s_ix[sl], 16 * ix_uses[sl])
            if i >= 2:
                p.wait("gpsimd", s_mm)
            p.on("gpsimd", lambda g, sl=sl: g.indirect_dma_start(
                out=xdrow[sl][:], out_offset=None,
                in_=agx_o.ap(), in_offset=IndirectOffsetOnAxis(idxt[sl][:, 0:1], 0)))
            gx_uses[sl] += 1
            p.inc("gpsimd", s_gx[sl], 16)
            p.wait("tensor", s_gx[sl], 16 * gx_uses[sl])
            for dc in range(DC):
                psl2 = pstc[dc % 2]
                p.on("tensor", lambda t, i=i, dc=dc, sl=sl, psl2=psl2: t.transpose(
                    psl2[:], xdrow[sl][:, dc * 128:(dc + 1) * 128], ident[:]))
                trm = p.inc("tensor", s_mm)
                p.wait("vector", s_mm, trm)
                p.on("vector", lambda v, i=i, dc=dc, psl2=psl2: v.tensor_copy(
                    xdT[:, dc * TC + i * 128: dc * TC + (i + 1) * 128], psl2[:]))
                cpm = p.inc("vector", s_dv)
                p.wait("tensor", s_dv, cpm)

        w1_uses = [0, 0]

        def h_phase(W1src, W3src, xsrc, xready_sem, xready_val, ph1s, ph3s,
                    ev_gate=None):
            gdv = {}
            for hc in range(HC):
                sl = hc % 2
                if w1_uses[sl] > 0:
                    p.wait("gpsimd", s_mm)
                p.on("gpsimd", lambda g, hc=hc, sl=sl, W1src=W1src: g.dma_start(
                    out=w1s[sl][:].rearrange("p (dc j) -> p dc j", j=128),
                    in_=W1src[:, hc * 128:(hc + 1) * 128].rearrange("(dc p) j -> p dc j", p=128)))
                p.inc("gpsimd", s_w1[sl], 16)
                p.on("gpsimd", lambda g, hc=hc, sl=sl, W3src=W3src: g.dma_start(
                    out=w3s[sl][:].rearrange("p (dc j) -> p dc j", j=128),
                    in_=W3src[:, hc * 128:(hc + 1) * 128].rearrange("(dc p) j -> p dc j", p=128)))
                p.inc("gpsimd", s_w1[sl], 16)
                w1_uses[sl] += 1
                p.wait("tensor", s_w1[sl], 32 * w1_uses[sl])
                if hc == 0:
                    p.wait("tensor", xready_sem, xready_val)
                ph1, ph3 = ph1s[sl], ph3s[sl]
                if hc == 0 and ev_gate is not None:
                    p.wait("tensor", s_ev, ev_gate)
                if hc >= 2:
                    p.wait("tensor", s_ev)
                    p.wait("tensor", s_dv)
                for half in range(2):
                    for dc in range(DC):
                        p.on("tensor", lambda t, dc=dc, half=half, sl=sl, ph1=ph1, xsrc=xsrc: t.matmul(
                            ph1[:, half * 512:(half + 1) * 512],
                            w1s[sl][:, dc * 128:(dc + 1) * 128],
                            xsrc[:, dc * TC + half * 512: dc * TC + (half + 1) * 512],
                            start=(dc == 0), stop=(dc == DC - 1)))
                    for dc in range(DC):
                        p.on("tensor", lambda t, dc=dc, half=half, sl=sl, ph3=ph3, xsrc=xsrc: t.matmul(
                            ph3[:, half * 512:(half + 1) * 512],
                            w3s[sl][:, dc * 128:(dc + 1) * 128],
                            xsrc[:, dc * TC + half * 512: dc * TC + (half + 1) * 512],
                            start=(dc == 0), stop=(dc == DC - 1)))
                hm = p.inc("tensor", s_mm)
                p.wait("scalar", s_mm, hm)
                if hc >= 2:
                    p.wait("scalar", s_dv)
                p.on("scalar", lambda s, sl=sl, ph1=ph1: s.activation(
                    slbf[sl][:], ph1[:], AF.Silu))
                sev = p.inc("scalar", s_ev)
                p.wait("vector", s_ev, sev)
                p.on("vector", lambda v, hc=hc, sl=sl, ph3=ph3: v.tensor_mul(
                    g_sb[:, hc * TC:(hc + 1) * TC], slbf[sl][:], ph3[:]))
                gdv[hc] = p.inc("vector", s_dv)
            return gdv

        p.free(cm_pt1c); p.free(cm_pt0c)
        ph1a, cm_ph1a = p.ps_tmp("ph1a", [128, 1024])
        ph1b, cm_ph1b = p.ps_tmp("ph1b", [128, 1024])
        ph3a, cm_ph3a = p.ps_tmp("ph3a", [128, 1024])
        ph3b, cm_ph3b = p.ps_tmp("ph3b", [128, 1024])
        xd_ready = p.ctr[s_dv]
        gdvE = h_phase(W1c, W3c, xdT, s_dv, xd_ready, [ph1a, ph1b], [ph3a, ph3b])
        glastE = gdvE[HC - 1]

        p.free(cm_ph3b); p.free(cm_ph3a); p.free(cm_ph1b); p.free(cm_ph1a)
        p.free(cm_xdT)
        w2c0_, cm_w2c0 = p.sb_tmp("w2col0", [128, HC * 512], BF16)
        w2c1_, cm_w2c1 = p.sb_tmp("w2col1", [128, HC * 512], BF16)
        w2col = [w2c0_, w2c1_]
        oes0_, cm_oes0 = p.sb_tmp("oest0", [128, 512], BF16)
        oes1_, cm_oes1 = p.sb_tmp("oest1", [128, 512], BF16)
        oest = [oes0_, oes1_]
        psO0, cm_psO0 = p.ps_tmp("psO0", [128, 512])
        psO1, cm_psO1 = p.ps_tmp("psO1", [128, 512])
        psOs = [psO0, psO1]
        w2_uses = [0, 0]
        so_uses = [0, 0]
        ev3 = []
        gidx = [0]
        for on4 in range(4):
            sl = on4 % 2
            if w2_uses[sl] > 0 or on4 == 0:
                p.wait("gpsimd", s_mm)
            p.on("gpsimd", lambda g, on4=on4, sl=sl: g.dma_start(
                out=w2col[sl][:].rearrange("p (hc j) -> p hc j", j=512),
                in_=W2c[:, on4 * 512:(on4 + 1) * 512].rearrange("(hc p) j -> p hc j", p=128)))
            w2_uses[sl] += 1
            p.inc("gpsimd", s_w2[sl], 16)
            p.wait("tensor", s_w2[sl], 16 * w2_uses[sl])
            if on4 == 0:
                p.wait("tensor", s_dv, glastE)
            for tc8 in range(8):
                gi = gidx[0]
                gidx[0] += 1
                ps = psOs[gi % 2]
                stsl = gi % 2
                if gi >= 2:
                    p.wait("tensor", s_ev, ev3[-2])
                for hc in range(HC):
                    p.on("tensor", lambda t, hc=hc, tc8=tc8, sl=sl, ps=ps: t.matmul(
                        ps[:],
                        g_sb[:, hc * TC + tc8 * 128: hc * TC + (tc8 + 1) * 128],
                        w2col[sl][:, hc * 512:(hc + 1) * 512],
                        start=(hc == 0), stop=(hc == HC - 1)))
                om = p.inc("tensor", s_mm)
                p.wait("scalar", s_mm, om)
                if so_uses[stsl] > 0:
                    p.wait("scalar", s_so[stsl], 16 * so_uses[stsl])
                p.on("scalar", lambda s, ps=ps, stsl=stsl: s.activation(
                    oest[stsl][:], ps[:], AF.Identity))
                ev = p.inc("scalar", s_ev)
                ev3.append(ev)
                p.wait("sync", s_ev, ev)
                p.on("sync", lambda sy, tc8=tc8, on4=on4, stsl=stsl: sy.dma_start(
                    out=ago_i.ap()[tc8 * 128:(tc8 + 1) * 128, on4 * 512:(on4 + 1) * 512],
                    in_=oest[stsl][:]))
                so_uses[stsl] += 1
                p.inc("sync", s_so[stsl], 16)

        fg_totals = emit_fg_scatters()

        for sl in range(2):
            p.wait("gpsimd", s_so[sl], 16 * so_uses[sl])
        p.on("gpsimd", lambda g: g.collective_compute(
            "AllGather", ALU.bypass, replica_groups=G18,
            ins=[ago_i.ap().opt()], outs=[ago_o.ap().opt()]))
        cc_oe = p.inc("gpsimd", s_cc)

        p.free(cm_psO1); p.free(cm_psO0)
        if stop_phase <= 5:
            p.drain()
            p.free_all()
            p.emit(block)
            return nc
        ph1a2, cm_ph1a2 = p.ps_tmp("ph1a2", [128, 1024])
        ph1b2, cm_ph1b2 = p.ps_tmp("ph1b2", [128, 1024])
        ph3a2, cm_ph3a2 = p.ps_tmp("ph3a2", [128, 1024])
        ph3b2, cm_ph3b2 = p.ps_tmp("ph3b2", [128, 1024])
        gdvS = h_phase(SW1, SW3, x2b, s_dv, 0,
                       [ph1a2, ph1b2], [ph3a2, ph3b2], ev_gate=ev3[-1])
        glastS = gdvS[HC - 1]

        p.free(cm_oes1); p.free(cm_oes0)
        p.free(cm_w2c1); p.free(cm_w2c0)
        p.wait("gpsimd", s_mm)
        p.wait("gpsimd", s_ev)
        sw0_, cm_sw0 = p.sb_tmp("sw2oc0", [128, HC * 128], BF16)
        sw1_, cm_sw1t = p.sb_tmp("sw2oc1", [128, HC * 128], BF16)
        sw2oc = [sw0_, sw1_]
        x2f0_, cm_x2f0 = p.sb_tmp("x2f0", [128, TC], F32)
        x2f1_, cm_x2f1 = p.sb_tmp("x2f1", [128, TC], F32)
        x2f = [x2f0_, x2f1_]
        p2s0_, cm_p2s0 = p.sb_tmp("p2st0", [128, TC], F32)
        p2s1_, cm_p2s1 = p.sb_tmp("p2st1", [128, TC], F32)
        p2st = [p2s0_, p2s1_]
        p.free(cm_ph3b2); p.free(cm_ph3a2); p.free(cm_ph1b2); p.free(cm_ph1a2)
        psP0, cm_psP0 = p.ps_tmp("psP0", [128, 1024])
        psP1, cm_psP1 = p.ps_tmp("psP1", [128, 1024])
        psPs = [psP0, psP1]
        sw2_uses = [0, 0]
        xf_uses = [0, 0]
        sp_uses = [0, 0]
        dvs_ms = []
        for oc in range(DC):
            sl = oc % 2
            if sw2_uses[sl] > 0:
                p.wait("gpsimd", s_mm)
            p.on("gpsimd", lambda g, oc=oc, sl=sl: g.dma_start(
                out=sw2oc[sl][:].rearrange("p (hc j) -> p hc j", j=128),
                in_=SW2[:, oc * 128:(oc + 1) * 128].rearrange("(hc p) j -> p hc j", p=128)))
            sw2_uses[sl] += 1
            p.inc("gpsimd", s_w2[sl], 16)
            if xf_uses[sl] > 0:
                p.wait("gpsimd", s_dv, dvs_ms[oc - 2])
            p.on("gpsimd", lambda g, oc=oc, sl=sl: g.dma_start(
                out=x2f[sl][:], in_=x2Td.ap()[oc * 128:(oc + 1) * 128, :]))
            xf_uses[sl] += 1
            p.inc("gpsimd", s_xf[sl], 16)
            ps = psPs[oc % 2]
            p.wait("tensor", s_w2[sl], 16 * (w2_uses[sl] + sw2_uses[sl]))
            if oc == 0:
                p.wait("tensor", s_dv, glastS)
            if oc >= 2:
                p.wait("tensor", s_dv, dvs_ms[oc - 2])
            for half in range(2):
                for hc in range(HC):
                    p.on("tensor", lambda t, oc=oc, hc=hc, half=half, sl=sl, ps=ps: t.matmul(
                        ps[:, half * 512:(half + 1) * 512],
                        sw2oc[sl][:, hc * 128:(hc + 1) * 128],
                        g_sb[:, hc * TC + half * 512: hc * TC + (half + 1) * 512],
                        start=(hc == 0), stop=(hc == HC - 1)))
            pm = p.inc("tensor", s_mm)
            p.wait("vector", s_mm, pm)
            p.wait("vector", s_xf[sl], 16 * xf_uses[sl])
            if sp_uses[sl] > 0:
                p.wait("vector", s_sp[sl], 16 * sp_uses[sl])
            p.on("vector", lambda v, sl=sl, ps=ps: v.tensor_add(p2st[sl][:], ps[:], x2f[sl][:]))
            dv_ = p.inc("vector", s_dv)
            dvs_ms.append(dv_)
            p.wait("sync", s_dv, dv_)
            p.on("sync", lambda sy, oc=oc, sl=sl: sy.dma_start(
                out=p2Td.ap()[oc * 128:(oc + 1) * 128, :], in_=p2st[sl][:]))
            sp_uses[sl] += 1
            p.inc("sync", s_sp[sl], 16)

        p.free(cm_psP1); p.free(cm_psP0)
        if stop_phase <= 6:
            p.drain()
            p.free_all()
            p.emit(block)
            return nc
        ph3_mm = p.ctr[s_mm]
        ph3_ev = p.ctr[s_ev]
        ph3_dv = p.ctr[s_dv]
        for cm in (cm_p2s1, cm_p2s0, cm_x2f1, cm_x2f0, cm_sw1t, cm_sw0,
                   cm_slbf1, cm_slbf0, cm_w3s1, cm_w3s0, cm_w1s1, cm_w1s0,
                   cm_xdr1, cm_xdr0, cm_idxt1, cm_idxt0, cm_z72, cm_g_sb):
            p.free(cm)
        p.wait("gpsimd", s_mm, ph3_mm)
        p.wait("gpsimd", s_ev, ph3_ev)
        p.wait("gpsimd", s_dv, ph3_dv)
        for sl in range(2):
            p.wait("gpsimd", s_sp[sl], 16 * sp_uses[sl])
            p.wait("gpsimd", s_so[sl], 16 * so_uses[sl])
        idxf0_, _c1 = p.sb_tmp("idxf0", [128, E], I32)
        idxf1_, _c2 = p.sb_tmp("idxf1", [128, E], I32)
        idxf = [idxf0_, idxf1_]
        gkt0_, _c3 = p.sb_tmp("gkt0", [128, E], F32)
        gkt1_, _c4 = p.sb_tmp("gkt1", [128, E], F32)
        gkt = [gkt0_, gkt1_]
        oeg0_, _c5 = p.sb_tmp("oeg0", [128, DIM], BF16)
        oeg1_, _c6 = p.sb_tmp("oeg1", [128, DIM], BF16)
        oeg = [oeg0_, oeg1_]
        oegs0_, _c7 = p.sb_tmp("oegs0", [128, DIM], BF16)
        oegs1_, _c8 = p.sb_tmp("oegs1", [128, DIM], BF16)
        oegs = [oegs0_, oegs1_]
        pf20_, _c9 = p.sb_tmp("pf20", [128, TC], BF16)
        pf21_, _c10 = p.sb_tmp("pf21", [128, TC], BF16)
        pf2 = [pf20_, pf21_]
        sq20_, _c11 = p.sb_tmp("sq20", [128, TC], BF16)
        sq21_, _c12 = p.sb_tmp("sq21", [128, TC], BF16)
        sq2 = [sq20_, sq21_]
        fstg0_, _c13 = p.sb_tmp("fstg0", [128, TC], F32)
        fstg1_, _c14 = p.sb_tmp("fstg1", [128, TC], F32)
        fstg = [fstg0_, fstg1_]
        m2_row, _c15 = p.sb_tmp("m2_row", [1, TC], F32)
        q2_row, _c16 = p.sb_tmp("q2_row", [1, TC], F32)
        t2_row, _c17 = p.sb_tmp("t2_row", [1, TC], F32)
        r2_row, _c18 = p.sb_tmp("r2_row", [1, TC], F32)
        pre2F, cm_pre2F = p.sb_tmp("pre2F", [128, DC * TC], F32)
        ps_t0d, cm_pt0d = p.ps_tmp("ps_t0d", [128, 128], BF16)
        ps_t1d, cm_pt1d = p.ps_tmp("ps_t1d", [128, 128], BF16)
        pstd = [ps_t0d, ps_t1d]
        psS2, cm_psS2 = p.ps_tmp("psS2", [1, 1024])
        psQ2, cm_psQ2 = p.ps_tmp("psQ2", [1, 1024])

        for sl in range(2):
            p.wait("gpsimd", s_sp[sl], 16 * sp_uses[sl])
        for dc in range(DC):
            p.on("gpsimd", lambda g, dc=dc: g.dma_start(
                out=pre2F[:, dc * TC:(dc + 1) * TC], in_=p2Td.ap()[dc * 128:(dc + 1) * 128, :]))
            p.inc("gpsimd", s_p2l, 16)
        p2l_done = p.ctr[s_p2l]

        if_uses = [0, 0]
        og_uses = [0, 0]
        p.wait("gpsimd", s_cc, cc_oe)
        for sl in range(2):
            p.wait("gpsimd", s_sc2[sl], fg_totals[sl])
        add_ms = {}
        for j in range(8):
            sl = j % 2
            if if_uses[sl] > 0:
                p.wait("gpsimd", s_og[sl], 16 * og_uses[sl])
            p.on("gpsimd", lambda g, j=j, sl=sl: g.dma_start(
                out=idxf[sl][:], in_=myf8[j * 128:(j + 1) * 128, :]))
            if_uses[sl] += 1
            p.inc("gpsimd", s_if[sl], 16)
            p.on("gpsimd", lambda g, j=j, sl=sl: g.dma_start(
                out=gkt[sl][:], in_=myg8[j * 128:(j + 1) * 128, :]))
            p.inc("gpsimd", s_if[sl], 16)
            p.wait("gpsimd", s_if[sl], 32 * if_uses[sl])
            if j >= 2:
                p.wait("gpsimd", s_dv, add_ms[j - 2])
            p.on("gpsimd", lambda g, sl=sl: g.indirect_dma_start(
                out=oeg[sl][:], out_offset=None,
                in_=ago_o.ap(), in_offset=IndirectOffsetOnAxis(idxf[sl][:, 0:1], 0)))
            og_uses[sl] += 1
            p.inc("gpsimd", s_og[sl], 16)
            p.wait("vector", s_og[sl], 16 * og_uses[sl])
            p.on("vector", lambda v, sl=sl: v.tensor_scalar(
                oegs[sl][:], oeg[sl][:], gkt[sl][:, 0:1], None, ALU.mult))
            sc_ms = p.inc("vector", s_dv)
            if j < 2:
                p.wait("tensor", s_p2l, p2l_done)
            for dc in range(DC):
                psl2 = pstd[dc % 2]
                p.wait("tensor", s_dv, sc_ms)
                p.on("tensor", lambda t, dc=dc, sl=sl, psl2=psl2: t.transpose(
                    psl2[:], oegs[sl][:, dc * 128:(dc + 1) * 128], ident[:]))
                trm = p.inc("tensor", s_mm)
                p.wait("vector", s_mm, trm)
                p.on("vector", lambda v, j=j, dc=dc, psl2=psl2: v.tensor_add(
                    pre2F[:, dc * TC + j * 128: dc * TC + (j + 1) * 128],
                    pre2F[:, dc * TC + j * 128: dc * TC + (j + 1) * 128], psl2[:]))
                cpm = p.inc("vector", s_dv)
                p.wait("tensor", s_dv, cpm)
            add_ms[j] = p.ctr[s_dv]

        st2_ms = None
        for dc in range(DC):
            sl = dc % 2
            p.wait("vector", s_dv, add_ms[7])
            if dc >= 2:
                p.wait("vector", s_mm)
            p.on("vector", lambda v, dc=dc, sl=sl: v.tensor_copy(
                pf2[sl][:], pre2F[:, dc * TC:(dc + 1) * TC]))
            c1 = p.inc("vector", s_dv)
            p.wait("scalar", s_dv, c1)
            p.on("scalar", lambda s, sl=sl: s.activation(sq2[sl][:], pf2[sl][:], AF.Square))
            sev = p.inc("scalar", s_ev)
            p.wait("tensor", s_ev, sev)
            for half in range(2):
                p.on("tensor", lambda t, dc=dc, half=half, sl=sl: t.matmul(
                    psS2[0:1, half * 512:(half + 1) * 512], ones_col[:],
                    pf2[sl][:, half * 512:(half + 1) * 512],
                    start=(dc == 0), stop=(dc == DC - 1)))
                p.on("tensor", lambda t, dc=dc, half=half, sl=sl: t.matmul(
                    psQ2[0:1, half * 512:(half + 1) * 512], ones_col[:],
                    sq2[sl][:, half * 512:(half + 1) * 512],
                    start=(dc == 0), stop=(dc == DC - 1)))
            st2_ms = p.inc("tensor", s_mm)

        p.wait("vector", s_mm, st2_ms)
        p.on("vector", lambda v: v.tensor_scalar(m2_row[:], psS2[0:1, :], 1.0 / DIM, None, ALU.mult))
        mm1 = p.inc("vector", s_dv)
        p.on("vector", lambda v: v.tensor_scalar(q2_row[:], psQ2[0:1, :], 1.0 / DIM, None, ALU.mult))
        mm2 = p.inc("vector", s_dv)
        p.wait("vector", s_dv, mm2)
        p.on("vector", lambda v: v.tensor_mul(t2_row[:], m2_row[:], m2_row[:]))
        mm3 = p.inc("vector", s_dv)
        p.wait("vector", s_dv, mm3)
        p.on("vector", lambda v: v.tensor_sub(q2_row[:], q2_row[:], t2_row[:]))
        mm4 = p.inc("vector", s_dv)
        p.wait("vector", s_dv, mm4)
        p.on("vector", lambda v: v.tensor_scalar(q2_row[:], q2_row[:], EPS, None, ALU.add))
        mm5 = p.inc("vector", s_dv)
        p.wait("scalar", s_dv, mm5)
        p.on("scalar", lambda s: s.activation(t2_row[:], q2_row[:], AF.Ln))
        le1 = p.inc("scalar", s_ev)
        p.wait("scalar", s_ev, le1)
        p.on("scalar", lambda s: s.activation(r2_row[:], t2_row[:], AF.Exp, scale=-0.5))
        le2 = p.inc("scalar", s_ev)
        p.free(cm_psQ2); p.free(cm_psS2); p.free(cm_pt1d); p.free(cm_pt0d)
        psM2, cm_psM2 = p.ps_tmp("psM2", [128, 1024])
        psR2, cm_psR2 = p.ps_tmp("psR2", [128, 1024])
        p.wait("tensor", s_ev, le2)
        p.wait("tensor", s_dv, mm1)
        p.wait("tensor", s_dv, mm2)
        for half in range(2):
            p.on("tensor", lambda t, half=half: t.matmul(
                psM2[:, half * 512:(half + 1) * 512], ones_rf[:],
                m2_row[0:1, half * 512:(half + 1) * 512], start=True, stop=True))
            p.on("tensor", lambda t, half=half: t.matmul(
                psR2[:, half * 512:(half + 1) * 512], ones_rf[:],
                r2_row[0:1, half * 512:(half + 1) * 512], start=True, stop=True))
        mr2 = p.inc("tensor", s_mm)

        fst_uses = [0, 0]
        for dc in range(DC):
            sl = dc % 2
            p.wait("vector", s_mm, mr2)
            if fst_uses[sl] > 0:
                p.wait("vector", s_fst[sl], 16 * fst_uses[sl])
            p.on("vector", lambda v, dc=dc, sl=sl: v.tensor_sub(
                fstg[sl][:], pre2F[:, dc * TC:(dc + 1) * TC], psM2[:]))
            f1_ = p.inc("vector", s_dv)
            p.wait("vector", s_dv, f1_)
            p.on("vector", lambda v, dc=dc, sl=sl: v.tensor_mul(
                fstg[sl][:], fstg[sl][:], psR2[:]))
            f2_ = p.inc("vector", s_dv)
            p.wait("sync", s_dv, f2_)
            p.on("sync", lambda sy, dc=dc, sl=sl: sy.dma_start(
                out=OUT_T[dc * 128:(dc + 1) * 128, :], in_=fstg[sl][:]))
            fst_uses[sl] += 1
            p.inc("sync", s_fst[sl], 16)

        for sl in range(2):
            p.wait("sync", s_fst[sl], 16 * fst_uses[sl])
        p.free_all()
        p.emit(block)
    return nc


def _host_fused_inputs(inputs, core):
    x = np.asarray(inputs["L0_attn_x"], np.float32).reshape(T, DIM)
    d = {"xT": np.ascontiguousarray(x[core * TC:(core + 1) * TC, :].T)}
    for pch in "qkvo":
        Wm = np.asarray(inputs[f"L0_attn_W{pch}"], np.float32)
        Bm = np.asarray(inputs[f"L0_attn_B{pch}"], np.float32)
        Am = np.asarray(inputs[f"L0_attn_A{pch}"], np.float32)
        bm = np.asarray(inputs[f"L0_attn_b{pch}"], np.float32)
        d[f"W{pch}T"] = np.ascontiguousarray(Wm.T)
        d[f"B{pch}T"] = np.ascontiguousarray(Bm.T)
        d[f"A{pch}T"] = np.ascontiguousarray(Am.T)
        d[f"b{pch}"] = np.ascontiguousarray(bm.reshape(DC, 128).T)
        d[f"b{pch}row"] = bm.reshape(1, DIM)
    d["Wg"] = np.asarray(inputs["L0_ffn_Wg"], np.float32)
    d["W1c"] = np.ascontiguousarray(np.asarray(inputs["L0_ffn_W1"], np.float32)[core])
    d["W3c"] = np.ascontiguousarray(np.asarray(inputs["L0_ffn_W3"], np.float32)[core])
    d["W2c"] = np.ascontiguousarray(np.asarray(inputs["L0_ffn_W2"], np.float32)[core])
    d["SW1"] = np.asarray(inputs["L0_ffn_sw1"], np.float32)
    d["SW3"] = np.asarray(inputs["L0_ffn_sw3"], np.float32)
    d["SW2"] = np.asarray(inputs["L0_ffn_sw2"], np.float32)
    d["MYEXP"] = np.full((1, 1), float(core), np.float32)
    d["LT"] = np.triu(np.ones((128, 128), np.float32), 1)
    d["IDENT"] = np.eye(128, dtype=np.float32)
    return d


def kernel_fused(**inputs):
    nc = build_fused()
    res = run_bass_kernel_spmd(nc, [_host_fused_inputs(inputs, c) for c in range(NCORE)],
                               core_ids=list(range(NCORE)))
    out = np.concatenate(
        [np.ascontiguousarray(np.asarray(res.results[c]["OUT_T"]).T) for c in range(NCORE)],
        axis=0)
    return out.reshape(2, 4096, DIM)
